# version 06
# baseline (speedup 1.0000x reference)
import math
import numpy as np
import ml_dtypes
from contextlib import ExitStack

import concourse.bass as bass
import concourse.mybir as mybir
from concourse.bass_utils import run_bass_kernel_spmd

F32 = mybir.dt.float32
BF16 = mybir.dt.bfloat16
U32 = mybir.dt.uint32
AF = mybir.ActivationFunctionType
ALU = mybir.AluOpType
AX = mybir.AxisListType

B, N, D, H, DH, J, E, FF = 4, 8192, 1024, 16, 64, 64, 14, 2048
NOWN = 4096
NQ = 1024
NSUB_A = N // 128
NSUB_Q = NQ // 128
DNORM = float(DH) ** -0.25
RATIO = float(J) ** -0.5
EPS_FEAT = 1e-4
EPS_DEN = 0.01
LN_EPS = 1e-5
LN8EM4 = math.log(EPS_FEAT / RATIO)
LNRATIO = math.log(RATIO)

ENGS = ("sync", "scalar", "vector", "tensor", "gpsimd")


class Sem:
    __slots__ = ("h", "n")

    def __init__(self, h):
        self.h = h
        self.n = 0


class Prog:
    def __init__(self):
        self.q = {e: [] for e in ENGS}

    def op(self, eng, fn, waits=(), inc=None, inck=1):
        wl = [(s.h, v) for (s, v) in waits if v and v > 0]
        it = None
        tot = None
        if inc is not None:
            inc.n += inck
            tot = inc.n
            it = (inc.h, inck)

        def thunk(e, fn=fn, wl=wl, it=it):
            for h, v in wl:
                e.wait_ge(h, v)
            inst = fn(e)
            if it is not None:
                inst.then_inc(it[0], it[1])

        self.q[eng].append(thunk)
        return tot

    def barrier(self, bar, drains=()):
        target = bar.n + len(ENGS)
        dr = {}
        for eng, s in drains:
            dr.setdefault(eng, []).append((s.h, s.n))
        for eng in ENGS:
            def thunk(e, bar=bar, target=target, w=dr.get(eng, ())):
                for h, v in w:
                    if v > 0:
                        e.wait_ge(h, v)
                e.sem_inc(bar.h, 1)
                e.wait_ge(bar.h, target)
            self.q[eng].append(thunk)
        bar.n = target


def _build(debug=None, nsub_a=NSUB_A, a_stages=99, vev=4, dkv=5, dbgname='kfb'):
    nc = bass.Bass("TRN2", target_bir_lowering=False, debug=False)

    def din(name, shape, dtype=F32):
        return nc.dram_tensor(name, shape, dtype, kind="ExternalInput").ap()

    x_full = din("x_full", [N, D])
    x_own = din("x_own", [NOWN, D])
    wq = din("wq", [D, D]); wk = din("wk", [D, D]); wv = din("wv", [D, D]); wo = din("wo", [D, D])
    bq = din("bq", [1, D]); bk = din("bk", [1, D]); bv = din("bv", [1, D]); bo = din("bo", [1, D])
    proj = din("proj", [H, J, DH])
    rw = din("rw", [D, E])
    w1 = din("w1", [E, D, FF]); b1 = din("b1", [E, FF])
    w2 = din("w2", [E, FF, D]); b2 = din("b2", [E, D])
    eyeb = din("eyeb", [128, 128], BF16)
    pscr = nc.dram_tensor("pscr", [64, 1024], BF16, kind="Internal").ap()
    wodram = nc.dram_tensor("wodram", [D, D], BF16, kind="Internal").ap()
    eyef = din("eyef", [128, 128], F32)
    out = nc.dram_tensor("out", [NOWN, D], F32, kind="ExternalOutput").ap()
    dbg = nc.dram_tensor("dbg", [128, 4096], F32, kind="ExternalOutput").ap() if debug else None

    p = Prog()

    with ExitStack() as ctx:
        def sbt(name, sh, dt=F32):
            return ctx.enter_context(nc.sbuf_tensor(name, sh, dt))

        wt_a = sbt("wt_a", [128, 8 * 1024], BF16)
        wt_b = sbt("wt_b", [128, 8 * 1024], BF16)
        projT = sbt("projT", [128, 8 * 64], BF16)
        kv_sb = sbt("kv_sb", [64, H * 65], BF16)
        ksumB = sbt("ksumB", [128, H * 64], BF16)
        rw_sb = sbt("rw_sb", [128, 8 * E], F32)
        bqr = sbt("bqr", [1, D], BF16)
        bkr = sbt("bkr", [1, D], BF16)
        bvr = sbt("bvr", [1, D], BF16)
        bor = sbt("bor", [1, D], BF16)
        b2r = [sbt(f"b2r{i}", [1, D], BF16) for i in range(2)]
        b1row = sbt("b1row", [1, FF], BF16)
        b1c = [sbt(f"b1c{i}", [128, 16]) for i in range(2)]
        eyeb_s = sbt("eyeb_s", [128, 128], BF16)
        eyef_s = sbt("eyef_s", [128, 128], F32)
        onesrow = sbt("onesrow", [1, 128])
        onesrow_b = sbt("onesrow_b", [1, 128], BF16)
        iotaB = sbt("iotaB", [128, E])
        negq = sbt("negq", [128, 64], BF16)
        posq = sbt("posq", [128, 1], BF16)
        onescol_b = sbt("onescol_b", [128, 1], BF16)
        m_row = sbt("m_row", [1, 16]); a1_row = sbt("a1_row", [1, 16]); c1_row = sbt("c1_row", [1, 16])
        c1b = sbt("c1b", [64, 16])
        trows = sbt("trows", [1, H * 65], BF16)
        mrun = sbt("mrun", [128, 16])
        stsc = sbt("stsc", [128, 16]); nstab = sbt("nstab", [128, 16])
        lgt = sbt("lgt", [128, E]); ejk = sbt("ejk", [128, E], BF16)
        nlmax = sbt("nlmax", [128, 1]); ssum = sbt("ssum", [128, 1])
        wrec = sbt("wrec", [128, 1]); ixf = sbt("ixf", [128, 1])
        mx8 = sbt("mx8", [128, 8]); ix8 = sbt("ix8", [128, 8], U32)
        den16 = sbt("den16", [128, 16]); rden16 = sbt("rden16", [128, 16])
        lnsc = [sbt(f"lnsc{i}", [128, 8]) for i in range(2)]
        cfix = sbt("cfix", [128, 8])
        xnF = sbt("xnF", [128, 8 * NQ], BF16)
        accq = sbt("accq", [128, NQ * 8], BF16)
        w1buf = sbt("w1buf", [128, 8 * FF], BF16)
        w2buf = sbt("w2buf", [128, 16 * D], BF16)
        hidb = sbt("hidb", [128, 16 * 512], BF16)
        wmask = sbt("wmask", [128, NSUB_Q * E])
        xb = [sbt(f"xb{i}", [128, D]) for i in range(2)]
        xnb = [sbt(f"xnb{i}", [128, D], BF16) for i in range(2)]
        scr32 = sbt("scr32", [128, 3 * D])
        xr32 = [scr32[:, 0:D], scr32[:, D:2 * D]]
        qeb = scr32[:, 0:D]; qpb = scr32[:, D:2 * D]; qprod = scr32[:, 2 * D:3 * D]
        xfb = sbt("xfb", [128, D], BF16)
        kfb = sbt("kfb", [128, D], BF16)
        sqb = sbt("sqb", [128, D], BF16)
        vtb = sbt("vtb", [128, H * 65], BF16)
        ekb = sbt("ekb", [128, H * 65 + 66], BF16)
        wst128 = sbt("wst128", [128, 1024], BF16)
        qfb = xnb[1]
        qnb = xnb[0]
        qtb = scr32[0:64, 2 * D:3 * D].bitcast(BF16)
        apre = qnb
        outb = hidb[:, 0:4096].bitcast(F32)[:, 0:1024]
        _hpool = [kfb, sqb, vtb, ekb, xfb, xnb[0], xnb[1], wst128]

        def htile(g, fc):
            if g == 0:
                return hidb[:, fc * 512:(fc + 1) * 512]
            t = _hpool[fc // 2]
            off = (fc % 2) * 512
            return t[:, off:off + 512]
        pstage = scr32[0:64, 0:H * 64]

        pAll = ctx.enter_context(nc.psum_tensor("pAll", [128, 4096], F32))

        def bank(k, dt=F32, nb=1):
            v = pAll[:, k * 512:(k + nb) * 512]
            if dt is BF16:
                v = v.bitcast(BF16)
            return v

        sems = {nm: Sem(ctx.enter_context(nc.semaphore(nm)))
                for nm in ("sIN", "sWD", "sPE", "sACT", "sDVE", "sGP", "sST", "bar",
                           "sMS", "sWT", "sW1", "sW2", "sSL0", "sSL1", "sXA", "sXB", "sBB")}
        sIN, sWD, sPE, sACT, sDVE, sGP, sST = (sems[k] for k in
            ("sIN", "sWD", "sPE", "sACT", "sDVE", "sGP", "sST"))
        sMS, sWT, sW1, sW2 = (sems[k] for k in ("sMS", "sWT", "sW1", "sW2"))
        sBB = sems["sBB"]
        sSL = [sems["sSL0"], sems["sSL1"]]
        sX = [sems["sXA"], sems["sXB"]]
        bar = sems["bar"]

        block = ctx.enter_context(nc.Block())

        p.op("gpsimd", lambda e: e.memset(onesrow[:, :], 1.0), inc=sGP)
        p.op("gpsimd", lambda e: e.memset(onesrow_b[:, :], 1.0), inc=sGP)
        p.op("gpsimd", lambda e: e.memset(negq[:, :], -0.0625), inc=sGP)
        p.op("gpsimd", lambda e: e.memset(posq[:, :], 0.0625), inc=sGP)
        p.op("gpsimd", lambda e: e.memset(onescol_b[:, :], 1.0), inc=sGP)
        p.op("gpsimd", lambda e: e.memset(mrun[:, :], -1e30), inc=sGP)
        for k, val in ((0, LN_EPS), (1, -0.5), (2, LN8EM4), (3, LNRATIO), (4, -1.0), (5, DNORM)):
            p.op("gpsimd", lambda e, k=k, val=val: e.memset(cfix[:, k:k + 1], val), inc=sGP)
        p.op("gpsimd", lambda e: e.iota(iotaB[:, :], pattern=[[1, E]], base=0,
                                        channel_multiplier=0,
                                        allow_small_or_imprecise_dtypes=True), inc=sGP)
        p.op("gpsimd", lambda e: e.memset(
            vtb[:, :].rearrange("p (h n) -> p h n", n=65)[:, :, 64:65], 1.0), inc=sGP)
        p.op("gpsimd", lambda e: e.memset(
            ekb[:, 0:1040].rearrange("p (h n) -> p h n", n=65)[:, :, 64:65], 1.0), inc=sGP)
        for dst, src in ((bqr[0:1, :], bq), (bkr[0:1, :], bk), (bvr[0:1, :], bv),
                         (bor[0:1, :], bo)):
            p.op("gpsimd", lambda e, d=dst, s=src: e.dma_start(out=d, in_=s[:, :]),
                 inc=sWD, inck=16)
        for dst, src in ((eyeb_s, eyeb), (eyef_s, eyef)):
            p.op("sync", lambda e, d=dst, s=src: e.dma_start(out=d[:, :], in_=s[:, :]),
                 inc=sMS, inck=16)
        p.op("sync", lambda e: e.dma_start(
            out=rw_sb[:, :].rearrange("p (c e) -> p c e", e=E),
            in_=rw.rearrange("(c p) e -> p c e", p=128)), inc=sMS, inck=16)
        p.op("sync", lambda e: e.dma_start(
            out=pstage.rearrange("j (h d) -> j h d", d=DH),
            in_=proj.rearrange("h j d -> j h d")), inc=sMS, inck=16)
        setup_in = sMS.n
        setup_gp = sGP.n

        wt_state = {"evac": 0, "war": [0, 0]}

        def emit_wt(dst, wsrc, extra_pe_wait=0):
            pTW = bank(0, BF16, 2)[:, 0:1024]
            for r in range(8):
                ld = p.op("gpsimd", lambda e, r=r, w=wsrc: e.dma_start(
                    out=wst128[:, :], in_=w[r * 128:(r + 1) * 128, :]),
                    waits=[(sACT, wt_state["evac"])], inc=sWT, inck=16)

                def tfn(e, r=r):
                    for dc in range(8):
                        inst = e.matmul(pTW[:, dc * 128:(dc + 1) * 128],
                                        wst128[:, dc * 128:(dc + 1) * 128], eyeb_s[:, :],
                                        is_transpose=True, start=(dc == 0), stop=(dc == 7),
                                        skip_group_check=True)
                    return inst
                tv = p.op("tensor", tfn,
                          waits=[(sWT, ld), (sACT, wt_state["evac"]), (sPE, extra_pe_wait),
                                 (sMS, setup_in)], inc=sPE)
                dv = dst[:, :].rearrange("p (dc n) -> p dc n", n=1024)[:, :, r * 128:(r + 1) * 128]
                ev = p.op("scalar", lambda e, dv=dv, pTW=pTW: e.activation(
                    out=dv, in_=pTW.rearrange("p (dc n) -> p dc n", n=128), func=AF.Copy),
                    waits=[(sPE, tv)], inc=sACT)
                wt_state["evac"] = ev
            return wt_state["evac"]

        emit_wt(wt_a, wk)
        emit_wt(wt_b, wv)
        pPJ = bank(2, F32, 2)[0:64, :]

        def pj_t(e):
            for h in range(H):
                inst = e.matmul(pPJ[0:64, h * 64:(h + 1) * 64],
                                pstage[:, h * 64:(h + 1) * 64], eyef_s[0:64, 0:64],
                                is_transpose=True, start=(h % 8 == 0), stop=(h % 8 == 7),
                                skip_group_check=True)
            return inst
        pv0 = p.op("tensor", pj_t, waits=[(sMS, setup_in)], inc=sPE)
        pe1 = p.op("scalar", lambda e: e.activation(
            out=hidb[0:64, 0:1024], in_=pPJ[0:64, :], func=AF.Copy, scale=cfix[0:64, 5:6]),
            waits=[(sPE, pv0), (sGP, setup_gp)], inc=sACT)
        pj_st = p.op("sync", lambda e: e.dma_start(out=pscr[:, :], in_=hidb[0:64, 0:1024]),
                     waits=[(sACT, pe1)], inc=sBB, inck=16)
        pjsrc = pscr.rearrange("p (a two d) -> p a two d", two=2, d=64)
        p.op("sync", lambda e: e.dma_start(
            out=projT[0:64, :].rearrange("p (a d) -> p a d", d=64), in_=pjsrc[:, :, 0, :]),
            waits=[(sBB, pj_st)], inc=sMS, inck=16)
        p.op("sync", lambda e: e.dma_start(
            out=projT[64:128, :].rearrange("p (a d) -> p a d", d=64), in_=pjsrc[:, :, 1, :]),
            waits=[(sBB, pj_st)], inc=sMS, inck=16)

        p.barrier(bar, drains=[("sync", sMS), ("sync", sST), ("sync", sX[0]), ("sync", sX[1]), ("gpsimd", sWD), ("gpsimd", sWT), ("gpsimd", sW1), ("gpsimd", sW2), ("gpsimd", sSL[0]), ("gpsimd", sSL[1]), ("sync", sBB)])

        if debug == "SETUP":
            _emit_bodies(p, block)
            return nc

        pT = bank(0, BF16)[:, 0:1024]
        pKV = bank(1, F32, 2)
        pDKp = (bank(3), bank(4))
        pUt = (bank(5)[0:65, 0:455], bank(6)[0:65, 0:455], bank(7)[0:65, 0:130])

        def mcol(h):
            return 8 * (h // 8) + 4 * (h % 2) + (h % 8) // 2

        def u_ap(h):
            return pUt[h // 7], (h % 7) * 65

        dma_d = {}; ln_d = {}; t_d = {}; te_d = {}; kf_d = {}; kfe_d = {}
        sq_d = {}; v_d = {}; ve_d = {}; dk_d = {}; ek_d = {}; st_d = {}; u_d = {}

        junkb = hidb[:, 0:2048].bitcast(BF16)[:, 0:1024]

        def a_ln(i):
            pp = i % 2
            dma_d[i] = p.op("sync", lambda e, i=i, pp=pp: e.dma_start(
                out=xb[pp][:, :], in_=x_full[i * 128:(i + 1) * 128, :]),
                waits=[(sDVE, ln_d.get(i - 2, 0)), (sACT, sqa_d.get(i - 2, 0))],
                inc=sX[pp], inck=16)
            sqa_d[i] = p.op("scalar", lambda e, pp=pp: e.activation(
                out=xnb[pp][:, :], in_=xb[pp][:, :], func=AF.Square,
                accum_out=lnsc[pp][:, 1:2]),
                waits=[(sX[pp], dma_d[i]), (sGP, setup_gp), (sPE, t_d.get(i - 2, 0))], inc=sACT)
            sma = p.op("vector", lambda e, pp=pp: e.tensor_scalar(
                out=junkb[:, :], in0=xb[pp][:, :], scalar1=1.0, scalar2=None, op0=ALU.mult,
                op1=ALU.add, accum_out=lnsc[pp][:, 0:1]),
                waits=[(sX[pp], dma_d[i]), (sPE, t_d.get(i - 2, 0))], inc=sDVE)
            v1a = p.op("vector", lambda e, pp=pp: e.tensor_scalar(
                out=lnsc[pp][:, 2:3], in0=lnsc[pp][:, 0:1], scalar1=1.0 / D, scalar2=None,
                op0=ALU.mult), waits=[(sDVE, sma)], inc=sDVE)
            v1b = p.op("vector", lambda e, pp=pp: e.tensor_tensor(
                out=lnsc[pp][:, 3:4], in0=lnsc[pp][:, 2:3], in1=lnsc[pp][:, 2:3], op=ALU.mult),
                waits=[(sDVE, v1a)], inc=sDVE)
            ve_v = p.op("vector", lambda e, pp=pp: e.tensor_scalar(
                out=lnsc[pp][:, 4:5], in0=lnsc[pp][:, 1:2], scalar1=1.0 / D,
                scalar2=lnsc[pp][:, 3:4], op0=ALU.mult, op1=ALU.subtract),
                waits=[(sDVE, v1b), (sACT, sqa_d[i])], inc=sDVE)
            a1a = p.op("scalar", lambda e, pp=pp: e.activation(
                out=lnsc[pp][:, 5:6], in_=lnsc[pp][:, 4:5], func=AF.Ln, bias=cfix[:, 0:1]),
                waits=[(sDVE, ve_v)], inc=sACT)
            a1 = p.op("scalar", lambda e, pp=pp: e.activation(
                out=lnsc[pp][:, 6:7], in_=lnsc[pp][:, 5:6], func=AF.Exp, scale=cfix[:, 1:2]),
                waits=[(sACT, a1a)], inc=sACT)
            ln_d[i] = p.op("vector", lambda e, pp=pp: e.tensor_scalar(
                out=xnb[pp][:, :], in0=xb[pp][:, :], scalar1=lnsc[pp][:, 2:3],
                scalar2=lnsc[pp][:, 6:7], op0=ALU.subtract, op1=ALU.mult),
                waits=[(sACT, a1)], inc=sDVE)

        def a_heavy(i):
            pp = i % 2

            def tx(e, pp=pp):
                for dc in range(8):
                    inst = e.matmul(pT[:, dc * 128:(dc + 1) * 128],
                                    xnb[pp][:, dc * 128:(dc + 1) * 128], eyeb_s[:, :],
                                    is_transpose=True, start=(dc == 0), stop=(dc == 7),
                                    skip_group_check=True)
                return inst
            t_d[i] = p.op("tensor", tx,
                          waits=[(sDVE, ln_d[i]), (sACT, te_d.get(i - 1, 0))], inc=sPE)
            te_d[i] = p.op("scalar", lambda e: e.activation(
                out=xfb[:, :], in_=pT[:, :], func=AF.Copy),
                waits=[(sPE, t_d[i]), (sPE, v_d.get(i - 1, 0))], inc=sACT)

            def kf(e):
                for ic in range(8):
                    for dc in range(8):
                        e.matmul(pKV[:, ic * 128:(ic + 1) * 128],
                                 wt_a[:, dc * 1024 + ic * 128:dc * 1024 + (ic + 1) * 128],
                                 xfb[:, dc * 128:(dc + 1) * 128],
                                 start=(ic % 4 == 0 and dc == 0), stop=False,
                                 skip_group_check=True)
                for ic in range(8):
                    inst = e.matmul(pKV[:, ic * 128:(ic + 1) * 128],
                                    bkr[0:1, ic * 128:(ic + 1) * 128], onesrow_b[0:1, :],
                                    start=False, stop=(ic % 4 == 3), skip_group_check=True)
                return inst
            kf_d[i] = p.op("tensor", kf,
                           waits=[(sACT, te_d[i]), (sACT, ve_d.get(i - 1, 0))], inc=sPE)
            kfe_d[i] = p.op("scalar", lambda e: e.activation(
                out=kfb[:, :], in_=pKV[:, :], func=AF.Copy),
                waits=[(sPE, kf_d[i]), (sPE, dk_d.get(i - 1, 0))], inc=sACT)
            sq_d[i] = p.op("vector", lambda e: e.tensor_tensor(
                out=sqb[:, :], in0=kfb[:, :], in1=kfb[:, :], op=ALU.mult),
                waits=[(sACT, kfe_d[i]), (sPE, dk_d.get(i - 1, 0))], inc=sDVE)

        def a_heavy2(i):
            pp = i % 2

            def vv(e):
                for nb in range(2):
                    for dc in range(8):
                        e.matmul(pKV[:, nb * 512:(nb + 1) * 512],
                                 xfb[:, dc * 128:(dc + 1) * 128],
                                 wt_b[:, dc * 1024 + nb * 512:dc * 1024 + (nb + 1) * 512],
                                 start=(dc == 0), stop=False, skip_group_check=True)
                for nb in range(2):
                    inst = e.matmul(pKV[:, nb * 512:(nb + 1) * 512],
                                    onesrow_b[0:1, :], bvr[0:1, nb * 512:(nb + 1) * 512],
                                    start=False, stop=True, skip_group_check=True)
                return inst
            v_d[i] = p.op("tensor", vv, waits=[(sACT, kfe_d[i])], inc=sPE)
            ve_d[i] = p.op("scalar", lambda e: e.activation(
                out=vtb[:, :].rearrange("p (h n) -> p h n", n=65)[:, :, 0:64],
                in_=pKV[:, :].rearrange("p (h n) -> p h n", n=64), func=AF.Copy),
                waits=[(sPE, v_d[i]), (sPE, u_d.get(i - 1, 0))], inc=sACT)

            dkw = {}; ekw = {}
            for w in range(2):
                def dk_wave(e, w=w):
                    for k in range(8):
                        h = 8 * w + k
                        rb = (h % 2) * 64
                        sl = k // 2
                        pb = pDKp[h % 2]
                        cb = (h // 2) * 128
                        kf_h = kfb[rb:rb + 64, cb:cb + 128]
                        sq_h = sqb[rb:rb + 64, cb:cb + 128]
                        e.matmul(pb[:, sl * 64:(sl + 1) * 64], kf_h,
                                 projT[rb:rb + 64, (h // 2) * 64:(h // 2) * 64 + 64],
                                 start=(k < 2), stop=False,
                                 skip_group_check=True, tile_position=(rb, 0))
                        e.matmul(pb[:, sl * 64:(sl + 1) * 64], sq_h, negq[rb:rb + 64, :],
                                 start=False, stop=False, skip_group_check=True,
                                 tile_position=(rb, 0))
                        inst = e.matmul(pb[:, 256 + sl:257 + sl], sq_h,
                                        posq[rb:rb + 64, :],
                                        start=False, stop=(k >= 6), skip_group_check=True,
                                        tile_position=(rb, 0))
                    return inst
                if w == 0:
                    wlw = [(sDVE, sq_d[i]), (sACT, ek_d.get(i - 1, 0)),
                           (sDVE, st_d.get(i - 1, 0))]
                else:
                    wlw = [(sACT, ekw[0])]
                dkw[w] = p.op("tensor", dk_wave, waits=wlw, inc=sPE)

                def ekfn(e, w=w):
                    for par in range(2):
                        base = (8 * w + par) * 65
                        ov = ekb[:, base:base + 4 * 130].rearrange(
                            "p (t n) -> p t n", n=130)[:, :, 0:64]
                        inst = e.activation(
                            out=ov,
                            in_=pDKp[par][:, 0:256].rearrange("p (t n) -> p t n", n=64),
                            func=AF.Exp)
                    return inst
                ekw[w] = p.op("scalar", ekfn,
                    waits=[(sPE, dkw[w]), (sPE, u_d.get(i - 1, 0) if w == 0 else 0)],
                    inc=sACT)
            dk_d[i] = dkw[1]
            ek_d[i] = ekw[1]
            return dkw, ekw

        def a_stab_u(i, dkw, ekw):
            def stab1(e):
                for w in range(2):
                    for par in range(2):
                        c0 = 8 * w + 4 * par
                        inst = e.tensor_reduce(
                            out=stsc[:, c0:c0 + 4],
                            in_=pDKp[par][:, 0:256].rearrange("p (t n) -> p t n", n=64),
                            axis=AX.X, op=ALU.max) if w == 1 or True else None
                return inst
            sv1 = p.op("vector", lambda e: (
                e.tensor_reduce(out=stsc[:, 0:4],
                                in_=pDKp[0][:, 0:256].rearrange("p (t n) -> p t n", n=64),
                                axis=AX.X, op=ALU.max),
                e.tensor_reduce(out=stsc[:, 4:8],
                                in_=pDKp[1][:, 0:256].rearrange("p (t n) -> p t n", n=64),
                                axis=AX.X, op=ALU.max))[-1],
                waits=[(sACT, ekw[0])], inc=sDVE)
            sv2 = p.op("vector", lambda e: (
                e.tensor_tensor(out=stsc[:, 0:4], in0=stsc[:, 0:4],
                                in1=pDKp[0][:, 256:260], op=ALU.add),
                e.tensor_tensor(out=stsc[:, 4:8], in0=stsc[:, 4:8],
                                in1=pDKp[1][:, 256:260], op=ALU.add))[-1],
                waits=[(sDVE, sv1)], inc=sDVE)
            st0 = p.op("vector", lambda e: e.tensor_tensor(
                out=mrun[:, 0:8], in0=mrun[:, 0:8], in1=stsc[:, 0:8], op=ALU.max),
                waits=[(sDVE, sv2)], inc=sDVE)
            return st0


        sqa_d = {}
        a_ln(0)
        for i in range(nsub_a):
            a_heavy(i)
            if i + 1 < nsub_a:
                a_ln(i + 1)
            pp = i % 2

            def vv(e):
                for nb in range(2):
                    for dc in range(8):
                        e.matmul(pKV[:, nb * 512:(nb + 1) * 512],
                                 xfb[:, dc * 128:(dc + 1) * 128],
                                 wt_b[:, dc * 1024 + nb * 512:dc * 1024 + (nb + 1) * 512],
                                 start=(dc == 0), stop=False, skip_group_check=True)
                for nb in range(2):
                    inst = e.matmul(pKV[:, nb * 512:(nb + 1) * 512],
                                    onesrow_b[0:1, :], bvr[0:1, nb * 512:(nb + 1) * 512],
                                    start=False, stop=True, skip_group_check=True)
                return inst
            v_d[i] = p.op("tensor", vv, waits=[(sACT, kfe_d[i])], inc=sPE)
            ve_d[i] = p.op("scalar", lambda e: e.activation(
                out=vtb[:, :].rearrange("p (h n) -> p h n", n=65)[:, :, 0:64],
                in_=pKV[:, :].rearrange("p (h n) -> p h n", n=64), func=AF.Copy),
                waits=[(sPE, v_d[i]), (sPE, u_d.get(i - 1, 0))], inc=sACT)

            dkw = {}; ekw = {}; stw = {}
            for w in range(2):
                def dk_wave(e, w=w):
                    for k in range(8):
                        h = 8 * w + k
                        rb = (h % 2) * 64
                        sl = k // 2
                        pb = pDKp[h % 2]
                        cb = (h // 2) * 128
                        kf_h = kfb[rb:rb + 64, cb:cb + 128]
                        sq_h = sqb[rb:rb + 64, cb:cb + 128]
                        e.matmul(pb[:, sl * 64:(sl + 1) * 64], kf_h,
                                 projT[rb:rb + 64, (h // 2) * 64:(h // 2) * 64 + 64],
                                 start=(k < 2), stop=False,
                                 skip_group_check=True, tile_position=(rb, 0))
                        e.matmul(pb[:, sl * 64:(sl + 1) * 64], sq_h, negq[rb:rb + 64, :],
                                 start=False, stop=False, skip_group_check=True,
                                 tile_position=(rb, 0))
                        inst = e.matmul(pb[:, 256 + sl:257 + sl], sq_h,
                                        posq[rb:rb + 64, :],
                                        start=(False), stop=(k >= 6), skip_group_check=True,
                                        tile_position=(rb, 0))
                    return inst
                if w == 0:
                    wlw = [(sDVE, sq_d[i]), (sACT, ek_d.get(i - 1, 0)),
                           (sDVE, st_d.get(i - 1, 0))]
                else:
                    wlw = [(sACT, ekw[0]), (sDVE, stw[0])]
                dkw[w] = p.op("tensor", dk_wave, waits=wlw, inc=sPE)

                def ekfn(e, w=w):
                    for par in range(2):
                        base = (8 * w + par) * 65
                        ov = ekb[:, base:base + 4 * 130].rearrange(
                            "p (t n) -> p t n", n=130)[:, :, 0:64]
                        inst = e.activation(
                            out=ov,
                            in_=pDKp[par][:, 0:256].rearrange("p (t n) -> p t n", n=64),
                            func=AF.Exp)
                    return inst
                ekw[w] = p.op("scalar", ekfn,
                    waits=[(sPE, dkw[w]), (sPE, u_d.get(i - 1, 0) if w == 0 else 0)],
                    inc=sACT)

                def stabf1(e, w=w):
                    e.tensor_reduce(out=stsc[:, 8 * w:8 * w + 4],
                                    in_=pDKp[0][:, 0:256].rearrange("p (t n) -> p t n", n=64),
                                    axis=AX.X, op=ALU.max)
                    return e.tensor_reduce(
                        out=stsc[:, 8 * w + 4:8 * w + 8],
                        in_=pDKp[1][:, 0:256].rearrange("p (t n) -> p t n", n=64),
                        axis=AX.X, op=ALU.max)
                sv1 = p.op("vector", stabf1, waits=[(sACT, ekw[w])], inc=sDVE)

                def stabf2(e, w=w):
                    e.tensor_tensor(out=stsc[:, 8 * w:8 * w + 4], in0=stsc[:, 8 * w:8 * w + 4],
                                    in1=pDKp[0][:, 256:260], op=ALU.add)
                    return e.tensor_tensor(out=stsc[:, 8 * w + 4:8 * w + 8],
                                           in0=stsc[:, 8 * w + 4:8 * w + 8],
                                           in1=pDKp[1][:, 256:260], op=ALU.add)
                sv2 = p.op("vector", stabf2, waits=[(sDVE, sv1)], inc=sDVE)
                stw[w] = p.op("vector", lambda e, w=w: e.tensor_tensor(
                    out=mrun[:, 8 * w:8 * w + 8], in0=mrun[:, 8 * w:8 * w + 8],
                    in1=stsc[:, 8 * w:8 * w + 8], op=ALU.max),
                    waits=[(sDVE, sv2)], inc=sDVE)
            dk_d[i] = dkw[1]
            ek_d[i] = ekw[1]
            st_d[i] = stw[1]

            def uacc(e, i=i):
                for h in range(H):
                    ut, c = u_ap(h)
                    e.matmul(ut[:, c:c + 64],
                             ekb[:, h * 65:(h + 1) * 65], vtb[:, h * 65:h * 65 + 64],
                             start=(i == 0 and h % 7 == 0), stop=False, skip_group_check=True)
                    inst = e.matmul(ut[:, c + 64:c + 65],
                                    ekb[:, h * 65:(h + 1) * 65], onescol_b[:, :],
                                    start=False, stop=False, skip_group_check=True)
                return inst
            u_d[i] = p.op("tensor", uacc,
                          waits=[(sACT, ek_d[i]), (sACT, ve_d[i])], inc=sPE)

        fv = p.op("tensor", lambda e: e.matmul(
            pDKp[0][0:16, 0:128], mrun[:, :], eyef_s[:, :], is_transpose=True,
            start=True, stop=True, skip_group_check=True),
            waits=[(sDVE, st_d[nsub_a - 1]), (sPE, u_d[nsub_a - 1]),
                   (sACT, ek_d[nsub_a - 1])], inc=sPE)
        fv2 = p.op("vector", lambda e: e.tensor_reduce(
            out=stsc[0:16, 0:1], in_=pDKp[0][0:16, 0:128], axis=AX.X, op=ALU.max),
            waits=[(sPE, fv)], inc=sDVE)
        fv3 = p.op("tensor", lambda e: e.matmul(
            pDKp[1][0:1, 0:16], stsc[0:16, 0:1], eyef_s[0:16, 0:16], is_transpose=True,
            start=True, stop=True, skip_group_check=True), waits=[(sDVE, fv2)], inc=sPE)
        fv4 = p.op("scalar", lambda e: e.activation(
            out=m_row[:, :], in_=pDKp[1][0:1, 0:16], func=AF.Copy), waits=[(sPE, fv3)], inc=sACT)

        def fexp(e):
            e.activation(out=a1_row[:, :], in_=m_row[:, :], func=AF.Exp, bias=cfix[0:1, 2:3])
            return e.activation(out=c1_row[:, :], in_=m_row[:, :], func=AF.Exp,
                                scale=cfix[0:1, 4:5], bias=cfix[0:1, 3:4])
        fv5 = p.op("scalar", fexp, waits=[(sACT, fv4)], inc=sACT)

        def ftr(e):
            for h in range(H):
                ut, c = u_ap(h)
                mc = mcol(h)
                inst = e.activation(out=trows[0:1, h * 65:(h + 1) * 65],
                                    in_=ut[64:65, c:c + 65], func=AF.Copy,
                                    scale=a1_row[0:1, mc:mc + 1])
            return inst
        fv6 = p.op("scalar", ftr, waits=[(sACT, fv5)], inc=sACT)

        def fcorr(e):
            for h in range(H):
                ut, c = u_ap(h)
                last = (h % 7 == 6) or (h == H - 1)
                inst = e.matmul(ut[0:64, c:c + 65], onesrow_b[0:1, 0:64],
                                trows[0:1, h * 65:(h + 1) * 65],
                                start=False, stop=last, skip_group_check=True)
            return inst
        fv7 = p.op("tensor", fcorr, waits=[(sACT, fv6)], inc=sPE)
        fv8 = p.op("tensor", lambda e: e.matmul(
            pDKp[0][0:64, 16:32], onesrow[0:1, 0:64], c1_row[0:1, :],
            start=True, stop=True, skip_group_check=True),
            waits=[(sACT, fv6), (sDVE, fv2)], inc=sPE)
        fv9 = p.op("scalar", lambda e: e.activation(
            out=c1b[:, :], in_=pDKp[0][0:64, 16:32], func=AF.Copy), waits=[(sPE, fv8)], inc=sACT)

        def fkv(e):
            for h in range(H):
                ut, c = u_ap(h)
                mc = mcol(h)
                inst = e.activation(out=kv_sb[:, h * 65:(h + 1) * 65],
                                    in_=ut[0:64, c:c + 65], func=AF.Copy,
                                    scale=c1b[:, mc:mc + 1])
            return inst
        fv10 = p.op("scalar", fkv, waits=[(sPE, fv7), (sACT, fv9)], inc=sACT)
        pKS = bank(0, BF16)[0:1, 0:1024]

        def fkst(e):
            for h in range(H):
                inst = e.matmul(pKS[0:1, h * 64:(h + 1) * 64],
                                kv_sb[:, h * 65 + 64:h * 65 + 65], eyeb_s[0:64, 0:64],
                                is_transpose=True, start=(h == 0), stop=(h == H - 1),
                                skip_group_check=True)
            return inst
        fv11 = p.op("tensor", fkst, waits=[(sACT, fv10)], inc=sPE)
        fv12 = p.op("scalar", lambda e: e.activation(
            out=trows[0:1, 0:1024], in_=pKS[0:1, :], func=AF.Copy),
            waits=[(sPE, fv11)], inc=sACT)
        pKB = bank(1, F32, 2)

        def fksb(e):
            for nb in range(2):
                inst = e.matmul(pKB[:, nb * 512:(nb + 1) * 512], onesrow_b[0:1, :],
                                trows[0:1, nb * 512:(nb + 1) * 512],
                                start=True, stop=True, skip_group_check=True)
            return inst
        fv13 = p.op("tensor", fksb, waits=[(sACT, fv12)], inc=sPE)
        fv14 = p.op("scalar", lambda e: e.activation(
            out=ksumB[:, :], in_=pKB[:, :], func=AF.Copy), waits=[(sPE, fv13)], inc=sACT)

        if debug == "A":
            dbgsrc = {"kfb": kfb[:, 0:1024], "xfb": xfb[:, 0:1024], "sqb": sqb[:, 0:1024],
                      "vtb": vtb[:, 0:1024], "ekb": ekb[:, 0:1024],
                      "xnb": xnb[0][:, 0:1024], "lnsc": lnsc[0][:, 0:8]}[dbgname]
            d0 = p.op("vector", lambda e: e.memset(scr32[:, 0:2048], 0.0),
                      waits=[(sACT, fv14)], inc=sDVE)
            d1 = p.op("vector", lambda e: e.tensor_copy(out=scr32[0:64, 0:1040], in_=kv_sb[:, :]),
                      waits=[(sDVE, d0)], inc=sDVE)
            d2 = p.op("vector", lambda e: e.tensor_copy(out=scr32[0:1, 1040:1056], in_=m_row[:, :]),
                      waits=[(sDVE, d1)], inc=sDVE)
            d3 = p.op("vector", lambda e: e.tensor_copy(out=scr32[:, 2048:3072], in_=ksumB[:, :]),
                      waits=[(sACT, fv14)], inc=sDVE)
            d4a = p.op("vector", lambda e: e.memset(scr32[:, 0:1024], 0.0),
                       waits=[(sDVE, d3), (sST, 32)], inc=sDVE)
            d4 = p.op("vector", lambda e: e.tensor_copy(
                out=scr32[:, 0:dbgsrc.shape[1]], in_=dbgsrc),
                waits=[(sDVE, d4a)], inc=sDVE)
            p.op("sync", lambda e: e.dma_start(out=dbg[:, 0:2048], in_=scr32[:, 0:2048]),
                 waits=[(sDVE, d2)], inc=sST, inck=16)
            p.op("sync", lambda e: e.dma_start(out=dbg[:, 2048:3072], in_=scr32[:, 2048:3072]),
                 waits=[(sDVE, d3)], inc=sST, inck=16)
            p.op("sync", lambda e: e.dma_start(out=dbg[:, 3072:4096], in_=scr32[:, 0:1024]),
                 waits=[(sDVE, d4)], inc=sST, inck=16)
            p.op("sync", lambda e: None, waits=[(sST, sST.n)])
            _emit_bodies(p, block)
            return nc

        p.barrier(bar, drains=[("sync", sMS), ("sync", sST), ("sync", sX[0]), ("sync", sX[1]), ("gpsimd", sWD), ("gpsimd", sWT), ("gpsimd", sW1), ("gpsimd", sW2), ("gpsimd", sSL[0]), ("gpsimd", sSL[1]), ("sync", sBB)])

        st_total = Sem(None)
        hid_last = {"pe": 0}
        out_last = {"pe": 0}
        w1_ld = {"v": 0}
        w2_ld = {"v": 0}
        b1_rdy = {0: 0, 1: 0}
        b2_rdy = {0: 0, 1: 0}
        gelu_last = {0: 0, 1: 0}
        fin_d = {}
        store_d = {}

        for q in range(4):
            pTf = bank(0, F32, 2)
            pR = bank(2)[:, 0:128]
            pTW2 = bank(3, BF16)[:, 0:1024]
            c0_ln = {}; c0_t = {}; c0_xe = {}; c0_xr = {}; c0_r = {}; c0_m = {}
            if q == 0:
                wt_state["evac"] = 0

                def emit_wt2(dst, wsrc):
                    for r in range(8):
                        ld = p.op("gpsimd", lambda e, r=r, w=wsrc: e.dma_start(
                            out=wst128[:, :], in_=w[r * 128:(r + 1) * 128, :]),
                            waits=[(sACT, wt_state["evac"])], inc=sWT, inck=16)

                        def tfn(e, r=r):
                            for dc in range(8):
                                inst = e.matmul(pTW2[:, dc * 128:(dc + 1) * 128],
                                                wst128[:, dc * 128:(dc + 1) * 128], eyeb_s[:, :],
                                                is_transpose=True, start=(dc == 0), stop=(dc == 7),
                                                skip_group_check=True)
                            return inst
                        tv = p.op("tensor", tfn, waits=[(sWT, ld), (sACT, wt_state["evac"])],
                                  inc=sPE)
                        dv = dst[:, :].rearrange("p (dc n) -> p dc n", n=1024)[:, :, r * 128:(r + 1) * 128]
                        ev = p.op("scalar", lambda e, dv=dv: e.activation(
                            out=dv, in_=pTW2.rearrange("p (dc n) -> p dc n", n=128), func=AF.Copy),
                            waits=[(sPE, tv)], inc=sACT)
                        wt_state["evac"] = ev
                emit_wt2(wt_a, wq)
                for r in range(8):
                    ld = p.op("gpsimd", lambda e, r=r: e.dma_start(
                        out=wst128[:, :], in_=wo[r * 128:(r + 1) * 128, :]),
                        waits=[(sBB, wt_state.get("wost", 0)), (sACT, wt_state["evac"])],
                        inc=sWT, inck=16)

                    def tfn(e, r=r):
                        for dc in range(8):
                            inst = e.matmul(pTW2[:, dc * 128:(dc + 1) * 128],
                                            wst128[:, dc * 128:(dc + 1) * 128], eyeb_s[:, :],
                                            is_transpose=True, start=(dc == 0), stop=(dc == 7),
                                            skip_group_check=True)
                        return inst
                    tv = p.op("tensor", tfn, waits=[(sWT, ld), (sACT, wt_state["evac"])],
                              inc=sPE)
                    ev = p.op("scalar", lambda e: e.activation(
                        out=wst128[:, :], in_=pTW2[:, :], func=AF.Copy),
                        waits=[(sPE, tv)], inc=sACT)
                    wt_state["evac"] = ev
                    st = p.op("sync", lambda e, r=r: e.dma_start(
                        out=wodram.rearrange("(dc p) i -> p dc i", p=128)[:, :, r * 128:(r + 1) * 128],
                        in_=wst128[:, :].rearrange("p (dc i) -> p dc i", i=128)),
                        waits=[(sACT, ev)], inc=sBB, inck=16)
                    wt_state["wost"] = st
                for a in range(8):
                    p.op("sync", lambda e, a=a: e.dma_start(
                        out=wt_b[0:64, a * 1024:(a + 1) * 1024],
                        in_=wodram[a * 64:(a + 1) * 64, :]),
                        waits=[(sBB, wt_state["wost"])], inc=sMS, inck=16)
                    p.op("sync", lambda e, a=a: e.dma_start(
                        out=wt_b[64:128, a * 1024:(a + 1) * 1024],
                        in_=wodram[(a + 8) * 64:(a + 9) * 64, :]),
                        waits=[(sBB, wt_state["wost"])], inc=sMS, inck=16)

            for j in range(NSUB_Q):
                pp = j % 2
                row0 = q * NQ + j * 128
                dmj = p.op("sync", lambda e, row0=row0, pp=pp: e.dma_start(
                    out=xb[pp][:, :], in_=x_own[row0:row0 + 128, :]),
                    waits=[(sDVE, c0_ln.get(j - 2, 0)), (sPE, c0_t.get(j - 2, 0))],
                    inc=sX[pp], inck=16)

                def ln1(e, pp=pp):
                    e.tensor_reduce(out=lnsc[pp][:, 0:1], in_=xb[pp][:, :], axis=AX.X, op=ALU.add)
                    e.tensor_tensor(out=xnb[pp][:, :], in0=xb[pp][:, :], in1=xb[pp][:, :],
                                    op=ALU.mult)
                    e.tensor_reduce(out=lnsc[pp][:, 1:2], in_=xnb[pp][:, :], axis=AX.X, op=ALU.add)
                    return e.tensor_scalar(out=lnsc[pp][:, 2:3], in0=lnsc[pp][:, 0:1],
                                           scalar1=1.0 / D, scalar2=None, op0=ALU.mult)
                v1a = p.op("vector", ln1, waits=[(sX[pp], dmj), (sPE, c0_t.get(j - 2, 0))],
                           inc=sDVE)
                v1b = p.op("vector", lambda e, pp=pp: e.tensor_tensor(
                    out=lnsc[pp][:, 3:4], in0=lnsc[pp][:, 2:3], in1=lnsc[pp][:, 2:3],
                    op=ALU.mult), waits=[(sDVE, v1a)], inc=sDVE)
                v1 = p.op("vector", lambda e, pp=pp: e.tensor_scalar(
                    out=lnsc[pp][:, 4:5], in0=lnsc[pp][:, 1:2], scalar1=1.0 / D,
                    scalar2=lnsc[pp][:, 3:4], op0=ALU.mult, op1=ALU.subtract),
                    waits=[(sDVE, v1b)], inc=sDVE)
                a1a = p.op("scalar", lambda e, pp=pp: e.activation(
                    out=lnsc[pp][:, 5:6], in_=lnsc[pp][:, 4:5], func=AF.Ln, bias=cfix[:, 0:1]),
                    waits=[(sDVE, v1)], inc=sACT)
                a1 = p.op("scalar", lambda e, pp=pp: e.activation(
                    out=lnsc[pp][:, 6:7], in_=lnsc[pp][:, 5:6], func=AF.Exp, scale=cfix[:, 1:2]),
                    waits=[(sACT, a1a)], inc=sACT)
                c0_ln[j] = p.op("vector", lambda e, pp=pp: e.tensor_scalar(
                    out=xb[pp][:, :], in0=xb[pp][:, :], scalar1=lnsc[pp][:, 2:3],
                    scalar2=lnsc[pp][:, 6:7], op0=ALU.subtract, op1=ALU.mult),
                    waits=[(sACT, a1)], inc=sDVE)

                def tx(e, pp=pp):
                    for dc in range(8):
                        inst = e.matmul(pTf[:, dc * 128:(dc + 1) * 128],
                                        xb[pp][:, dc * 128:(dc + 1) * 128], eyef_s[:, :],
                                        is_transpose=True, start=(dc % 4 == 0), stop=(dc % 4 == 3),
                                        skip_group_check=True)
                    return inst
                c0_t[j] = p.op("tensor", tx,
                               waits=[(sDVE, c0_ln[j]), (sACT, c0_xe.get(j - 1, 0)),
                                      (sACT, c0_xr.get(j - 1, 0))], inc=sPE)
                dvx = xnF[:, :].rearrange("p (dc n) -> p dc n", n=NQ)[:, :, j * 128:(j + 1) * 128]
                c0_xe[j] = p.op("scalar", lambda e, dvx=dvx: e.activation(
                    out=dvx, in_=pTf[:, :].rearrange("p (dc n) -> p dc n", n=128), func=AF.Copy),
                    waits=[(sPE, c0_t[j])], inc=sACT)
                c0_xr[j] = p.op("scalar", lambda e, pp=pp: e.activation(
                    out=xr32[pp], in_=pTf[:, :], func=AF.Copy),
                    waits=[(sPE, c0_t[j]), (sPE, c0_r.get(j - 1, 0))], inc=sACT)

                def rmm(e, pp=pp):
                    for dc in range(8):
                        inst = e.matmul(pR[:, 0:E], xr32[pp][:, dc * 128:(dc + 1) * 128],
                                        rw_sb[:, dc * E:(dc + 1) * E],
                                        start=(dc == 0), stop=(dc == 7), skip_group_check=True)
                    return inst
                c0_r[j] = p.op("tensor", rmm,
                               waits=[(sACT, c0_xr[j]), (sDVE, c0_m.get(j - 1, 0))], inc=sPE)

                rp0 = p.op("vector", lambda e: e.tensor_copy(out=lgt[:, :], in_=pR[:, 0:E]),
                           waits=[(sPE, c0_r[j])], inc=sDVE)
                rp1v = p.op("vector", lambda e: e.tensor_reduce(
                    out=nlmax[:, :], in_=lgt[:, :], axis=AX.X, op=ALU.max, negate=True),
                    waits=[(sDVE, rp0)], inc=sDVE)
                rp2v = p.op("scalar", lambda e: e.activation(
                    out=ejk[:, :], in_=lgt[:, :], func=AF.Exp, bias=nlmax[:, 0:1],
                    accum_out=ssum[:, :]), waits=[(sDVE, rp1v)], inc=sACT)
                rp3a = p.op("vector", lambda e: e.max(out=mx8[:, :], in_=lgt[:, :]),
                            waits=[(sACT, rp2v)], inc=sDVE)
                rp3b = p.op("vector", lambda e: e.max_index(
                    out=ix8[:, :], in_max=mx8[:, :], in_values=lgt[:, :]),
                    waits=[(sDVE, rp3a)], inc=sDVE)
                rp3c = p.op("vector", lambda e: (
                    e.tensor_copy(out=ixf[:, :], in_=ix8[:, 0:1]),
                    e.reciprocal(out=wrec[:, :], in_=ssum[:, :]))[-1],
                    waits=[(sDVE, rp3b)], inc=sDVE)
                c0_m[j] = p.op("vector", lambda e, j=j: e.tensor_scalar(
                    out=wmask[:, j * E:(j + 1) * E], in0=iotaB[:, :],
                    scalar1=ixf[:, 0:1], scalar2=wrec[:, 0:1],
                    op0=ALU.is_equal, op1=ALU.mult), waits=[(sDVE, rp3c)], inc=sDVE)

            if debug == "WT" and q == 0:
                dW0 = p.op("vector", lambda e: e.tensor_copy(
                    out=scr32[:, 0:1024], in_=wt_b[:, 0:1024]),
                    waits=[(sMS, sMS.n), (sBB, sBB.n), (sWT, sWT.n),
                           (sDVE, c0_m[NSUB_Q - 1])], inc=sDVE)
                p.op("sync", lambda e: e.dma_start(out=dbg[:, 0:1024], in_=scr32[:, 0:1024]),
                     waits=[(sDVE, dW0)], inc=sST, inck=16)
                dW1 = p.op("vector", lambda e: e.tensor_copy(
                    out=scr32[:, 0:1024], in_=wt_a[:, 0:1024]),
                    waits=[(sST, sST.n)], inc=sDVE)
                p.op("sync", lambda e: e.dma_start(out=dbg[:, 1024:2048], in_=scr32[:, 0:1024]),
                     waits=[(sDVE, dW1)], inc=sST, inck=16)
                dW2 = p.op("vector", lambda e: e.tensor_copy(
                    out=scr32[:, 0:1024], in_=projT[:, 0:512].broadcast_to([128, 512]) if False else wt_b[:, 1024:2048]),
                    waits=[(sST, sST.n)], inc=sDVE)
                p.op("sync", lambda e: e.dma_start(out=dbg[:, 2048:3072], in_=scr32[:, 0:1024]),
                     waits=[(sDVE, dW2)], inc=sST, inck=16)
                p.op("sync", lambda e: None, waits=[(sST, sST.n)])
                _emit_bodies(p, block)
                return nc

            if debug == "C0" and q == 0:
                d1 = p.op("vector", lambda e: e.tensor_copy(
                    out=scr32[:, 0:NSUB_Q * E], in_=wmask[:, :]),
                    waits=[(sDVE, c0_m[NSUB_Q - 1])], inc=sDVE)
                p.op("sync", lambda e: e.dma_start(out=dbg[:, 0:NSUB_Q * E],
                                                   in_=scr32[:, 0:NSUB_Q * E]),
                     waits=[(sDVE, d1)], inc=sST, inck=16)
                p.op("sync", lambda e: None, waits=[(sST, sST.n)])
                _emit_bodies(p, block)
                return nc

            p.barrier(bar, drains=[("sync", sMS), ("sync", sST), ("sync", sX[0]), ("sync", sX[1]), ("gpsimd", sWD), ("gpsimd", sWT), ("gpsimd", sW1), ("gpsimd", sW2), ("gpsimd", sSL[0]), ("gpsimd", sSL[1]), ("sync", sBB)])

            pH = (bank(0), bank(1))
            pO = (bank(2, F32, 2), bank(4, F32, 2))
            pB1 = bank(6, BF16)[:, 0:32]
            gelu_d = {}
            outmm_d = {}
            stt_d = {}
            b1tv_prev = {"v": 0}
            out_em = {0: 0, 1: 0}
            for eix in range(E):
                em = eix % 2
                b1_ld = p.op("gpsimd", lambda e, eix=eix: e.dma_start(
                    out=b1row[0:1, :], in_=b1[eix:eix + 1, :]),
                    waits=[(sPE, b1tv_prev["v"]), (sACT, b1_rdy[em])], inc=sSL[em], inck=16)
                b2_ld = p.op("gpsimd", lambda e, eix=eix, em=em: e.dma_start(
                    out=b2r[em][0:1, :], in_=b2[eix:eix + 1, :]),
                    waits=[(sPE, out_em[em])], inc=sSL[em], inck=16)
                b2_rdy[em] = b2_ld
                w1_ld["v"] = p.op("gpsimd", lambda e, eix=eix: e.dma_start(
                    out=w1buf[:, :].rearrange("p (dc c) -> p dc c", c=FF),
                    in_=w1[eix].rearrange("(dc p) c -> p dc c", p=128)),
                    waits=[(sPE, hid_last["pe"])], inc=sW1, inck=16)
                w2_ld["v"] = p.op("gpsimd", lambda e, eix=eix: e.dma_start(
                    out=w2buf[:, :].rearrange("p (fc c) -> p fc c", c=D),
                    in_=w2[eix].rearrange("(fc p) c -> p fc c", p=128)),
                    waits=[(sPE, out_last["pe"])], inc=sW2, inck=16)

                def b1t(e):
                    for k in range(16):
                        inst = e.matmul(pB1[:, 2 * k:2 * k + 1],
                                        b1row[0:1, k * 128:(k + 1) * 128],
                                        eyeb_s[0:1, 0:1],
                                        is_transpose=True, start=(k == 0), stop=(k == 15),
                                        skip_group_check=True)
                    return inst
                b1tv = p.op("tensor", b1t, waits=[(sSL[em], b1_ld), (sACT, b1_rdy[em])], inc=sPE)
                b1tv_prev["v"] = b1tv
                b1_rdy[em] = p.op("scalar", lambda e, em=em: e.activation(
                    out=b1c[em][:, :],
                    in_=pB1[:, 0:32].rearrange("p (k two) -> p k two", two=2)[:, :, 0:1],
                    func=AF.Copy),
                    waits=[(sPE, b1tv), (sACT, gelu_last[em])], inc=sACT)
                b1_now = b1_rdy[em]

                for g in range(2):
                    for fo in range(16):
                        def hmm(e, g=g, fo=fo):
                            for dc in range(8):
                                inst = e.matmul(
                                    pH[fo % 2][:, :],
                                    w1buf[:, dc * FF + fo * 128:dc * FF + (fo + 1) * 128],
                                    xnF[:, dc * NQ + g * 512:dc * NQ + (g + 1) * 512],
                                    start=(dc == 0), stop=(dc == 7), skip_group_check=True)
                            return inst
                        pw = gelu_d.get((g, fo - 2), 0) if fo >= 2 else \
                            gelu_d.get((1 if g == 0 else g - 1, fo + 14), 0)
                        hv = p.op("tensor", hmm,
                                  waits=[(sW1, w1_ld["v"] if (g == 0 and fo == 0) else 0),
                                         (sACT, pw)],
                                  inc=sPE)
                        if g == 1 and fo == 15:
                            hid_last["pe"] = hv
                        gwl = [(sPE, hv), (sACT, b1_now if fo == 0 and g == 0 else 0),
                               (sPE, outmm_d.get((g, 3), 0) if fo <= 1 else 0)]
                        gelu_d[(g, fo)] = p.op("scalar", lambda e, g=g, fo=fo, em=em: e.activation(
                            out=htile(g, fo), in_=pH[fo % 2][:, :],
                            func=AF.Gelu, bias=b1c[em][:, fo:fo + 1]),
                            waits=gwl, inc=sACT)
                gelu_last[em] = gelu_d[(1, 15)]
                for g in range(2):
                    for ts_ in range(4):
                        s = g * 4 + ts_

                        def omm(e, g=g, ts_=ts_, em=em):
                            po = pO[ts_ % 2]
                            for fc in range(16):
                                for hf in range(2):
                                    e.matmul(po[:, hf * 512:(hf + 1) * 512],
                                             htile(g, fc)[:, ts_ * 128:(ts_ + 1) * 128],
                                             w2buf[:, fc * D + hf * 512:fc * D + (hf + 1) * 512],
                                             start=(fc == 0), stop=False, skip_group_check=True)
                            for hf in range(2):
                                inst = e.matmul(po[:, hf * 512:(hf + 1) * 512],
                                                onesrow_b[0:1, :],
                                                b2r[em][0:1, hf * 512:(hf + 1) * 512],
                                                start=False, stop=True, skip_group_check=True)
                            return inst
                        owl = [(sACT, gelu_d[(g, 15)]),
                               (sW2, w2_ld["v"] if (g == 0 and ts_ == 0) else 0),
                               (sSL[em], b2_rdy[em] if (g == 0 and ts_ == 0) else 0),
                               (sDVE, stt_d.get((g, ts_ - 2), 0) if ts_ >= 2 else
                                stt_d.get((0 if g == 1 else 1, ts_ + 2), 0))]
                        outmm_d[(g, ts_)] = p.op("tensor", omm, waits=owl, inc=sPE)
                        if g == 1 and ts_ == 3:
                            out_last["pe"] = outmm_d[(g, ts_)]
                            out_em[em] = outmm_d[(g, ts_)]

                        def sttf(e, s=s, ts_=ts_, eix=eix):
                            po = pO[ts_ % 2]
                            acc_sl = accq[:, s * 1024:(s + 1) * 1024]
                            msk = wmask[:, s * E + eix:s * E + eix + 1]
                            if eix == 0:
                                return e.tensor_scalar(out=acc_sl, in0=po[:, :], scalar1=msk,
                                                       scalar2=None, op0=ALU.mult)
                            return e.scalar_tensor_tensor(out=acc_sl, in0=po[:, :], scalar=msk,
                                                          in1=acc_sl, op0=ALU.mult, op1=ALU.add)
                        stt_d[(g, ts_)] = p.op("vector", sttf,
                                               waits=[(sPE, outmm_d[(g, ts_)]),
                                                      (sDVE, fin_d.get((q - 1, 7), 0) if eix == 0 else 0)],
                                               inc=sDVE)

            if debug == "C1" and q == 0:
                last_store = 0
                for half in range(2):
                    d1 = p.op("vector", lambda e, half=half: e.tensor_copy(
                        out=scr32[:, 0:2048], in_=accq[:, half * 2048:(half + 1) * 2048]),
                        waits=[(sDVE, stt_d[(1, 3)]), (sST, last_store)], inc=sDVE)
                    last_store = p.op("sync", lambda e, half=half: e.dma_start(
                        out=dbg[:, half * 2048:(half + 1) * 2048], in_=scr32[:, 0:2048]),
                        waits=[(sDVE, d1)], inc=sST, inck=16)
                p.op("sync", lambda e: None, waits=[(sST, sST.n)])
                _emit_bodies(p, block)
                return nc

            p.barrier(bar, drains=[("sync", sMS), ("sync", sST), ("sync", sX[0]), ("sync", sX[1]), ("gpsimd", sWD), ("gpsimd", sWT), ("gpsimd", sW1), ("gpsimd", sW2), ("gpsimd", sSL[0]), ("gpsimd", sSL[1]), ("sync", sBB)])

            pQ = bank(0, F32, 2)
            pQT = bank(6, BF16, 2)
            pDKdp = (bank(2), bank(3))
            pAO = bank(4, F32, 2)
            pWo = bank(6, F32, 2)
            d_dma = {}; d_q = {}; d_qe = {}; d_sq = {}; d_dk = {}; d_qp = {}; d_ek = {}
            d_pr = {}; d_qt = {}; d_qte = {}; d_ao = {}; d_aoe = {}; d_wo = {}

            def emit_qmm(j):
                def qmm(e, j=j):
                    for ic in range(8):
                        for dc in range(8):
                            e.matmul(pQ[:, ic * 128:(ic + 1) * 128],
                                     wt_a[:, dc * 1024 + ic * 128:dc * 1024 + (ic + 1) * 128],
                                     xnF[:, dc * NQ + j * 128:dc * NQ + (j + 1) * 128],
                                     start=(ic % 4 == 0 and dc == 0), stop=False,
                                     skip_group_check=True)
                    for ic in range(8):
                        inst = e.matmul(pQ[:, ic * 128:(ic + 1) * 128],
                                        bqr[0:1, ic * 128:(ic + 1) * 128], onesrow_b[0:1, :],
                                        start=False, stop=(ic % 4 == 3), skip_group_check=True)
                    return inst
                d_q[j] = p.op("tensor", qmm,
                              waits=[(sACT, d_qe.get(j - 1, 0))], inc=sPE)
            emit_qmm(0)
            for j in range(NSUB_Q):
                pp = j % 2
                row0 = q * NQ + j * 128
                d_dma[j] = p.op("sync", lambda e, row0=row0, pp=pp: e.dma_start(
                    out=xb[pp][:, :], in_=x_own[row0:row0 + 128, :]),
                    waits=[(sDVE, fin_d.get((q, j - 2), 0))], inc=sX[pp], inck=16)
                d_qe[j] = p.op("scalar", lambda e: e.activation(
                    out=qfb[:, :], in_=pQ[:, :], func=AF.Copy),
                    waits=[(sPE, d_q[j]), (sPE, d_dk.get(j - 1, 0))], inc=sACT)
                d_sq[j] = p.op("vector", lambda e: e.tensor_tensor(
                    out=sqb[:, :], in0=qfb[:, :], in1=qfb[:, :], op=ALU.mult),
                    waits=[(sACT, d_qe[j]), (sPE, d_dk.get(j - 1, 0))], inc=sDVE)

                ekw = {}; qpw = {}; dkw = {}
                for w in range(2):
                    def dq_wave(e, w=w):
                        for k in range(8):
                            h = 8 * w + k
                            rb = (h % 2) * 64
                            sl = k // 2
                            pb = pDKdp[h % 2]
                            cb = (h // 2) * 128
                            qf_h = qfb[rb:rb + 64, cb:cb + 128]
                            sq_h = sqb[rb:rb + 64, cb:cb + 128]
                            e.matmul(pb[:, sl * 64:(sl + 1) * 64], qf_h,
                                     projT[rb:rb + 64, (h // 2) * 64:(h // 2) * 64 + 64],
                                     start=(k < 2), stop=False, skip_group_check=True,
                                     tile_position=(rb, 0))
                            e.matmul(pb[:, sl * 64:(sl + 1) * 64], sq_h, negq[rb:rb + 64, :],
                                     start=False, stop=False, skip_group_check=True,
                                     tile_position=(rb, 0))
                            inst = e.matmul(pb[:, 256 + sl:257 + sl], sq_h,
                                            posq[rb:rb + 64, :],
                                            start=False, stop=(k >= 6), skip_group_check=True,
                                            tile_position=(rb, 0))
                        return inst
                    if w == 0:
                        wlw = [(sDVE, d_sq[j]), (sACT, d_ek.get(j - 1, 0)),
                               (sDVE, d_qp.get(j - 1, 0))]
                    else:
                        wlw = [(sACT, ekw[0]), (sDVE, qpw[0])]
                    dkw[w] = p.op("tensor", dq_wave, waits=wlw, inc=sPE)

                    def qpf1(e, w=w):
                        for par in range(2):
                            c0 = 8 * w + 4 * par
                            inst = e.tensor_reduce(
                                out=stsc[:, c0:c0 + 4],
                                in_=pDKdp[par][:, 0:256].rearrange("p (t n) -> p t n", n=64),
                                axis=AX.X, op=ALU.max, negate=True)
                        return inst
                    qv1 = p.op("vector", qpf1, waits=[(sPE, dkw[w])], inc=sDVE)

                    def qpf2(e, w=w):
                        for par in range(2):
                            c0 = 8 * w + 4 * par
                            inst = e.tensor_tensor(out=nstab[:, c0:c0 + 4],
                                                   in0=stsc[:, c0:c0 + 4],
                                                   in1=pDKdp[par][:, 256:260], op=ALU.subtract)
                        return inst
                    qpw[w] = p.op("vector", qpf2, waits=[(sDVE, qv1)], inc=sDVE)

                    def ekf(e, w=w):
                        for k in range(8):
                            h = 8 * w + k
                            mc = mcol(h)
                            inst = e.activation(out=qeb[:, h * 64:(h + 1) * 64],
                                                in_=pDKdp[h % 2][:, (k // 2) * 64:(k // 2 + 1) * 64],
                                                func=AF.Exp, bias=nstab[:, mc:mc + 1])
                        return inst
                    ekw[w] = p.op("scalar", ekf,
                                  waits=[(sDVE, qpw[w]),
                                         (sDVE, d_pr.get(j - 1, 0) if w == 0 else 0)], inc=sACT)
                d_dk[j] = dkw[1]; d_qp[j] = qpw[1]; d_ek[j] = ekw[1]

                def prf1(e):
                    e.tensor_scalar(out=qpb[:, :], in0=qeb[:, :], scalar1=RATIO,
                                    scalar2=EPS_FEAT, op0=ALU.mult, op1=ALU.add)
                    e.tensor_tensor(out=qprod[:, :], in0=qpb[:, :], in1=ksumB[:, :], op=ALU.mult)
                    return e.tensor_reduce(out=den16[:, :],
                                           in_=qprod[:, :].rearrange("p (h n) -> p h n", n=64),
                                           axis=AX.X, op=ALU.add)
                pr1 = p.op("vector", prf1,
                           waits=[(sACT, d_ek[j]), (sPE, d_qt.get(j - 1, 0))], inc=sDVE)
                pr2 = p.op("vector", lambda e: e.tensor_scalar(
                    out=den16[:, :], in0=den16[:, :], scalar1=EPS_DEN, scalar2=None,
                    op0=ALU.max), waits=[(sDVE, pr1)], inc=sDVE)
                pr3 = p.op("vector", lambda e: e.reciprocal(out=rden16[:, :], in_=den16[:, :]),
                           waits=[(sDVE, pr2)], inc=sDVE)
                d_pr[j] = p.op("vector", lambda e: e.tensor_tensor(
                    out=qnb[:, :].rearrange("p (h n) -> p h n", n=64),
                    in0=qpb[:, :].rearrange("p (h n) -> p h n", n=64),
                    in1=rden16[:, :].rearrange("p (h one) -> p h one", one=1).broadcast_to([128, 16, 64]),
                    op=ALU.mult), waits=[(sDVE, pr3)], inc=sDVE)

                if j + 1 < NSUB_Q:
                    emit_qmm(j + 1)

                def qtf(e):
                    for h in range(H):
                        inst = e.matmul(pQT[0:64, h * 128:(h + 1) * 128],
                                        qnb[:, h * 64:(h + 1) * 64], eyeb_s[:, :],
                                        is_transpose=True, start=(h % 8 == 0), stop=(h % 8 == 7),
                                        skip_group_check=True)
                    return inst
                d_qt[j] = p.op("tensor", qtf,
                               waits=[(sDVE, d_pr[j]), (sDVE, fin_d.get((q, j - 1), 0))],
                               inc=sPE)
                d_qte[j] = p.op("vector", lambda e: e.tensor_copy(
                    out=qtb[:, :], in_=pQT[0:64, 0:2048]),
                    waits=[(sPE, d_qt[j])], inc=sDVE)

                def aof(e):
                    for h in range(H):
                        a = h % 8
                        rb = (h // 8) * 64
                        inst = e.matmul(pAO[rb:rb + 64, a * 128:(a + 1) * 128],
                                        kv_sb[:, h * 65:h * 65 + 64],
                                        qtb[:, h * 128:(h + 1) * 128],
                                        start=(h % 4 == 0), stop=(h == 11 or h == 15),
                                        skip_group_check=True, tile_position=(0, rb))
                    return inst
                d_ao[j] = p.op("tensor", aof,
                               waits=[(sDVE, d_qte[j]), (sACT, d_aoe.get(j - 1, 0))], inc=sPE)
                d_aoe[j] = p.op("scalar", lambda e: e.activation(
                    out=apre[:, :], in_=pAO[:, :], func=AF.Copy),
                    waits=[(sPE, d_ao[j]), (sPE, d_wo.get(j - 1, 0))], inc=sACT)

                def wof(e):
                    for nb in range(2):
                        for a in range(8):
                            e.matmul(pWo[:, nb * 512:(nb + 1) * 512],
                                     apre[:, a * 128:(a + 1) * 128],
                                     wt_b[:, a * 1024 + nb * 512:a * 1024 + (nb + 1) * 512],
                                     start=(a == 0), stop=False, skip_group_check=True)
                    for nb in range(2):
                        inst = e.matmul(pWo[:, nb * 512:(nb + 1) * 512], onesrow_b[0:1, :],
                                        bor[0:1, nb * 512:(nb + 1) * 512],
                                        start=False, stop=True, skip_group_check=True)
                    return inst
                if debug == "D0" and q == 0 and j == 0:
                    dD0 = p.op("vector", lambda e: e.tensor_copy(
                        out=scr32[:, 0:1024], in_=apre[:, 0:1024]),
                        waits=[(sACT, d_aoe[0])], inc=sDVE)
                    p.op("sync", lambda e: e.dma_start(out=dbg[:, 0:1024], in_=scr32[:, 0:1024]),
                         waits=[(sDVE, dD0)], inc=sST, inck=16)
                    dD1 = p.op("vector", lambda e: e.tensor_copy(
                        out=scr32[0:64, 0:1024], in_=qtb[0:64, 0:2048].bitcast(F32)[0:64, 0:1024]),
                        waits=[(sST, sST.n)], inc=sDVE)
                    p.op("sync", lambda e: e.dma_start(out=dbg[:, 1024:2048], in_=scr32[:, 0:1024]),
                         waits=[(sDVE, dD1)], inc=sST, inck=16)
                    dD2 = p.op("vector", lambda e: e.tensor_copy(
                        out=scr32[:, 0:16], in_=den16[:, :]),
                        waits=[(sST, sST.n)], inc=sDVE)
                    p.op("sync", lambda e: e.dma_start(out=dbg[:, 2048:2064], in_=scr32[:, 0:16]),
                         waits=[(sDVE, dD2)], inc=sST, inck=16)
                    p.op("sync", lambda e: None, waits=[(sST, sST.n)])
                    _emit_bodies(p, block)
                    return nc

                d_wo[j] = p.op("tensor", wof,
                               waits=[(sACT, d_aoe[j])], inc=sPE)

                def finf(e, j=j, pp=pp):
                    s = j
                    e.scalar_tensor_tensor(out=outb[:, :], in0=pWo[:, :], scalar=1.0,
                                           in1=accq[:, s * 1024:(s + 1) * 1024],
                                           op0=ALU.mult, op1=ALU.add)
                    return e.tensor_tensor(out=outb[:, :], in0=outb[:, :], in1=xb[pp][:, :],
                                           op=ALU.add)
                fin_d[(q, j)] = p.op("vector", finf,
                                     waits=[(sPE, d_wo[j]), (sX[pp], d_dma[j]),
                                            (sST, store_d.get((q, j - 1), 0))], inc=sDVE)
                store_d[(q, j)] = p.op("sync", lambda e, row0=row0: e.dma_start(
                    out=out[row0:row0 + 128, :], in_=outb[:, :]),
                    waits=[(sDVE, fin_d[(q, j)])], inc=sST, inck=16)

            p.barrier(bar, drains=[("sync", sMS), ("sync", sST), ("sync", sX[0]), ("sync", sX[1]), ("gpsimd", sWD), ("gpsimd", sWT), ("gpsimd", sW1), ("gpsimd", sW2), ("gpsimd", sSL[0]), ("gpsimd", sSL[1]), ("sync", sBB)])

        p.op("sync", lambda e: None, waits=[(sST, sST.n)])

        _emit_bodies(p, block)
    return nc


def _emit_bodies(p, block):
    def body(eng):
        def run(e):
            for thunk in p.q[eng]:
                thunk(e)
        return run
    block.sync(body("sync"))
    block.scalar(body("scalar"))
    block.vector(body("vector"))
    block.tensor(body("tensor"))
    block.gpsimd(body("gpsimd"))


_NC_CACHE = {}


def _get_nc(debug=None):
    if debug not in _NC_CACHE:
        _NC_CACHE[debug] = _build(debug)
    return _NC_CACHE[debug]


def make_in_maps(inputs):
    x = np.ascontiguousarray(inputs["L0_attn_query"], dtype=np.float32)
    eyeb = np.eye(128, dtype=ml_dtypes.bfloat16)
    eyef = np.eye(128, dtype=np.float32)
    shared = {
        "wq": np.ascontiguousarray(inputs["L0_attn_Wq"]),
        "wk": np.ascontiguousarray(inputs["L0_attn_Wk"]),
        "wv": np.ascontiguousarray(inputs["L0_attn_Wv"]),
        "wo": np.ascontiguousarray(inputs["L0_attn_Wo"]),
        "bq": np.ascontiguousarray(inputs["L0_attn_bq"]).reshape(1, D),
        "bk": np.ascontiguousarray(inputs["L0_attn_bk"]).reshape(1, D),
        "bv": np.ascontiguousarray(inputs["L0_attn_bv"]).reshape(1, D),
        "bo": np.ascontiguousarray(inputs["L0_attn_bo"]).reshape(1, D),
        "proj": np.ascontiguousarray(inputs["L0_attn_proj"]),
        "rw": np.ascontiguousarray(inputs["L0_ffn_router_w"]),
        "w1": np.ascontiguousarray(inputs["L0_ffn_w1"]),
        "b1": np.ascontiguousarray(inputs["L0_ffn_b1"]),
        "w2": np.ascontiguousarray(inputs["L0_ffn_w2"]),
        "b2": np.ascontiguousarray(inputs["L0_ffn_b2"]),
        "eyeb": eyeb,
        "eyef": eyef,
    }
    in_maps = []
    for c in range(8):
        b, h = c // 2, c % 2
        im = dict(shared)
        im["x_full"] = np.ascontiguousarray(x[b])
        im["x_own"] = np.ascontiguousarray(x[b, h * NOWN:(h + 1) * NOWN])
        in_maps.append(im)
    return in_maps


def kernel(**inputs):
    nc = _get_nc(None)
    in_maps = make_in_maps(inputs)
    res = run_bass_kernel_spmd(nc, in_maps, core_ids=list(range(8)))
    outa = np.empty((B, N, D), np.float32)
    for c in range(8):
        b, h = c // 2, c % 2
        outa[b, h * NOWN:(h + 1) * NOWN] = res.results[c]["out"]
    return outa


# version 10
# speedup vs baseline: 1.2737x; 1.2737x over previous
import math
import numpy as np
import ml_dtypes
from contextlib import ExitStack

import concourse.bass as bass
import concourse.mybir as mybir
from concourse.bass_utils import run_bass_kernel_spmd

F32 = mybir.dt.float32
BF16 = mybir.dt.bfloat16
U32 = mybir.dt.uint32
AF = mybir.ActivationFunctionType
ALU = mybir.AluOpType
AX = mybir.AxisListType

B, N, D, H, DH, J, E, FF = 4, 8192, 1024, 16, 64, 64, 14, 2048
NOWN = 4096
NQ = 1024
NSUB_A = N // 128
NSUB_Q = NQ // 128
DNORM = float(DH) ** -0.25
RATIO = float(J) ** -0.5
EPS_FEAT = 1e-4
EPS_DEN = 0.01
LN_EPS = 1e-5
LN8EM4 = math.log(EPS_FEAT / RATIO)
LNRATIO = math.log(RATIO)

ENGS = ("sync", "scalar", "vector", "tensor", "gpsimd")


class Sem:
    __slots__ = ("h", "n")

    def __init__(self, h):
        self.h = h
        self.n = 0


class Prog:
    def __init__(self):
        self.q = {e: [] for e in ENGS}

    def op(self, eng, fn, waits=(), inc=None, inck=1):
        wl = [(s.h, v) for (s, v) in waits if v and v > 0]
        it = None
        tot = None
        if inc is not None:
            inc.n += inck
            tot = inc.n
            it = (inc.h, inck)

        def thunk(e, fn=fn, wl=wl, it=it):
            for h, v in wl:
                e.wait_ge(h, v)
            inst = fn(e)
            if it is not None:
                inst.then_inc(it[0], it[1])

        self.q[eng].append(thunk)
        return tot

    def barrier(self, bar, drains=()):
        target = bar.n + len(ENGS)
        dr = {}
        for eng, s in drains:
            dr.setdefault(eng, []).append((s.h, s.n))
        for eng in ENGS:
            def thunk(e, bar=bar, target=target, w=dr.get(eng, ())):
                for h, v in w:
                    if v > 0:
                        e.wait_ge(h, v)
                e.sem_inc(bar.h, 1)
                e.wait_ge(bar.h, target)
            self.q[eng].append(thunk)
        bar.n = target


def _build(debug=None, nsub_a=NSUB_A, a_stages=99, vev=4, dkv=5, dbgname='kfb'):
    nc = bass.Bass("TRN2", target_bir_lowering=False, debug=False)

    def din(name, shape, dtype=F32):
        return nc.dram_tensor(name, shape, dtype, kind="ExternalInput").ap()

    x_full = din("x_full", [N, D])
    x_own = din("x_own", [NOWN, D])
    wq = din("wq", [D, D]); wk = din("wk", [D, D]); wv = din("wv", [D, D]); wo = din("wo", [D, D])
    bq = din("bq", [1, D]); bk = din("bk", [1, D]); bv = din("bv", [1, D]); bo = din("bo", [1, D])
    proj = din("proj", [H, J, DH])
    rw = din("rw", [D, E])
    w1 = din("w1", [E, D, FF]); b1 = din("b1", [E, FF])
    w2 = din("w2", [E, FF, D]); b2 = din("b2", [E, D])
    eyeb = din("eyeb", [128, 128], BF16)
    pscr = nc.dram_tensor("pscr", [64, 1024], BF16, kind="Internal").ap()
    wodram = nc.dram_tensor("wodram", [D, D], BF16, kind="Internal").ap()
    eyef = din("eyef", [128, 128], F32)
    out = nc.dram_tensor("out", [NOWN, D], F32, kind="ExternalOutput").ap()
    dbg = nc.dram_tensor("dbg", [128, 4096], F32, kind="ExternalOutput").ap() if debug else None

    p = Prog()

    with ExitStack() as ctx:
        def sbt(name, sh, dt=F32):
            return ctx.enter_context(nc.sbuf_tensor(name, sh, dt))

        wt_a = sbt("wt_a", [128, 8 * 1024], BF16)
        wt_b = sbt("wt_b", [128, 8 * 1024], BF16)
        projT = sbt("projT", [128, 8 * 64], BF16)
        kv_sb = sbt("kv_sb", [64, H * 65], BF16)
        ksumB = sbt("ksumB", [128, H * 64], BF16)
        rw_sb = sbt("rw_sb", [128, 8 * E], F32)
        bqr = sbt("bqr", [1, D], BF16)
        bkr = sbt("bkr", [1, D], BF16)
        bvr = sbt("bvr", [1, D], BF16)
        bor = sbt("bor", [1, D], BF16)
        b2all = sbt("b2all", [14, D], BF16)
        wmTq = sbt("wmTq", [14, NSUB_Q * 128], BF16)
        b1row = sbt("b1row", [1, FF], BF16)
        b1c = [sbt(f"b1c{i}", [128, 16]) for i in range(2)]
        eyeb_s = sbt("eyeb_s", [128, 128], BF16)
        eyef_s = sbt("eyef_s", [128, 128], F32)
        onesrow = sbt("onesrow", [1, 128])
        onesrow_b = sbt("onesrow_b", [1, 128], BF16)
        iotaB = sbt("iotaB", [128, E])
        negq = sbt("negq", [128, 64], BF16)
        posq = sbt("posq", [128, 1], BF16)
        onescol_b = sbt("onescol_b", [128, 1], BF16)
        m_row = sbt("m_row", [1, 16]); a1_row = sbt("a1_row", [1, 16]); c1_row = sbt("c1_row", [1, 16])
        c1b = sbt("c1b", [64, 16])
        trows = sbt("trows", [1, H * 65], BF16)
        mrun = sbt("mrun", [128, 16])
        stsc = sbt("stsc", [128, 16]); nstab = sbt("nstab", [128, 16])
        lgt = sbt("lgt", [128, E]); ejk = sbt("ejk", [128, E], BF16)
        nlmax = sbt("nlmax", [128, 1]); ssum = sbt("ssum", [128, 1])
        wrec = sbt("wrec", [128, 1]); ixf = sbt("ixf", [128, 1])
        mx8 = sbt("mx8", [128, 8]); ix8 = sbt("ix8", [128, 8], U32)
        den16 = sbt("den16", [128, 16]); rden16 = sbt("rden16", [128, 16])
        lnsc = [sbt(f"lnsc{i}", [128, 8]) for i in range(2)]
        cfix = sbt("cfix", [128, 8])
        xnF = sbt("xnF", [128, 8 * NQ], BF16)
        accq = sbt("accq", [128, NQ * 8], BF16)
        w1buf = sbt("w1buf", [128, 8 * FF], BF16)
        w2buf = sbt("w2buf", [128, 16 * D], BF16)
        hidb = sbt("hidb", [128, 16 * 512], BF16)
        wmask = sbt("wmask", [128, NSUB_Q * E])
        xb = [sbt(f"xb{i}", [128, D]) for i in range(2)]
        xnb = [sbt(f"xnb{i}", [128, D], BF16) for i in range(2)]
        scr32 = sbt("scr32", [128, 3 * D])
        xr32 = [scr32[:, 0:D], scr32[:, D:2 * D]]
        qeb = scr32[:, 0:D]; qpb = scr32[:, D:2 * D]; qprod = scr32[:, 2 * D:3 * D]
        xfb = sbt("xfb", [128, D], BF16)
        kfb = sbt("kfb", [128, D], BF16)
        sqb = sbt("sqb", [128, D], BF16)
        vtb = sbt("vtb", [128, H * 65], BF16)
        ekb = sbt("ekb", [128, H * 65 + 66], BF16)
        wst128 = sbt("wst128", [128, 1024], BF16)
        qfb = xnb[1]
        qnb = xnb[0]
        qtb = scr32[0:64, 2 * D:3 * D].bitcast(BF16)
        apre = qnb
        outb = hidb[:, 0:4096].bitcast(F32)[:, 0:1024]
        _hpool = [kfb, sqb, vtb, ekb, xfb, xnb[0], xnb[1], wst128]

        def htile(g, fc):
            if g == 0:
                return hidb[:, fc * 512:(fc + 1) * 512]
            t = _hpool[fc // 2]
            off = (fc % 2) * 512
            return t[:, off:off + 512]
        pstage = scr32[0:64, 0:H * 64]

        pAll = ctx.enter_context(nc.psum_tensor("pAll", [128, 4096], F32))

        def bank(k, dt=F32, nb=1):
            v = pAll[:, k * 512:(k + nb) * 512]
            if dt is BF16:
                v = v.bitcast(BF16)
            return v

        sems = {nm: Sem(ctx.enter_context(nc.semaphore(nm)))
                for nm in ("sIN", "sWD", "sPE", "sACT", "sDVE", "sGP", "sST", "bar",
                           "sMS", "sWT", "sW1", "sW2", "sSL0", "sSL1", "sXA", "sXB", "sBB")}
        sIN, sWD, sPE, sACT, sDVE, sGP, sST = (sems[k] for k in
            ("sIN", "sWD", "sPE", "sACT", "sDVE", "sGP", "sST"))
        sMS, sWT, sW1, sW2 = (sems[k] for k in ("sMS", "sWT", "sW1", "sW2"))
        sBB = sems["sBB"]
        sSL = [sems["sSL0"], sems["sSL1"]]
        sX = [sems["sXA"], sems["sXB"]]
        bar = sems["bar"]

        block = ctx.enter_context(nc.Block())

        p.op("gpsimd", lambda e: e.memset(onesrow[:, :], 1.0), inc=sGP)
        p.op("gpsimd", lambda e: e.memset(onesrow_b[:, :], 1.0), inc=sGP)
        p.op("gpsimd", lambda e: e.memset(negq[:, :], -0.0625), inc=sGP)
        p.op("gpsimd", lambda e: e.memset(posq[:, :], 0.0625), inc=sGP)
        p.op("gpsimd", lambda e: e.memset(onescol_b[:, :], 1.0), inc=sGP)
        p.op("gpsimd", lambda e: e.memset(mrun[:, :], -1e30), inc=sGP)
        for k, val in ((0, LN_EPS), (1, -0.5), (2, LN8EM4), (3, LNRATIO), (4, -1.0), (5, DNORM)):
            p.op("gpsimd", lambda e, k=k, val=val: e.memset(cfix[:, k:k + 1], val), inc=sGP)
        p.op("gpsimd", lambda e: e.iota(iotaB[:, :], pattern=[[1, E]], base=0,
                                        channel_multiplier=0,
                                        allow_small_or_imprecise_dtypes=True), inc=sGP)
        p.op("gpsimd", lambda e: e.memset(
            vtb[:, :].rearrange("p (h n) -> p h n", n=65)[:, :, 64:65], 1.0), inc=sGP)
        p.op("gpsimd", lambda e: e.memset(
            ekb[:, 0:1040].rearrange("p (h n) -> p h n", n=65)[:, :, 64:65], 1.0), inc=sGP)
        for dst, src in ((bqr[0:1, :], bq), (bkr[0:1, :], bk), (bvr[0:1, :], bv),
                         (bor[0:1, :], bo)):
            p.op("gpsimd", lambda e, d=dst, s=src: e.dma_start(out=d, in_=s[:, :]),
                 inc=sWD, inck=16)
        p.op("gpsimd", lambda e: e.dma_start(out=b2all[:, :], in_=b2[:, :]), inc=sWD, inck=16)
        for dst, src in ((eyeb_s, eyeb), (eyef_s, eyef)):
            p.op("sync", lambda e, d=dst, s=src: e.dma_start(out=d[:, :], in_=s[:, :]),
                 inc=sMS, inck=16)
        p.op("sync", lambda e: e.dma_start(
            out=rw_sb[:, :].rearrange("p (c e) -> p c e", e=E),
            in_=rw.rearrange("(c p) e -> p c e", p=128)), inc=sMS, inck=16)
        p.op("sync", lambda e: e.dma_start(
            out=pstage.rearrange("j (h d) -> j h d", d=DH),
            in_=proj.rearrange("h j d -> j h d")), inc=sMS, inck=16)
        setup_in = sMS.n
        setup_gp = sGP.n

        wt_state = {"evac": 0, "war": [0, 0]}

        wstp = [wst128, accq[:, 0:1024]]
        wt_evs = {0: 0, 1: 0}

        def emit_wt(dst, wsrc, extra_pe_wait=0):
            pTWp = [bank(0, BF16)[:, 0:1024], bank(1, BF16)[:, 0:1024]]
            for r in range(8):
                rp = r % 2
                ld = p.op("gpsimd", lambda e, r=r, rp=rp, w=wsrc: e.dma_start(
                    out=wstp[rp][:, :], in_=w[r * 128:(r + 1) * 128, :]),
                    waits=[(sPE, wt_state.get("t" + str(rp), 0))], inc=sWT, inck=16)

                def tfn(e, r=r, rp=rp):
                    for dc in range(8):
                        inst = e.matmul(pTWp[rp][:, dc * 128:(dc + 1) * 128],
                                        wstp[rp][:, dc * 128:(dc + 1) * 128], eyeb_s[:, :],
                                        is_transpose=True, start=(dc == 0), stop=(dc == 7),
                                        skip_group_check=True)
                    return inst
                tv = p.op("tensor", tfn,
                          waits=[(sWT, ld), (sACT, wt_evs[rp]), (sPE, extra_pe_wait),
                                 (sMS, setup_in)], inc=sPE)
                wt_state["t" + str(rp)] = tv
                dv = dst[:, :].rearrange("p (dc n) -> p dc n", n=1024)[:, :, r * 128:(r + 1) * 128]
                ev = p.op("scalar", lambda e, dv=dv, rp=rp: e.activation(
                    out=dv, in_=pTWp[rp].rearrange("p (dc n) -> p dc n", n=128), func=AF.Copy),
                    waits=[(sPE, tv)], inc=sACT)
                wt_evs[rp] = ev
                wt_state["evac"] = ev
            return wt_state["evac"]

        emit_wt(wt_a, wk)
        emit_wt(wt_b, wv)
        pPJ = bank(2, F32, 2)[0:64, :]

        def pj_t(e):
            for h in range(H):
                inst = e.matmul(pPJ[0:64, h * 64:(h + 1) * 64],
                                pstage[:, h * 64:(h + 1) * 64], eyef_s[0:64, 0:64],
                                is_transpose=True, start=(h % 8 == 0), stop=(h % 8 == 7),
                                skip_group_check=True)
            return inst
        pv0 = p.op("tensor", pj_t, waits=[(sMS, setup_in)], inc=sPE)
        pe1 = p.op("scalar", lambda e: e.activation(
            out=hidb[0:64, 0:1024], in_=pPJ[0:64, :], func=AF.Copy, scale=cfix[0:64, 5:6]),
            waits=[(sPE, pv0), (sGP, setup_gp)], inc=sACT)
        pj_st = p.op("sync", lambda e: e.dma_start(out=pscr[:, :], in_=hidb[0:64, 0:1024]),
                     waits=[(sACT, pe1)], inc=sBB, inck=16)
        pjsrc = pscr.rearrange("p (a two d) -> p a two d", two=2, d=64)
        p.op("sync", lambda e: e.dma_start(
            out=projT[0:64, :].rearrange("p (a d) -> p a d", d=64), in_=pjsrc[:, :, 0, :]),
            waits=[(sBB, pj_st)], inc=sMS, inck=16)
        p.op("sync", lambda e: e.dma_start(
            out=projT[64:128, :].rearrange("p (a d) -> p a d", d=64), in_=pjsrc[:, :, 1, :]),
            waits=[(sBB, pj_st)], inc=sMS, inck=16)

        p.barrier(bar, drains=[("sync", sMS), ("sync", sST), ("sync", sX[0]), ("sync", sX[1]), ("gpsimd", sWD), ("gpsimd", sWT), ("gpsimd", sW1), ("gpsimd", sW2), ("gpsimd", sSL[0]), ("gpsimd", sSL[1]), ("sync", sBB)])

        if debug == "SETUP":
            _emit_bodies(p, block)
            return nc

        pT = bank(0, BF16)[:, 0:1024]
        pKV = bank(1, F32, 2)
        pDKp = (bank(3), bank(4))
        pUt = (bank(5)[0:65, 0:455], bank(6)[0:65, 0:455], bank(7)[0:65, 0:130])

        def mcol(h):
            return 8 * (h // 8) + 4 * (h % 2) + (h % 8) // 2

        def u_ap(h):
            return pUt[h // 7], (h % 7) * 65

        dma_d = {}; ln_d = {}; t_d = {}; te_d = {}; kf_d = {}; kfe_d = {}
        sq_d = {}; v_d = {}; ve_d = {}; dk_d = {}; ek_d = {}; st_d = {}; u_d = {}

        junkb = hidb[:, 0:2048].bitcast(BF16)[:, 0:1024]

        def a_ln(i):
            pp = i % 2
            dma_d[i] = p.op("sync", lambda e, i=i, pp=pp: e.dma_start(
                out=xb[pp][:, :], in_=x_full[i * 128:(i + 1) * 128, :]),
                waits=[(sDVE, ln_d.get(i - 2, 0)), (sACT, sqa_d.get(i - 2, 0))],
                inc=sX[pp], inck=16)
            sqa_d[i] = p.op("scalar", lambda e, pp=pp: e.activation(
                out=xnb[pp][:, :], in_=xb[pp][:, :], func=AF.Square,
                accum_out=lnsc[pp][:, 1:2]),
                waits=[(sX[pp], dma_d[i]), (sGP, setup_gp), (sPE, t_d.get(i - 2, 0))], inc=sACT)
            sma = p.op("vector", lambda e, pp=pp: e.tensor_scalar(
                out=junkb[:, :], in0=xb[pp][:, :], scalar1=1.0, scalar2=None, op0=ALU.mult,
                op1=ALU.add, accum_out=lnsc[pp][:, 0:1]),
                waits=[(sX[pp], dma_d[i]), (sPE, t_d.get(i - 2, 0))], inc=sDVE)
            v1a = p.op("vector", lambda e, pp=pp: e.tensor_scalar(
                out=lnsc[pp][:, 2:3], in0=lnsc[pp][:, 0:1], scalar1=1.0 / D, scalar2=None,
                op0=ALU.mult), waits=[(sDVE, sma)], inc=sDVE)
            v1b = p.op("vector", lambda e, pp=pp: e.tensor_tensor(
                out=lnsc[pp][:, 3:4], in0=lnsc[pp][:, 2:3], in1=lnsc[pp][:, 2:3], op=ALU.mult),
                waits=[(sDVE, v1a)], inc=sDVE)
            ve_v = p.op("vector", lambda e, pp=pp: e.tensor_scalar(
                out=lnsc[pp][:, 4:5], in0=lnsc[pp][:, 1:2], scalar1=1.0 / D,
                scalar2=lnsc[pp][:, 3:4], op0=ALU.mult, op1=ALU.subtract),
                waits=[(sDVE, v1b), (sACT, sqa_d[i])], inc=sDVE)
            a1a = p.op("scalar", lambda e, pp=pp: e.activation(
                out=lnsc[pp][:, 5:6], in_=lnsc[pp][:, 4:5], func=AF.Ln, bias=cfix[:, 0:1]),
                waits=[(sDVE, ve_v)], inc=sACT)
            a1 = p.op("scalar", lambda e, pp=pp: e.activation(
                out=lnsc[pp][:, 6:7], in_=lnsc[pp][:, 5:6], func=AF.Exp, scale=cfix[:, 1:2]),
                waits=[(sACT, a1a)], inc=sACT)
            ln_d[i] = p.op("vector", lambda e, pp=pp: e.tensor_scalar(
                out=xnb[pp][:, :], in0=xb[pp][:, :], scalar1=lnsc[pp][:, 2:3],
                scalar2=lnsc[pp][:, 6:7], op0=ALU.subtract, op1=ALU.mult),
                waits=[(sACT, a1)], inc=sDVE)

        def a_heavy(i):
            pp = i % 2

            def tx(e, pp=pp):
                for dc in range(8):
                    inst = e.matmul(pT[:, dc * 128:(dc + 1) * 128],
                                    xnb[pp][:, dc * 128:(dc + 1) * 128], eyeb_s[:, :],
                                    is_transpose=True, start=(dc == 0), stop=(dc == 7),
                                    skip_group_check=True)
                return inst
            t_d[i] = p.op("tensor", tx,
                          waits=[(sDVE, ln_d[i]), (sACT, te_d.get(i - 1, 0))], inc=sPE)
            te_d[i] = p.op("scalar", lambda e: e.activation(
                out=xfb[:, :], in_=pT[:, :], func=AF.Copy),
                waits=[(sPE, t_d[i]), (sPE, v_d.get(i - 1, 0))], inc=sACT)

            def kf(e):
                for ic in range(8):
                    for dc in range(8):
                        e.matmul(pKV[:, ic * 128:(ic + 1) * 128],
                                 wt_a[:, dc * 1024 + ic * 128:dc * 1024 + (ic + 1) * 128],
                                 xfb[:, dc * 128:(dc + 1) * 128],
                                 start=(ic % 4 == 0 and dc == 0), stop=False,
                                 skip_group_check=True)
                for ic in range(8):
                    inst = e.matmul(pKV[:, ic * 128:(ic + 1) * 128],
                                    bkr[0:1, ic * 128:(ic + 1) * 128], onesrow_b[0:1, :],
                                    start=False, stop=(ic % 4 == 3), skip_group_check=True)
                return inst
            kf_d[i] = p.op("tensor", kf,
                           waits=[(sACT, te_d[i]), (sACT, ve_d.get(i - 1, 0))], inc=sPE)
            kfe_d[i] = p.op("scalar", lambda e: e.activation(
                out=kfb[:, :], in_=pKV[:, :], func=AF.Copy),
                waits=[(sPE, kf_d[i]), (sPE, dk_d.get(i - 1, 0))], inc=sACT)
            sq_d[i] = p.op("vector", lambda e: e.tensor_tensor(
                out=sqb[:, :], in0=kfb[:, :], in1=kfb[:, :], op=ALU.mult),
                waits=[(sACT, kfe_d[i]), (sPE, dk_d.get(i - 1, 0))], inc=sDVE)

        sqa_d = {}
        a_ln(0)
        for i in range(nsub_a):
            a_heavy(i)
            if i + 1 < nsub_a:
                a_ln(i + 1)
            pp = i % 2

            def vv(e):
                for nb in range(2):
                    for dc in range(8):
                        e.matmul(pKV[:, nb * 512:(nb + 1) * 512],
                                 xfb[:, dc * 128:(dc + 1) * 128],
                                 wt_b[:, dc * 1024 + nb * 512:dc * 1024 + (nb + 1) * 512],
                                 start=(dc == 0), stop=False, skip_group_check=True)
                for nb in range(2):
                    inst = e.matmul(pKV[:, nb * 512:(nb + 1) * 512],
                                    onesrow_b[0:1, :], bvr[0:1, nb * 512:(nb + 1) * 512],
                                    start=False, stop=True, skip_group_check=True)
                return inst
            v_d[i] = p.op("tensor", vv, waits=[(sACT, kfe_d[i])], inc=sPE)
            ve_d[i] = p.op("scalar", lambda e: e.activation(
                out=vtb[:, :].rearrange("p (h n) -> p h n", n=65)[:, :, 0:64],
                in_=pKV[:, :].rearrange("p (h n) -> p h n", n=64), func=AF.Copy),
                waits=[(sPE, v_d[i]), (sPE, u_d.get(i - 1, 0))], inc=sACT)

            dkw = {}; ekw = {}
            for w in range(2):
                def dk_wave(e, w=w):
                    for k in range(8):
                        h = 8 * w + k
                        rb = (h % 2) * 64
                        sl = k // 2
                        pb = pDKp[h % 2]
                        cb = (h // 2) * 128
                        kf_h = kfb[rb:rb + 64, cb:cb + 128]
                        sq_h = sqb[rb:rb + 64, cb:cb + 128]
                        e.matmul(pb[:, sl * 64:(sl + 1) * 64], kf_h,
                                 projT[rb:rb + 64, (h // 2) * 64:(h // 2) * 64 + 64],
                                 start=(k < 2), stop=False,
                                 skip_group_check=True, tile_position=(rb, 0))
                        e.matmul(pb[:, sl * 64:(sl + 1) * 64], sq_h, negq[rb:rb + 64, :],
                                 start=False, stop=False, skip_group_check=True,
                                 tile_position=(rb, 0))
                        inst = e.matmul(pb[:, 256 + sl:257 + sl], sq_h,
                                        posq[rb:rb + 64, :],
                                        start=(False), stop=(k >= 6), skip_group_check=True,
                                        tile_position=(rb, 0))
                    return inst
                if w == 0:
                    wlw = [(sDVE, sq_d[i]), (sACT, ek_d.get(i - 1, 0))]
                else:
                    wlw = [(sACT, ekw[0])]
                dkw[w] = p.op("tensor", dk_wave, waits=wlw, inc=sPE)

                def ekfn(e, w=w):
                    for par in range(2):
                        base = (8 * w + par) * 65
                        ov = ekb[:, base:base + 4 * 130].rearrange(
                            "p (t n) -> p t n", n=130)[:, :, 0:64]
                        inst = e.activation(
                            out=ov,
                            in_=pDKp[par][:, 0:256].rearrange("p (t n) -> p t n", n=64),
                            func=AF.Exp)
                    return inst
                ekw[w] = p.op("scalar", ekfn,
                    waits=[(sPE, dkw[w]),
                           (sPE, u_d.get(i - 1, 0) if w == 0 else 0),
                           (sDVE, st_d.get(i - 1, 0) if w == 0 else 0)],
                    inc=sACT)

            dk_d[i] = dkw[1]
            ek_d[i] = ekw[1]
            sv1 = p.op("vector", lambda e: e.tensor_reduce(
                out=stsc[:, 0:16],
                in_=ekb[:, 0:1040].rearrange("p (h n) -> p h n", n=65),
                axis=AX.X, op=ALU.max), waits=[(sACT, ekw[1])], inc=sDVE)
            st_d[i] = p.op("vector", lambda e: e.tensor_tensor(
                out=mrun[:, :], in0=mrun[:, :], in1=stsc[:, 0:16], op=ALU.max),
                waits=[(sDVE, sv1)], inc=sDVE)

            def uacc(e, i=i):
                for h in range(H):
                    ut, c = u_ap(h)
                    e.matmul(ut[:, c:c + 64],
                             ekb[:, h * 65:(h + 1) * 65], vtb[:, h * 65:h * 65 + 64],
                             start=(i == 0 and h % 7 == 0), stop=False, skip_group_check=True)
                    inst = e.matmul(ut[:, c + 64:c + 65],
                                    ekb[:, h * 65:(h + 1) * 65], onescol_b[:, :],
                                    start=False, stop=False, skip_group_check=True)
                return inst
            u_d[i] = p.op("tensor", uacc,
                          waits=[(sACT, ek_d[i]), (sACT, ve_d[i])], inc=sPE)

        fv = p.op("tensor", lambda e: e.matmul(
            pDKp[0][0:16, 0:128], mrun[:, :], eyef_s[:, :], is_transpose=True,
            start=True, stop=True, skip_group_check=True),
            waits=[(sDVE, st_d[nsub_a - 1]), (sPE, u_d[nsub_a - 1]),
                   (sACT, ek_d[nsub_a - 1])], inc=sPE)
        fv2 = p.op("vector", lambda e: e.tensor_reduce(
            out=stsc[0:16, 0:1], in_=pDKp[0][0:16, 0:128], axis=AX.X, op=ALU.max),
            waits=[(sPE, fv)], inc=sDVE)
        fv3 = p.op("tensor", lambda e: e.matmul(
            pDKp[1][0:1, 0:16], stsc[0:16, 0:1], eyef_s[0:16, 0:16], is_transpose=True,
            start=True, stop=True, skip_group_check=True), waits=[(sDVE, fv2)], inc=sPE)
        fv4 = p.op("scalar", lambda e: e.activation(
            out=m_row[:, :], in_=pDKp[1][0:1, 0:16], func=AF.Ln), waits=[(sPE, fv3)], inc=sACT)

        def fexp(e):
            e.activation(out=a1_row[:, :], in_=m_row[:, :], func=AF.Exp, bias=cfix[0:1, 2:3])
            return e.activation(out=c1_row[:, :], in_=m_row[:, :], func=AF.Exp,
                                scale=cfix[0:1, 4:5], bias=cfix[0:1, 3:4])
        fv5 = p.op("scalar", fexp, waits=[(sACT, fv4)], inc=sACT)

        def ftr(e):
            for h in range(H):
                ut, c = u_ap(h)
                inst = e.activation(out=trows[0:1, h * 65:(h + 1) * 65],
                                    in_=ut[64:65, c:c + 65], func=AF.Copy,
                                    scale=a1_row[0:1, h:h + 1])
            return inst
        fv6 = p.op("scalar", ftr, waits=[(sACT, fv5)], inc=sACT)

        def fcorr(e):
            for h in range(H):
                ut, c = u_ap(h)
                last = (h % 7 == 6) or (h == H - 1)
                inst = e.matmul(ut[0:64, c:c + 65], onesrow_b[0:1, 0:64],
                                trows[0:1, h * 65:(h + 1) * 65],
                                start=False, stop=last, skip_group_check=True)
            return inst
        fv7 = p.op("tensor", fcorr, waits=[(sACT, fv6)], inc=sPE)
        fv8 = p.op("tensor", lambda e: e.matmul(
            pDKp[0][0:64, 16:32], onesrow[0:1, 0:64], c1_row[0:1, :],
            start=True, stop=True, skip_group_check=True),
            waits=[(sACT, fv6), (sDVE, fv2)], inc=sPE)
        fv9 = p.op("scalar", lambda e: e.activation(
            out=c1b[:, :], in_=pDKp[0][0:64, 16:32], func=AF.Copy), waits=[(sPE, fv8)], inc=sACT)

        def fkv(e):
            for h in range(H):
                ut, c = u_ap(h)
                inst = e.activation(out=kv_sb[:, h * 65:(h + 1) * 65],
                                    in_=ut[0:64, c:c + 65], func=AF.Copy,
                                    scale=c1b[:, h:h + 1])
            return inst
        fv10 = p.op("scalar", fkv, waits=[(sPE, fv7), (sACT, fv9)], inc=sACT)
        pKS = bank(0, BF16)[0:1, 0:1024]

        def fkst(e):
            for h in range(H):
                inst = e.matmul(pKS[0:1, h * 64:(h + 1) * 64],
                                kv_sb[:, h * 65 + 64:h * 65 + 65], eyeb_s[0:64, 0:64],
                                is_transpose=True, start=(h == 0), stop=(h == H - 1),
                                skip_group_check=True)
            return inst
        fv11 = p.op("tensor", fkst, waits=[(sACT, fv10)], inc=sPE)
        fv12 = p.op("scalar", lambda e: e.activation(
            out=trows[0:1, 0:1024], in_=pKS[0:1, :], func=AF.Copy),
            waits=[(sPE, fv11)], inc=sACT)
        pKB = bank(1, F32, 2)

        def fksb(e):
            for nb in range(2):
                inst = e.matmul(pKB[:, nb * 512:(nb + 1) * 512], onesrow_b[0:1, :],
                                trows[0:1, nb * 512:(nb + 1) * 512],
                                start=True, stop=True, skip_group_check=True)
            return inst
        fv13 = p.op("tensor", fksb, waits=[(sACT, fv12)], inc=sPE)
        fv14 = p.op("scalar", lambda e: e.activation(
            out=ksumB[:, :], in_=pKB[:, :], func=AF.Copy), waits=[(sPE, fv13)], inc=sACT)

        if debug == "A":
            dbgsrc = {"kfb": kfb[:, 0:1024], "xfb": xfb[:, 0:1024], "sqb": sqb[:, 0:1024],
                      "vtb": vtb[:, 0:1024], "ekb": ekb[:, 0:1024],
                      "xnb": xnb[0][:, 0:1024], "lnsc": lnsc[0][:, 0:8]}[dbgname]
            d0 = p.op("vector", lambda e: e.memset(scr32[:, 0:2048], 0.0),
                      waits=[(sACT, fv14)], inc=sDVE)
            d1 = p.op("vector", lambda e: e.tensor_copy(out=scr32[0:64, 0:1040], in_=kv_sb[:, :]),
                      waits=[(sDVE, d0)], inc=sDVE)
            d2 = p.op("vector", lambda e: e.tensor_copy(out=scr32[0:1, 1040:1056], in_=m_row[:, :]),
                      waits=[(sDVE, d1)], inc=sDVE)
            d3 = p.op("vector", lambda e: e.tensor_copy(out=scr32[:, 2048:3072], in_=ksumB[:, :]),
                      waits=[(sACT, fv14)], inc=sDVE)
            d4a = p.op("vector", lambda e: e.memset(scr32[:, 0:1024], 0.0),
                       waits=[(sDVE, d3), (sST, 32)], inc=sDVE)
            d4 = p.op("vector", lambda e: e.tensor_copy(
                out=scr32[:, 0:dbgsrc.shape[1]], in_=dbgsrc),
                waits=[(sDVE, d4a)], inc=sDVE)
            p.op("sync", lambda e: e.dma_start(out=dbg[:, 0:2048], in_=scr32[:, 0:2048]),
                 waits=[(sDVE, d2)], inc=sST, inck=16)
            p.op("sync", lambda e: e.dma_start(out=dbg[:, 2048:3072], in_=scr32[:, 2048:3072]),
                 waits=[(sDVE, d3)], inc=sST, inck=16)
            p.op("sync", lambda e: e.dma_start(out=dbg[:, 3072:4096], in_=scr32[:, 0:1024]),
                 waits=[(sDVE, d4)], inc=sST, inck=16)
            p.op("sync", lambda e: None, waits=[(sST, sST.n)])
            _emit_bodies(p, block)
            return nc

        p.barrier(bar, drains=[("sync", sMS), ("sync", sST), ("sync", sX[0]), ("sync", sX[1]), ("gpsimd", sWD), ("gpsimd", sWT), ("gpsimd", sW1), ("gpsimd", sW2), ("gpsimd", sSL[0]), ("gpsimd", sSL[1]), ("sync", sBB)])

        st_total = Sem(None)
        hid_last = {"pe": 0}
        out_last = {"pe": 0}
        w1_ld = {"v": 0}
        w2_ld = {"v": 0}
        b1_rdy = {0: 0, 1: 0}
        b2_rdy = {0: 0, 1: 0}
        gelu_last = {0: 0, 1: 0}
        fin_d = {}
        store_d = {}

        for q in range(4):
            pTf = bank(0, F32, 2)
            pR = bank(2)[:, 0:128]
            pWT = bank(4)[:, 0:128]
            pTW2 = bank(3, BF16)[:, 0:1024]
            c0_ln = {}; c0_t = {}; c0_xe = {}; c0_xr = {}; c0_r = {}; c0_m = {}; c0_sqa = {}
            c0_wme = {}
            if q == 0:
                wt_state["evac"] = 0

                def emit_wt2(dst, wsrc):
                    for r in range(8):
                        ld = p.op("gpsimd", lambda e, r=r, w=wsrc: e.dma_start(
                            out=wst128[:, :], in_=w[r * 128:(r + 1) * 128, :]),
                            waits=[(sACT, wt_state["evac"])], inc=sWT, inck=16)

                        def tfn(e, r=r):
                            for dc in range(8):
                                inst = e.matmul(pTW2[:, dc * 128:(dc + 1) * 128],
                                                wst128[:, dc * 128:(dc + 1) * 128], eyeb_s[:, :],
                                                is_transpose=True, start=(dc == 0), stop=(dc == 7),
                                                skip_group_check=True)
                            return inst
                        tv = p.op("tensor", tfn, waits=[(sWT, ld), (sACT, wt_state["evac"])],
                                  inc=sPE)
                        dv = dst[:, :].rearrange("p (dc n) -> p dc n", n=1024)[:, :, r * 128:(r + 1) * 128]
                        ev = p.op("scalar", lambda e, dv=dv: e.activation(
                            out=dv, in_=pTW2.rearrange("p (dc n) -> p dc n", n=128), func=AF.Copy),
                            waits=[(sPE, tv)], inc=sACT)
                        wt_state["evac"] = ev
                emit_wt2(wt_a, wq)
                for r in range(8):
                    ld = p.op("gpsimd", lambda e, r=r: e.dma_start(
                        out=wst128[:, :], in_=wo[r * 128:(r + 1) * 128, :]),
                        waits=[(sBB, wt_state.get("wost", 0)), (sACT, wt_state["evac"])],
                        inc=sWT, inck=16)

                    def tfn(e, r=r):
                        for dc in range(8):
                            inst = e.matmul(pTW2[:, dc * 128:(dc + 1) * 128],
                                            wst128[:, dc * 128:(dc + 1) * 128], eyeb_s[:, :],
                                            is_transpose=True, start=(dc == 0), stop=(dc == 7),
                                            skip_group_check=True)
                        return inst
                    tv = p.op("tensor", tfn, waits=[(sWT, ld), (sACT, wt_state["evac"])],
                              inc=sPE)
                    ev = p.op("scalar", lambda e: e.activation(
                        out=wst128[:, :], in_=pTW2[:, :], func=AF.Copy),
                        waits=[(sPE, tv)], inc=sACT)
                    wt_state["evac"] = ev
                    st = p.op("sync", lambda e, r=r: e.dma_start(
                        out=wodram.rearrange("(dc p) i -> p dc i", p=128)[:, :, r * 128:(r + 1) * 128],
                        in_=wst128[:, :].rearrange("p (dc i) -> p dc i", i=128)),
                        waits=[(sACT, ev)], inc=sBB, inck=16)
                    wt_state["wost"] = st
                for a in range(8):
                    p.op("sync", lambda e, a=a: e.dma_start(
                        out=wt_b[0:64, a * 1024:(a + 1) * 1024],
                        in_=wodram[a * 64:(a + 1) * 64, :]),
                        waits=[(sBB, wt_state["wost"])], inc=sMS, inck=16)
                    p.op("sync", lambda e, a=a: e.dma_start(
                        out=wt_b[64:128, a * 1024:(a + 1) * 1024],
                        in_=wodram[(a + 8) * 64:(a + 9) * 64, :]),
                        waits=[(sBB, wt_state["wost"])], inc=sMS, inck=16)

            for j in range(NSUB_Q):
                pp = j % 2
                row0 = q * NQ + j * 128
                dmj = p.op("sync", lambda e, row0=row0, pp=pp: e.dma_start(
                    out=xb[pp][:, :], in_=x_own[row0:row0 + 128, :]),
                    waits=[(sDVE, c0_ln.get(j - 2, 0)), (sPE, c0_t.get(j - 2, 0)),
                           (sACT, c0_sqa.get(j - 2, 0))],
                    inc=sX[pp], inck=16)

                sqa = p.op("scalar", lambda e, pp=pp: e.activation(
                    out=xnb[pp][:, :], in_=xb[pp][:, :], func=AF.Square,
                    accum_out=lnsc[pp][:, 1:2]),
                    waits=[(sX[pp], dmj), (sPE, c0_t.get(j - 2, 0))], inc=sACT)
                c0_sqa[j] = sqa
                sma = p.op("vector", lambda e, pp=pp: e.tensor_scalar(
                    out=junkb[:, :], in0=xb[pp][:, :], scalar1=1.0, scalar2=None, op0=ALU.mult,
                    op1=ALU.add, accum_out=lnsc[pp][:, 0:1]),
                    waits=[(sX[pp], dmj), (sPE, c0_t.get(j - 2, 0))], inc=sDVE)
                v1a = p.op("vector", lambda e, pp=pp: e.tensor_scalar(
                    out=lnsc[pp][:, 2:3], in0=lnsc[pp][:, 0:1], scalar1=1.0 / D, scalar2=None,
                    op0=ALU.mult), waits=[(sDVE, sma)], inc=sDVE)
                v1b = p.op("vector", lambda e, pp=pp: e.tensor_tensor(
                    out=lnsc[pp][:, 3:4], in0=lnsc[pp][:, 2:3], in1=lnsc[pp][:, 2:3],
                    op=ALU.mult), waits=[(sDVE, v1a)], inc=sDVE)
                v1 = p.op("vector", lambda e, pp=pp: e.tensor_scalar(
                    out=lnsc[pp][:, 4:5], in0=lnsc[pp][:, 1:2], scalar1=1.0 / D,
                    scalar2=lnsc[pp][:, 3:4], op0=ALU.mult, op1=ALU.subtract),
                    waits=[(sDVE, v1b), (sACT, sqa)], inc=sDVE)
                a1a = p.op("scalar", lambda e, pp=pp: e.activation(
                    out=lnsc[pp][:, 5:6], in_=lnsc[pp][:, 4:5], func=AF.Ln, bias=cfix[:, 0:1]),
                    waits=[(sDVE, v1)], inc=sACT)
                a1 = p.op("scalar", lambda e, pp=pp: e.activation(
                    out=lnsc[pp][:, 6:7], in_=lnsc[pp][:, 5:6], func=AF.Exp, scale=cfix[:, 1:2]),
                    waits=[(sACT, a1a)], inc=sACT)
                c0_ln[j] = p.op("vector", lambda e, pp=pp: e.tensor_scalar(
                    out=xb[pp][:, :], in0=xb[pp][:, :], scalar1=lnsc[pp][:, 2:3],
                    scalar2=lnsc[pp][:, 6:7], op0=ALU.subtract, op1=ALU.mult),
                    waits=[(sACT, a1)], inc=sDVE)

                def tx(e, pp=pp):
                    for dc in range(8):
                        inst = e.matmul(pTf[:, dc * 128:(dc + 1) * 128],
                                        xb[pp][:, dc * 128:(dc + 1) * 128], eyef_s[:, :],
                                        is_transpose=True, start=(dc % 4 == 0), stop=(dc % 4 == 3),
                                        skip_group_check=True)
                    return inst
                c0_t[j] = p.op("tensor", tx,
                               waits=[(sDVE, c0_ln[j]), (sACT, c0_xe.get(j - 1, 0)),
                                      (sACT, c0_xr.get(j - 1, 0))], inc=sPE)
                dvx = xnF[:, :].rearrange("p (dc n) -> p dc n", n=NQ)[:, :, j * 128:(j + 1) * 128]
                c0_xe[j] = p.op("scalar", lambda e, dvx=dvx: e.activation(
                    out=dvx, in_=pTf[:, :].rearrange("p (dc n) -> p dc n", n=128), func=AF.Copy),
                    waits=[(sPE, c0_t[j])], inc=sACT)
                c0_xr[j] = p.op("scalar", lambda e, pp=pp: e.activation(
                    out=xr32[pp], in_=pTf[:, :], func=AF.Copy),
                    waits=[(sPE, c0_t[j]), (sPE, c0_r.get(j - 1, 0))], inc=sACT)

                def rmm(e, pp=pp):
                    for dc in range(8):
                        inst = e.matmul(pR[:, 0:E], xr32[pp][:, dc * 128:(dc + 1) * 128],
                                        rw_sb[:, dc * E:(dc + 1) * E],
                                        start=(dc == 0), stop=(dc == 7), skip_group_check=True)
                    return inst
                c0_r[j] = p.op("tensor", rmm,
                               waits=[(sACT, c0_xr[j]), (sDVE, c0_m.get(j - 1, 0))], inc=sPE)

                rp0 = p.op("vector", lambda e: e.tensor_copy(out=lgt[:, :], in_=pR[:, 0:E]),
                           waits=[(sPE, c0_r[j])], inc=sDVE)
                rp1v = p.op("vector", lambda e: e.tensor_reduce(
                    out=nlmax[:, :], in_=lgt[:, :], axis=AX.X, op=ALU.max, negate=True),
                    waits=[(sDVE, rp0)], inc=sDVE)
                rp2v = p.op("scalar", lambda e: e.activation(
                    out=ejk[:, :], in_=lgt[:, :], func=AF.Exp, bias=nlmax[:, 0:1],
                    accum_out=ssum[:, :]), waits=[(sDVE, rp1v)], inc=sACT)
                rp3a = p.op("vector", lambda e: e.max(out=mx8[:, :], in_=lgt[:, :]),
                            waits=[(sACT, rp2v)], inc=sDVE)
                rp3b = p.op("vector", lambda e: e.max_index(
                    out=ix8[:, :], in_max=mx8[:, :], in_values=lgt[:, :]),
                    waits=[(sDVE, rp3a)], inc=sDVE)
                rp3c = p.op("vector", lambda e: (
                    e.tensor_copy(out=ixf[:, :], in_=ix8[:, 0:1]),
                    e.reciprocal(out=wrec[:, :], in_=ssum[:, :]))[-1],
                    waits=[(sDVE, rp3b)], inc=sDVE)
                c0_m[j] = p.op("vector", lambda e, j=j: e.tensor_scalar(
                    out=wmask[:, j * E:(j + 1) * E], in0=iotaB[:, :],
                    scalar1=ixf[:, 0:1], scalar2=wrec[:, 0:1],
                    op0=ALU.is_equal, op1=ALU.mult), waits=[(sDVE, rp3c)], inc=sDVE)
                wt_v = p.op("tensor", lambda e, j=j: e.matmul(
                    pWT[0:14, 0:128], wmask[:, j * E:(j + 1) * E],
                    eyef_s[:, :], is_transpose=True, start=True, stop=True,
                    skip_group_check=True),
                    waits=[(sDVE, c0_m[j]), (sACT, c0_wme.get(j - 1, 0))], inc=sPE)
                c0_wme[j] = p.op("scalar", lambda e, j=j: e.activation(
                    out=wmTq[:, j * 128:(j + 1) * 128], in_=pWT[0:14, 0:128], func=AF.Copy),
                    waits=[(sPE, wt_v)], inc=sACT)

            if debug == "WT" and q == 0:
                dW0 = p.op("vector", lambda e: e.tensor_copy(
                    out=scr32[:, 0:1024], in_=wt_b[:, 0:1024]),
                    waits=[(sMS, sMS.n), (sBB, sBB.n), (sWT, sWT.n),
                           (sDVE, c0_m[NSUB_Q - 1])], inc=sDVE)
                p.op("sync", lambda e: e.dma_start(out=dbg[:, 0:1024], in_=scr32[:, 0:1024]),
                     waits=[(sDVE, dW0)], inc=sST, inck=16)
                dW1 = p.op("vector", lambda e: e.tensor_copy(
                    out=scr32[:, 0:1024], in_=wt_a[:, 0:1024]),
                    waits=[(sST, sST.n)], inc=sDVE)
                p.op("sync", lambda e: e.dma_start(out=dbg[:, 1024:2048], in_=scr32[:, 0:1024]),
                     waits=[(sDVE, dW1)], inc=sST, inck=16)
                dW2 = p.op("vector", lambda e: e.tensor_copy(
                    out=scr32[:, 0:1024], in_=projT[:, 0:512].broadcast_to([128, 512]) if False else wt_b[:, 1024:2048]),
                    waits=[(sST, sST.n)], inc=sDVE)
                p.op("sync", lambda e: e.dma_start(out=dbg[:, 2048:3072], in_=scr32[:, 0:1024]),
                     waits=[(sDVE, dW2)], inc=sST, inck=16)
                p.op("sync", lambda e: None, waits=[(sST, sST.n)])
                _emit_bodies(p, block)
                return nc

            if debug == "C0" and q == 0:
                d1 = p.op("vector", lambda e: e.tensor_copy(
                    out=scr32[:, 0:NSUB_Q * E], in_=wmask[:, :]),
                    waits=[(sDVE, c0_m[NSUB_Q - 1])], inc=sDVE)
                p.op("sync", lambda e: e.dma_start(out=dbg[:, 0:NSUB_Q * E],
                                                   in_=scr32[:, 0:NSUB_Q * E]),
                     waits=[(sDVE, d1)], inc=sST, inck=16)
                p.op("sync", lambda e: None, waits=[(sST, sST.n)])
                _emit_bodies(p, block)
                return nc

            p.barrier(bar, drains=[("sync", sMS), ("sync", sST), ("sync", sX[0]), ("sync", sX[1]), ("gpsimd", sWD), ("gpsimd", sWT), ("gpsimd", sW1), ("gpsimd", sW2), ("gpsimd", sSL[0]), ("gpsimd", sSL[1]), ("sync", sBB)])

            pH = (bank(0), bank(1))
            pO = (bank(2, F32, 2), bank(4, F32, 2))
            pB1 = bank(6, BF16)[:, 0:32]
            gelu_d = {}
            outmm_d = {}
            stt_d = {}
            b1tv_prev = {"v": 0}
            out_em = {0: 0, 1: 0}
            for s in range(NSUB_Q):
                def b2mm(e, s=s):
                    for hf in range(2):
                        inst = e.matmul(pO[s % 2][:, hf * 512:(hf + 1) * 512],
                                        wmTq[:, s * 128:(s + 1) * 128],
                                        b2all[:, hf * 512:(hf + 1) * 512],
                                        start=True, stop=True, skip_group_check=True)
                    return inst
                bmv = p.op("tensor", b2mm,
                           waits=[(sACT, c0_wme[NSUB_Q - 1]),
                                  (sDVE, stt_d.get(("init", s - 2), 0))], inc=sPE)
                stt_d[("init", s)] = p.op("vector", lambda e, s=s: e.tensor_copy(
                    out=accq[:, s * 1024:(s + 1) * 1024], in_=pO[s % 2][:, :]),
                    waits=[(sPE, bmv), (sDVE, fin_d.get((q - 1, 7), 0) if s == 0 else 0)],
                    inc=sDVE)
            for eix in range(E):
                em = eix % 2
                b1_ld = p.op("gpsimd", lambda e, eix=eix: e.dma_start(
                    out=b1row[0:1, :], in_=b1[eix:eix + 1, :]),
                    waits=[(sPE, b1tv_prev["v"]), (sACT, b1_rdy[em])], inc=sSL[em], inck=16)
                w1_ld["v"] = p.op("gpsimd", lambda e, eix=eix: e.dma_start(
                    out=w1buf[:, :].rearrange("p (dc c) -> p dc c", c=FF),
                    in_=w1[eix].rearrange("(dc p) c -> p dc c", p=128)),
                    waits=[(sPE, hid_last["pe"])], inc=sW1, inck=16)
                w2_ld["v"] = p.op("gpsimd", lambda e, eix=eix: e.dma_start(
                    out=w2buf[:, :].rearrange("p (fc c) -> p fc c", c=D),
                    in_=w2[eix].rearrange("(fc p) c -> p fc c", p=128)),
                    waits=[(sPE, out_last["pe"])], inc=sW2, inck=16)

                def b1t(e):
                    for k in range(16):
                        inst = e.matmul(pB1[:, 2 * k:2 * k + 1],
                                        b1row[0:1, k * 128:(k + 1) * 128],
                                        eyeb_s[0:1, 0:1],
                                        is_transpose=True, start=(k == 0), stop=(k == 15),
                                        skip_group_check=True)
                    return inst
                b1tv = p.op("tensor", b1t, waits=[(sSL[em], b1_ld), (sACT, b1_rdy[em])], inc=sPE)
                b1tv_prev["v"] = b1tv
                b1_rdy[em] = p.op("scalar", lambda e, em=em: e.activation(
                    out=b1c[em][:, :],
                    in_=pB1[:, 0:32].rearrange("p (k two) -> p k two", two=2)[:, :, 0:1],
                    func=AF.Copy),
                    waits=[(sPE, b1tv), (sACT, gelu_last[em])], inc=sACT)
                b1_now = b1_rdy[em]

                for g in range(2):
                    for fo in range(16):
                        def hmm(e, g=g, fo=fo):
                            for dc in range(8):
                                inst = e.matmul(
                                    pH[fo % 2][:, :],
                                    w1buf[:, dc * FF + fo * 128:dc * FF + (fo + 1) * 128],
                                    xnF[:, dc * NQ + g * 512:dc * NQ + (g + 1) * 512],
                                    start=(dc == 0), stop=(dc == 7), skip_group_check=True)
                            return inst
                        pw = gelu_d.get((g, fo - 2), 0) if fo >= 2 else \
                            gelu_d.get((1 if g == 0 else g - 1, fo + 14), 0)
                        hv = p.op("tensor", hmm,
                                  waits=[(sW1, w1_ld["v"] if (g == 0 and fo == 0) else 0),
                                         (sACT, pw)],
                                  inc=sPE)
                        if g == 1 and fo == 15:
                            hid_last["pe"] = hv
                        gwl = [(sPE, hv), (sACT, b1_now if fo == 0 and g == 0 else 0),
                               (sPE, outmm_d.get((g, 3), 0) if fo <= 1 else 0)]
                        gelu_d[(g, fo)] = p.op("scalar", lambda e, g=g, fo=fo, em=em: e.activation(
                            out=htile(g, fo), in_=pH[fo % 2][:, :],
                            func=AF.Gelu, bias=b1c[em][:, fo:fo + 1]),
                            waits=gwl, inc=sACT)
                gelu_last[em] = gelu_d[(1, 15)]
                for g in range(2):
                    for ts_ in range(4):
                        s = g * 4 + ts_

                        def omm(e, g=g, ts_=ts_, em=em):
                            po = pO[ts_ % 2]
                            for fc in range(16):
                                for hf in range(2):
                                    inst = e.matmul(po[:, hf * 512:(hf + 1) * 512],
                                             htile(g, fc)[:, ts_ * 128:(ts_ + 1) * 128],
                                             w2buf[:, fc * D + hf * 512:fc * D + (hf + 1) * 512],
                                             start=(fc == 0), stop=(fc == 15),
                                             skip_group_check=True)
                            return inst
                        prev_stt = stt_d.get((g, ts_ - 2), 0) if ts_ >= 2 else \
                            stt_d.get((0 if g == 1 else 1, ts_ + 2), 0)
                        if eix == 0 and prev_stt == 0:
                            prev_stt = stt_d.get(("init", 6 + ts_ % 2), 0)
                        owl = [(sACT, gelu_d[(g, 15)]),
                               (sW2, w2_ld["v"] if (g == 0 and ts_ == 0) else 0),
                               (sDVE, prev_stt)]
                        outmm_d[(g, ts_)] = p.op("tensor", omm, waits=owl, inc=sPE)
                        if g == 1 and ts_ == 3:
                            out_last["pe"] = outmm_d[(g, ts_)]
                            out_em[em] = outmm_d[(g, ts_)]

                        def sttf(e, s=s, ts_=ts_, eix=eix):
                            po = pO[ts_ % 2]
                            acc_sl = accq[:, s * 1024:(s + 1) * 1024]
                            msk = wmask[:, s * E + eix:s * E + eix + 1]
                            return e.scalar_tensor_tensor(out=acc_sl, in0=po[:, :], scalar=msk,
                                                          in1=acc_sl, op0=ALU.mult, op1=ALU.add)
                        stt_d[(g, ts_)] = p.op("vector", sttf,
                                               waits=[(sPE, outmm_d[(g, ts_)]),
                                                      (sDVE, stt_d.get(("init", s), 0) if eix == 0 else 0)],
                                               inc=sDVE)

            if debug == "C1" and q == 0:
                last_store = 0
                for half in range(2):
                    d1 = p.op("vector", lambda e, half=half: e.tensor_copy(
                        out=scr32[:, 0:2048], in_=accq[:, half * 2048:(half + 1) * 2048]),
                        waits=[(sDVE, stt_d[(1, 3)]), (sST, last_store)], inc=sDVE)
                    last_store = p.op("sync", lambda e, half=half: e.dma_start(
                        out=dbg[:, half * 2048:(half + 1) * 2048], in_=scr32[:, 0:2048]),
                        waits=[(sDVE, d1)], inc=sST, inck=16)
                p.op("sync", lambda e: None, waits=[(sST, sST.n)])
                _emit_bodies(p, block)
                return nc

            p.barrier(bar, drains=[("sync", sMS), ("sync", sST), ("sync", sX[0]), ("sync", sX[1]), ("gpsimd", sWD), ("gpsimd", sWT), ("gpsimd", sW1), ("gpsimd", sW2), ("gpsimd", sSL[0]), ("gpsimd", sSL[1]), ("sync", sBB)])

            pQ = bank(0, F32, 2)
            pQT = bank(6, BF16, 2)
            pDKdp = (bank(2), bank(3))
            pAO = bank(4, F32, 2)
            pWo = bank(6, F32, 2)
            d_dma = {}; d_q = {}; d_qe = {}; d_sq = {}; d_dk = {}; d_qp = {}; d_ek = {}
            d_pr = {}; d_qt = {}; d_qte = {}; d_ao = {}; d_aoe = {}; d_wo = {}

            def emit_qmm(j):
                def qmm(e, j=j):
                    for ic in range(8):
                        for dc in range(8):
                            e.matmul(pQ[:, ic * 128:(ic + 1) * 128],
                                     wt_a[:, dc * 1024 + ic * 128:dc * 1024 + (ic + 1) * 128],
                                     xnF[:, dc * NQ + j * 128:dc * NQ + (j + 1) * 128],
                                     start=(ic % 4 == 0 and dc == 0), stop=False,
                                     skip_group_check=True)
                    for ic in range(8):
                        inst = e.matmul(pQ[:, ic * 128:(ic + 1) * 128],
                                        bqr[0:1, ic * 128:(ic + 1) * 128], onesrow_b[0:1, :],
                                        start=False, stop=(ic % 4 == 3), skip_group_check=True)
                    return inst
                d_q[j] = p.op("tensor", qmm,
                              waits=[(sACT, d_qe.get(j - 1, 0))], inc=sPE)
            emit_qmm(0)
            for j in range(NSUB_Q):
                pp = j % 2
                row0 = q * NQ + j * 128
                d_dma[j] = p.op("sync", lambda e, row0=row0, pp=pp: e.dma_start(
                    out=xb[pp][:, :], in_=x_own[row0:row0 + 128, :]),
                    waits=[(sDVE, fin_d.get((q, j - 2), 0))], inc=sX[pp], inck=16)
                d_qe[j] = p.op("scalar", lambda e: e.activation(
                    out=qfb[:, :], in_=pQ[:, :], func=AF.Copy),
                    waits=[(sPE, d_q[j]), (sPE, d_dk.get(j - 1, 0))], inc=sACT)
                d_sq[j] = p.op("vector", lambda e: e.tensor_tensor(
                    out=sqb[:, :], in0=qfb[:, :], in1=qfb[:, :], op=ALU.mult),
                    waits=[(sACT, d_qe[j]), (sPE, d_dk.get(j - 1, 0))], inc=sDVE)

                ekw = {}; qpw = {}; dkw = {}
                for w in range(2):
                    def dq_wave(e, w=w):
                        for k in range(8):
                            h = 8 * w + k
                            rb = (h % 2) * 64
                            sl = k // 2
                            pb = pDKdp[h % 2]
                            cb = (h // 2) * 128
                            qf_h = qfb[rb:rb + 64, cb:cb + 128]
                            sq_h = sqb[rb:rb + 64, cb:cb + 128]
                            e.matmul(pb[:, sl * 64:(sl + 1) * 64], qf_h,
                                     projT[rb:rb + 64, (h // 2) * 64:(h // 2) * 64 + 64],
                                     start=(k < 2), stop=False, skip_group_check=True,
                                     tile_position=(rb, 0))
                            e.matmul(pb[:, sl * 64:(sl + 1) * 64], sq_h, negq[rb:rb + 64, :],
                                     start=False, stop=False, skip_group_check=True,
                                     tile_position=(rb, 0))
                            inst = e.matmul(pb[:, 256 + sl:257 + sl], sq_h,
                                            posq[rb:rb + 64, :],
                                            start=False, stop=(k >= 6), skip_group_check=True,
                                            tile_position=(rb, 0))
                        return inst
                    if w == 0:
                        wlw = [(sDVE, d_sq[j]), (sACT, d_ek.get(j - 1, 0)),
                               (sDVE, d_qp.get(j - 1, 0))]
                    else:
                        wlw = [(sACT, ekw[0]), (sDVE, qpw[0])]
                    dkw[w] = p.op("tensor", dq_wave, waits=wlw, inc=sPE)

                    def qpf1(e, w=w):
                        for par in range(2):
                            c0 = 8 * w + 4 * par
                            inst = e.tensor_reduce(
                                out=stsc[:, c0:c0 + 4],
                                in_=pDKdp[par][:, 0:256].rearrange("p (t n) -> p t n", n=64),
                                axis=AX.X, op=ALU.max, negate=True)
                        return inst
                    qv1 = p.op("vector", qpf1, waits=[(sPE, dkw[w])], inc=sDVE)

                    def qpf2(e, w=w):
                        for par in range(2):
                            c0 = 8 * w + 4 * par
                            inst = e.tensor_tensor(out=nstab[:, c0:c0 + 4],
                                                   in0=stsc[:, c0:c0 + 4],
                                                   in1=pDKdp[par][:, 256:260], op=ALU.subtract)
                        return inst
                    qpw[w] = p.op("vector", qpf2, waits=[(sDVE, qv1)], inc=sDVE)

                    def ekf(e, w=w):
                        for k in range(8):
                            h = 8 * w + k
                            mc = mcol(h)
                            inst = e.activation(out=qeb[:, h * 64:(h + 1) * 64],
                                                in_=pDKdp[h % 2][:, (k // 2) * 64:(k // 2 + 1) * 64],
                                                func=AF.Exp, bias=nstab[:, mc:mc + 1])
                        return inst
                    ekw[w] = p.op("scalar", ekf,
                                  waits=[(sDVE, qpw[w]),
                                         (sDVE, d_pr.get(j - 1, 0) if w == 0 else 0)], inc=sACT)
                d_dk[j] = dkw[1]; d_qp[j] = qpw[1]; d_ek[j] = ekw[1]

                def prf1(e):
                    e.tensor_scalar(out=qpb[:, :], in0=qeb[:, :], scalar1=RATIO,
                                    scalar2=EPS_FEAT, op0=ALU.mult, op1=ALU.add)
                    e.tensor_tensor(out=qprod[:, :], in0=qpb[:, :], in1=ksumB[:, :], op=ALU.mult)
                    return e.tensor_reduce(out=den16[:, :],
                                           in_=qprod[:, :].rearrange("p (h n) -> p h n", n=64),
                                           axis=AX.X, op=ALU.add)
                pr1 = p.op("vector", prf1,
                           waits=[(sACT, d_ek[j]), (sPE, d_qt.get(j - 1, 0))], inc=sDVE)
                pr2 = p.op("vector", lambda e: e.tensor_scalar(
                    out=den16[:, :], in0=den16[:, :], scalar1=EPS_DEN, scalar2=None,
                    op0=ALU.max), waits=[(sDVE, pr1)], inc=sDVE)
                pr3 = p.op("vector", lambda e: e.reciprocal(out=rden16[:, :], in_=den16[:, :]),
                           waits=[(sDVE, pr2)], inc=sDVE)
                d_pr[j] = p.op("vector", lambda e: e.tensor_tensor(
                    out=qnb[:, :].rearrange("p (h n) -> p h n", n=64),
                    in0=qpb[:, :].rearrange("p (h n) -> p h n", n=64),
                    in1=rden16[:, :].rearrange("p (h one) -> p h one", one=1).broadcast_to([128, 16, 64]),
                    op=ALU.mult), waits=[(sDVE, pr3)], inc=sDVE)

                if j + 1 < NSUB_Q:
                    emit_qmm(j + 1)

                def qtf(e):
                    for h in range(H):
                        inst = e.matmul(pQT[0:64, h * 128:(h + 1) * 128],
                                        qnb[:, h * 64:(h + 1) * 64], eyeb_s[:, :],
                                        is_transpose=True, start=(h % 8 == 0), stop=(h % 8 == 7),
                                        skip_group_check=True)
                    return inst
                d_qt[j] = p.op("tensor", qtf,
                               waits=[(sDVE, d_pr[j]), (sDVE, fin_d.get((q, j - 1), 0))],
                               inc=sPE)
                d_qte[j] = p.op("vector", lambda e: e.tensor_copy(
                    out=qtb[:, :], in_=pQT[0:64, 0:2048]),
                    waits=[(sPE, d_qt[j])], inc=sDVE)

                def aof(e):
                    for h in range(H):
                        a = h % 8
                        rb = (h // 8) * 64
                        inst = e.matmul(pAO[rb:rb + 64, a * 128:(a + 1) * 128],
                                        kv_sb[:, h * 65:h * 65 + 64],
                                        qtb[:, h * 128:(h + 1) * 128],
                                        start=(h % 4 == 0), stop=(h == 11 or h == 15),
                                        skip_group_check=True, tile_position=(0, rb))
                    return inst
                d_ao[j] = p.op("tensor", aof,
                               waits=[(sDVE, d_qte[j]), (sACT, d_aoe.get(j - 1, 0))], inc=sPE)
                d_aoe[j] = p.op("scalar", lambda e: e.activation(
                    out=apre[:, :], in_=pAO[:, :], func=AF.Copy),
                    waits=[(sPE, d_ao[j]), (sPE, d_wo.get(j - 1, 0))], inc=sACT)

                def wof(e):
                    for nb in range(2):
                        for a in range(8):
                            e.matmul(pWo[:, nb * 512:(nb + 1) * 512],
                                     apre[:, a * 128:(a + 1) * 128],
                                     wt_b[:, a * 1024 + nb * 512:a * 1024 + (nb + 1) * 512],
                                     start=(a == 0), stop=False, skip_group_check=True)
                    for nb in range(2):
                        inst = e.matmul(pWo[:, nb * 512:(nb + 1) * 512], onesrow_b[0:1, :],
                                        bor[0:1, nb * 512:(nb + 1) * 512],
                                        start=False, stop=True, skip_group_check=True)
                    return inst
                if debug == "D0" and q == 0 and j == 0:
                    dD0 = p.op("vector", lambda e: e.tensor_copy(
                        out=scr32[:, 0:1024], in_=apre[:, 0:1024]),
                        waits=[(sACT, d_aoe[0])], inc=sDVE)
                    p.op("sync", lambda e: e.dma_start(out=dbg[:, 0:1024], in_=scr32[:, 0:1024]),
                         waits=[(sDVE, dD0)], inc=sST, inck=16)
                    dD1 = p.op("vector", lambda e: e.tensor_copy(
                        out=scr32[0:64, 0:1024], in_=qtb[0:64, 0:2048].bitcast(F32)[0:64, 0:1024]),
                        waits=[(sST, sST.n)], inc=sDVE)
                    p.op("sync", lambda e: e.dma_start(out=dbg[:, 1024:2048], in_=scr32[:, 0:1024]),
                         waits=[(sDVE, dD1)], inc=sST, inck=16)
                    dD2 = p.op("vector", lambda e: e.tensor_copy(
                        out=scr32[:, 0:16], in_=den16[:, :]),
                        waits=[(sST, sST.n)], inc=sDVE)
                    p.op("sync", lambda e: e.dma_start(out=dbg[:, 2048:2064], in_=scr32[:, 0:16]),
                         waits=[(sDVE, dD2)], inc=sST, inck=16)
                    p.op("sync", lambda e: None, waits=[(sST, sST.n)])
                    _emit_bodies(p, block)
                    return nc

                d_wo[j] = p.op("tensor", wof,
                               waits=[(sACT, d_aoe[j])], inc=sPE)

                def finf(e, j=j, pp=pp):
                    s = j
                    e.scalar_tensor_tensor(out=outb[:, :], in0=pWo[:, :], scalar=1.0,
                                           in1=accq[:, s * 1024:(s + 1) * 1024],
                                           op0=ALU.mult, op1=ALU.add)
                    return e.tensor_tensor(out=outb[:, :], in0=outb[:, :], in1=xb[pp][:, :],
                                           op=ALU.add)
                fin_d[(q, j)] = p.op("vector", finf,
                                     waits=[(sPE, d_wo[j]), (sX[pp], d_dma[j]),
                                            (sST, store_d.get((q, j - 1), 0))], inc=sDVE)
                store_d[(q, j)] = p.op("sync", lambda e, row0=row0: e.dma_start(
                    out=out[row0:row0 + 128, :], in_=outb[:, :]),
                    waits=[(sDVE, fin_d[(q, j)])], inc=sST, inck=16)

            p.barrier(bar, drains=[("sync", sMS), ("sync", sST), ("sync", sX[0]), ("sync", sX[1]), ("gpsimd", sWD), ("gpsimd", sWT), ("gpsimd", sW1), ("gpsimd", sW2), ("gpsimd", sSL[0]), ("gpsimd", sSL[1]), ("sync", sBB)])

        p.op("sync", lambda e: None, waits=[(sST, sST.n)])

        _emit_bodies(p, block)
    return nc


def _emit_bodies(p, block):
    def body(eng):
        def run(e):
            for thunk in p.q[eng]:
                thunk(e)
        return run
    block.sync(body("sync"))
    block.scalar(body("scalar"))
    block.vector(body("vector"))
    block.tensor(body("tensor"))
    block.gpsimd(body("gpsimd"))


_NC_CACHE = {}


def _get_nc(debug=None):
    if debug not in _NC_CACHE:
        _NC_CACHE[debug] = _build(debug)
    return _NC_CACHE[debug]


def make_in_maps(inputs):
    x = np.ascontiguousarray(inputs["L0_attn_query"], dtype=np.float32)
    eyeb = np.eye(128, dtype=ml_dtypes.bfloat16)
    eyef = np.eye(128, dtype=np.float32)
    shared = {
        "wq": np.ascontiguousarray(inputs["L0_attn_Wq"]),
        "wk": np.ascontiguousarray(inputs["L0_attn_Wk"]),
        "wv": np.ascontiguousarray(inputs["L0_attn_Wv"]),
        "wo": np.ascontiguousarray(inputs["L0_attn_Wo"]),
        "bq": np.ascontiguousarray(inputs["L0_attn_bq"]).reshape(1, D),
        "bk": np.ascontiguousarray(inputs["L0_attn_bk"]).reshape(1, D),
        "bv": np.ascontiguousarray(inputs["L0_attn_bv"]).reshape(1, D),
        "bo": np.ascontiguousarray(inputs["L0_attn_bo"]).reshape(1, D),
        "proj": np.ascontiguousarray(inputs["L0_attn_proj"]),
        "rw": np.ascontiguousarray(inputs["L0_ffn_router_w"]),
        "w1": np.ascontiguousarray(inputs["L0_ffn_w1"]),
        "b1": np.ascontiguousarray(inputs["L0_ffn_b1"]),
        "w2": np.ascontiguousarray(inputs["L0_ffn_w2"]),
        "b2": np.ascontiguousarray(inputs["L0_ffn_b2"]),
        "eyeb": eyeb,
        "eyef": eyef,
    }
    in_maps = []
    for c in range(8):
        b, h = c // 2, c % 2
        im = dict(shared)
        im["x_full"] = np.ascontiguousarray(x[b])
        im["x_own"] = np.ascontiguousarray(x[b, h * NOWN:(h + 1) * NOWN])
        in_maps.append(im)
    return in_maps


def kernel(**inputs):
    nc = _get_nc(None)
    in_maps = make_in_maps(inputs)
    res = run_bass_kernel_spmd(nc, in_maps, core_ids=list(range(8)))
    outa = np.empty((B, N, D), np.float32)
    for c in range(8):
        b, h = c // 2, c % 2
        outa[b, h * NOWN:(h + 1) * NOWN] = res.results[c]["out"]
    return outa


# version 13
# speedup vs baseline: 1.2844x; 1.0084x over previous
import math
import numpy as np
import ml_dtypes
from contextlib import ExitStack

import concourse.bass as bass
import concourse.mybir as mybir
from concourse.bass_utils import run_bass_kernel_spmd

F32 = mybir.dt.float32
BF16 = mybir.dt.bfloat16
U32 = mybir.dt.uint32
AF = mybir.ActivationFunctionType
ALU = mybir.AluOpType
AX = mybir.AxisListType

B, N, D, H, DH, J, E, FF = 4, 8192, 1024, 16, 64, 64, 14, 2048
NOWN = 4096
NQ = 1024
NSUB_A = N // 128
NSUB_Q = NQ // 128
DNORM = float(DH) ** -0.25
RATIO = float(J) ** -0.5
EPS_FEAT = 1e-4
EPS_DEN = 0.01
LN_EPS = 1e-5
LN8EM4 = math.log(EPS_FEAT / RATIO)
LNRATIO = math.log(RATIO)

ENGS = ("sync", "scalar", "vector", "tensor", "gpsimd")


class Sem:
    __slots__ = ("h", "n")

    def __init__(self, h):
        self.h = h
        self.n = 0


class Prog:
    def __init__(self):
        self.q = {e: [] for e in ENGS}

    def op(self, eng, fn, waits=(), inc=None, inck=1):
        wl = [(s.h, v) for (s, v) in waits if v and v > 0]
        it = None
        tot = None
        if inc is not None:
            inc.n += inck
            tot = inc.n
            it = (inc.h, inck)

        def thunk(e, fn=fn, wl=wl, it=it):
            for h, v in wl:
                e.wait_ge(h, v)
            inst = fn(e)
            if it is not None:
                inst.then_inc(it[0], it[1])

        self.q[eng].append(thunk)
        return tot

    def barrier(self, bar, drains=()):
        target = bar.n + len(ENGS)
        dr = {}
        for eng, s in drains:
            dr.setdefault(eng, []).append((s.h, s.n))
        for eng in ENGS:
            def thunk(e, bar=bar, target=target, w=dr.get(eng, ())):
                for h, v in w:
                    if v > 0:
                        e.wait_ge(h, v)
                e.sem_inc(bar.h, 1)
                e.wait_ge(bar.h, target)
            self.q[eng].append(thunk)
        bar.n = target


def _build(debug=None, nsub_a=NSUB_A, a_stages=99, vev=4, dkv=5, dbgname='kfb'):
    nc = bass.Bass("TRN2", target_bir_lowering=False, debug=False)

    def din(name, shape, dtype=F32):
        return nc.dram_tensor(name, shape, dtype, kind="ExternalInput").ap()

    x_full = din("x_full", [N, D])
    x_own = din("x_own", [NOWN, D])
    wq = din("wq", [D, D]); wk = din("wk", [D, D]); wv = din("wv", [D, D]); wo = din("wo", [D, D])
    bq = din("bq", [1, D]); bk = din("bk", [1, D]); bv = din("bv", [1, D]); bo = din("bo", [1, D])
    proj = din("proj", [H, J, DH])
    rw = din("rw", [D, E])
    w1 = din("w1", [E, D, FF]); b1 = din("b1", [E, FF])
    w2 = din("w2", [E, FF, D]); b2 = din("b2", [E, D])
    eyeb = din("eyeb", [128, 128], BF16)
    pscr = nc.dram_tensor("pscr", [64, 1024], BF16, kind="Internal").ap()
    wodram = nc.dram_tensor("wodram", [D, D], BF16, kind="Internal").ap()
    eyef = din("eyef", [128, 128], F32)
    out = nc.dram_tensor("out", [NOWN, D], F32, kind="ExternalOutput").ap()
    dbg = nc.dram_tensor("dbg", [128, 4096], F32, kind="ExternalOutput").ap() if debug else None

    p = Prog()

    with ExitStack() as ctx:
        def sbt(name, sh, dt=F32):
            return ctx.enter_context(nc.sbuf_tensor(name, sh, dt))

        wt_a = sbt("wt_a", [128, 8 * 1024], BF16)
        wt_b = sbt("wt_b", [128, 8 * 1024], BF16)
        projT = sbt("projT", [128, 8 * 64], BF16)
        kv_sb = sbt("kv_sb", [64, H * 65], BF16)
        ksumB = sbt("ksumB", [128, H * 64], BF16)
        rw_sb = sbt("rw_sb", [128, 8 * E], F32)
        bqr = sbt("bqr", [1, D], BF16)
        bkr = sbt("bkr", [1, D], BF16)
        bvr = sbt("bvr", [1, D], BF16)
        bor = sbt("bor", [1, D], BF16)
        b2all = sbt("b2all", [14, D], BF16)
        wmTq = sbt("wmTq", [14, NSUB_Q * 128], BF16)
        b1row = sbt("b1row", [1, FF], BF16)
        b1c = [sbt(f"b1c{i}", [128, 16]) for i in range(2)]
        eyeb_s = sbt("eyeb_s", [128, 128], BF16)
        eyef_s = sbt("eyef_s", [128, 128], F32)
        onesrow = sbt("onesrow", [1, 128])
        onesrow_b = sbt("onesrow_b", [1, 128], BF16)
        iotaB = sbt("iotaB", [128, E])
        negq = sbt("negq", [128, 64], BF16)
        posq = sbt("posq", [128, 1], BF16)
        onescol_b = sbt("onescol_b", [128, 1], BF16)
        m_row = sbt("m_row", [1, 16]); a1_row = sbt("a1_row", [1, 16]); c1_row = sbt("c1_row", [1, 16])
        c1b = sbt("c1b", [64, 16])
        trows = sbt("trows", [1, H * 65], BF16)
        mrun = sbt("mrun", [128, 16])
        stsc = sbt("stsc", [128, 16]); nstab = sbt("nstab", [128, 16])
        lgt = sbt("lgt", [128, E]); ejk = sbt("ejk", [128, E], BF16)
        nlmax = sbt("nlmax", [128, 1]); ssum = sbt("ssum", [128, 1])
        wrec = sbt("wrec", [128, 1]); ixf = sbt("ixf", [128, 1])
        mx8 = sbt("mx8", [128, 8]); ix8 = sbt("ix8", [128, 8], U32)
        den16 = sbt("den16", [128, 16]); rden16 = sbt("rden16", [128, 16])
        lnsc = [sbt(f"lnsc{i}", [128, 8]) for i in range(4)]
        cfix = sbt("cfix", [128, 8])
        xnF = sbt("xnF", [128, 8 * NQ], BF16)
        accq = sbt("accq", [128, NQ * 8], BF16)
        w1buf = sbt("w1buf", [128, 8 * FF], BF16)
        w2buf = sbt("w2buf", [128, 16 * D], BF16)
        hidb = sbt("hidb", [128, 16 * 512], BF16)
        wmask = sbt("wmask", [128, NSUB_Q * E])
        xb = [sbt(f"xb{i}", [128, D]) for i in range(2)]
        xnb = [sbt(f"xnb{i}", [128, D], BF16) for i in range(2)]
        scr32 = sbt("scr32", [128, 3 * D])
        xr32 = [scr32[:, 0:D], scr32[:, D:2 * D]]
        qeb = scr32[:, 0:D]; qpb = scr32[:, D:2 * D]; qprod = scr32[:, 2 * D:3 * D]
        xfb = sbt("xfb", [128, D], BF16)
        kfb = sbt("kfb", [128, D], BF16)
        sqb = sbt("sqb", [128, D], BF16)
        vtb = sbt("vtb", [128, H * 65], BF16)
        ekb = sbt("ekb", [128, H * 65 + 66], BF16)
        wst128 = sbt("wst128", [128, 1024], BF16)
        qfb = xnb[1]
        qnb = xnb[0]
        hidbF = hidb[:, :].bitcast(F32)
        xb4 = [xb[0], xb[1], hidbF[:, 1024:2048], hidbF[:, 2048:3072]]
        qtb = scr32[0:64, 2 * D:3 * D].bitcast(BF16)
        apre = kfb
        outb = hidb[:, 0:4096].bitcast(F32)[:, 0:1024]
        _hpool = [kfb, sqb, vtb, ekb, xfb, xnb[0], xnb[1], wst128]

        def htile(g, fc):
            if g == 0:
                return hidb[:, fc * 512:(fc + 1) * 512]
            t = _hpool[fc // 2]
            off = (fc % 2) * 512
            return t[:, off:off + 512]
        pstage = scr32[0:64, 0:H * 64]

        pAll = ctx.enter_context(nc.psum_tensor("pAll", [128, 4096], F32))

        def bank(k, dt=F32, nb=1):
            v = pAll[:, k * 512:(k + nb) * 512]
            if dt is BF16:
                v = v.bitcast(BF16)
            return v

        sems = {nm: Sem(ctx.enter_context(nc.semaphore(nm)))
                for nm in ("sIN", "sWD", "sPE", "sACT", "sDVE", "sGP", "sST", "bar",
                           "sMS", "sWT", "sW1", "sW2", "sSL0", "sSL1", "sXA", "sXB", "sXC", "sXD", "sBB")}
        sIN, sWD, sPE, sACT, sDVE, sGP, sST = (sems[k] for k in
            ("sIN", "sWD", "sPE", "sACT", "sDVE", "sGP", "sST"))
        sMS, sWT, sW1, sW2 = (sems[k] for k in ("sMS", "sWT", "sW1", "sW2"))
        sBB = sems["sBB"]
        sSL = [sems["sSL0"], sems["sSL1"]]
        sX = [sems["sXA"], sems["sXB"]]
        sXq = [sems["sXA"], sems["sXB"], sems["sXC"], sems["sXD"]]
        bar = sems["bar"]

        block = ctx.enter_context(nc.Block())

        p.op("gpsimd", lambda e: e.memset(onesrow[:, :], 1.0), inc=sGP)
        p.op("gpsimd", lambda e: e.memset(onesrow_b[:, :], 1.0), inc=sGP)
        p.op("gpsimd", lambda e: e.memset(negq[:, :], -0.0625), inc=sGP)
        p.op("gpsimd", lambda e: e.memset(posq[:, :], 0.0625), inc=sGP)
        p.op("gpsimd", lambda e: e.memset(onescol_b[:, :], 1.0), inc=sGP)
        p.op("gpsimd", lambda e: e.memset(mrun[:, :], -1e30), inc=sGP)
        for k, val in ((0, LN_EPS), (1, -0.5), (2, LN8EM4), (3, LNRATIO), (4, -1.0), (5, DNORM)):
            p.op("gpsimd", lambda e, k=k, val=val: e.memset(cfix[:, k:k + 1], val), inc=sGP)
        p.op("gpsimd", lambda e: e.iota(iotaB[:, :], pattern=[[1, E]], base=0,
                                        channel_multiplier=0,
                                        allow_small_or_imprecise_dtypes=True), inc=sGP)
        p.op("gpsimd", lambda e: e.memset(
            vtb[:, :].rearrange("p (h n) -> p h n", n=65)[:, :, 64:65], 1.0), inc=sGP)
        p.op("gpsimd", lambda e: e.memset(
            ekb[:, 0:1040].rearrange("p (h n) -> p h n", n=65)[:, :, 64:65], 1.0), inc=sGP)
        for dst, src in ((bqr[0:1, :], bq), (bkr[0:1, :], bk), (bvr[0:1, :], bv),
                         (bor[0:1, :], bo)):
            p.op("gpsimd", lambda e, d=dst, s=src: e.dma_start(out=d, in_=s[:, :]),
                 inc=sWD, inck=16)
        p.op("gpsimd", lambda e: e.dma_start(out=b2all[:, :], in_=b2[:, :]), inc=sWD, inck=16)
        for dst, src in ((eyeb_s, eyeb), (eyef_s, eyef)):
            p.op("sync", lambda e, d=dst, s=src: e.dma_start(out=d[:, :], in_=s[:, :]),
                 inc=sMS, inck=16)
        p.op("sync", lambda e: e.dma_start(
            out=rw_sb[:, :].rearrange("p (c e) -> p c e", e=E),
            in_=rw.rearrange("(c p) e -> p c e", p=128)), inc=sMS, inck=16)
        p.op("sync", lambda e: e.dma_start(
            out=pstage.rearrange("j (h d) -> j h d", d=DH),
            in_=proj.rearrange("h j d -> j h d")), inc=sMS, inck=16)
        setup_in = sMS.n
        setup_gp = sGP.n

        wt_state = {"evac": 0, "war": [0, 0]}

        wstp = [wst128, accq[:, 0:1024]]
        wt_evs = {0: 0, 1: 0}

        def emit_wt(dst, wsrc, extra_pe_wait=0):
            pTWp = [bank(0, BF16)[:, 0:1024], bank(1, BF16)[:, 0:1024]]
            for r in range(8):
                rp = r % 2
                ld = p.op("gpsimd", lambda e, r=r, rp=rp, w=wsrc: e.dma_start(
                    out=wstp[rp][:, :], in_=w[r * 128:(r + 1) * 128, :]),
                    waits=[(sPE, wt_state.get("t" + str(rp), 0))], inc=sWT, inck=16)

                def tfn(e, r=r, rp=rp):
                    for dc in range(8):
                        inst = e.matmul(pTWp[rp][:, dc * 128:(dc + 1) * 128],
                                        wstp[rp][:, dc * 128:(dc + 1) * 128], eyeb_s[:, :],
                                        is_transpose=True, start=(dc == 0), stop=(dc == 7),
                                        skip_group_check=True)
                    return inst
                tv = p.op("tensor", tfn,
                          waits=[(sWT, ld), (sACT, wt_evs[rp]), (sPE, extra_pe_wait),
                                 (sMS, setup_in)], inc=sPE)
                wt_state["t" + str(rp)] = tv
                dv = dst[:, :].rearrange("p (dc n) -> p dc n", n=1024)[:, :, r * 128:(r + 1) * 128]
                ev = p.op("scalar", lambda e, dv=dv, rp=rp: e.activation(
                    out=dv, in_=pTWp[rp].rearrange("p (dc n) -> p dc n", n=128), func=AF.Copy),
                    waits=[(sPE, tv)], inc=sACT)
                wt_evs[rp] = ev
                wt_state["evac"] = ev
            return wt_state["evac"]

        emit_wt(wt_a, wk)
        emit_wt(wt_b, wv)
        pPJ = bank(2, F32, 2)[0:64, :]

        def pj_t(e):
            for h in range(H):
                inst = e.matmul(pPJ[0:64, h * 64:(h + 1) * 64],
                                pstage[:, h * 64:(h + 1) * 64], eyef_s[0:64, 0:64],
                                is_transpose=True, start=(h % 8 == 0), stop=(h % 8 == 7),
                                skip_group_check=True)
            return inst
        pv0 = p.op("tensor", pj_t, waits=[(sMS, setup_in)], inc=sPE)
        pe1 = p.op("scalar", lambda e: e.activation(
            out=hidb[0:64, 0:1024], in_=pPJ[0:64, :], func=AF.Copy, scale=cfix[0:64, 5:6]),
            waits=[(sPE, pv0), (sGP, setup_gp)], inc=sACT)
        pj_st = p.op("sync", lambda e: e.dma_start(out=pscr[:, :], in_=hidb[0:64, 0:1024]),
                     waits=[(sACT, pe1)], inc=sBB, inck=16)
        pjsrc = pscr.rearrange("p (a two d) -> p a two d", two=2, d=64)
        p.op("sync", lambda e: e.dma_start(
            out=projT[0:64, :].rearrange("p (a d) -> p a d", d=64), in_=pjsrc[:, :, 0, :]),
            waits=[(sBB, pj_st)], inc=sMS, inck=16)
        p.op("sync", lambda e: e.dma_start(
            out=projT[64:128, :].rearrange("p (a d) -> p a d", d=64), in_=pjsrc[:, :, 1, :]),
            waits=[(sBB, pj_st)], inc=sMS, inck=16)

        p.barrier(bar, drains=[("sync", sMS), ("sync", sST), ("sync", sX[0]), ("sync", sX[1]), ("sync", sXq[2]), ("sync", sXq[3]), ("gpsimd", sWD), ("gpsimd", sWT), ("gpsimd", sW1), ("gpsimd", sW2), ("gpsimd", sSL[0]), ("gpsimd", sSL[1]), ("sync", sBB)])

        if debug == "SETUP":
            _emit_bodies(p, block)
            return nc

        pT = bank(0, BF16)[:, 0:1024]
        pKV = bank(1, F32, 2)
        pDKp = (bank(3), bank(4))
        pUt = (bank(5)[0:65, 0:455], bank(6)[0:65, 0:455], bank(7)[0:65, 0:130])

        def mcol(h):
            return 8 * (h // 8) + 4 * (h % 2) + (h % 8) // 2

        def u_ap(h):
            return pUt[h // 7], (h % 7) * 65

        dma_d = {}; ln_d = {}; t_d = {}; te_d = {}; kf_d = {}; kfe_d = {}
        sq_d = {}; v_d = {}; ve_d = {}; dk_d = {}; ek_d = {}; st_d = {}; u_d = {}

        junkb = hidb[:, 0:2048].bitcast(BF16)[:, 0:1024]

        def a_ln(i):
            pp = i % 2
            dma_d[i] = p.op("sync", lambda e, i=i, pp=pp: e.dma_start(
                out=xb[pp][:, :], in_=x_full[i * 128:(i + 1) * 128, :]),
                waits=[(sDVE, ln_d.get(i - 2, 0)), (sACT, sqa_d.get(i - 2, 0))],
                inc=sX[pp], inck=16)
            sqa_d[i] = p.op("scalar", lambda e, pp=pp: e.activation(
                out=xnb[pp][:, :], in_=xb[pp][:, :], func=AF.Square,
                accum_out=lnsc[pp][:, 1:2]),
                waits=[(sX[pp], dma_d[i]), (sGP, setup_gp), (sPE, t_d.get(i - 2, 0))], inc=sACT)
            sma = p.op("vector", lambda e, pp=pp: e.tensor_scalar(
                out=junkb[:, :], in0=xb[pp][:, :], scalar1=1.0, scalar2=None, op0=ALU.mult,
                op1=ALU.add, accum_out=lnsc[pp][:, 0:1]),
                waits=[(sX[pp], dma_d[i]), (sPE, t_d.get(i - 2, 0))], inc=sDVE)
            v1a = p.op("vector", lambda e, pp=pp: e.tensor_scalar(
                out=lnsc[pp][:, 2:3], in0=lnsc[pp][:, 0:1], scalar1=1.0 / D, scalar2=None,
                op0=ALU.mult), waits=[(sDVE, sma)], inc=sDVE)
            v1b = p.op("vector", lambda e, pp=pp: e.tensor_tensor(
                out=lnsc[pp][:, 3:4], in0=lnsc[pp][:, 2:3], in1=lnsc[pp][:, 2:3], op=ALU.mult),
                waits=[(sDVE, v1a)], inc=sDVE)
            ve_v = p.op("vector", lambda e, pp=pp: e.tensor_scalar(
                out=lnsc[pp][:, 4:5], in0=lnsc[pp][:, 1:2], scalar1=1.0 / D,
                scalar2=lnsc[pp][:, 3:4], op0=ALU.mult, op1=ALU.subtract),
                waits=[(sDVE, v1b), (sACT, sqa_d[i])], inc=sDVE)
            a1a = p.op("scalar", lambda e, pp=pp: e.activation(
                out=lnsc[pp][:, 5:6], in_=lnsc[pp][:, 4:5], func=AF.Ln, bias=cfix[:, 0:1]),
                waits=[(sDVE, ve_v)], inc=sACT)
            a1 = p.op("scalar", lambda e, pp=pp: e.activation(
                out=lnsc[pp][:, 6:7], in_=lnsc[pp][:, 5:6], func=AF.Exp, scale=cfix[:, 1:2]),
                waits=[(sACT, a1a)], inc=sACT)
            ln_d[i] = p.op("vector", lambda e, pp=pp: e.tensor_scalar(
                out=xnb[pp][:, :], in0=xb[pp][:, :], scalar1=lnsc[pp][:, 2:3],
                scalar2=lnsc[pp][:, 6:7], op0=ALU.subtract, op1=ALU.mult),
                waits=[(sACT, a1)], inc=sDVE)

        def a_heavy(i):
            pp = i % 2

            def tx(e, pp=pp):
                for dc in range(8):
                    inst = e.matmul(pT[:, dc * 128:(dc + 1) * 128],
                                    xnb[pp][:, dc * 128:(dc + 1) * 128], eyeb_s[:, :],
                                    is_transpose=True, start=(dc == 0), stop=(dc == 7),
                                    skip_group_check=True)
                return inst
            t_d[i] = p.op("tensor", tx,
                          waits=[(sDVE, ln_d[i]), (sACT, te_d.get(i - 1, 0))], inc=sPE)
            te_d[i] = p.op("scalar", lambda e: e.activation(
                out=xfb[:, :], in_=pT[:, :], func=AF.Copy),
                waits=[(sPE, t_d[i]), (sPE, v_d.get(i - 1, 0))], inc=sACT)

            def kf(e):
                for ic in range(8):
                    for dc in range(8):
                        e.matmul(pKV[:, ic * 128:(ic + 1) * 128],
                                 wt_a[:, dc * 1024 + ic * 128:dc * 1024 + (ic + 1) * 128],
                                 xfb[:, dc * 128:(dc + 1) * 128],
                                 start=(ic % 4 == 0 and dc == 0), stop=False,
                                 skip_group_check=True)
                for ic in range(8):
                    inst = e.matmul(pKV[:, ic * 128:(ic + 1) * 128],
                                    bkr[0:1, ic * 128:(ic + 1) * 128], onesrow_b[0:1, :],
                                    start=False, stop=(ic % 4 == 3), skip_group_check=True)
                return inst
            kf_d[i] = p.op("tensor", kf,
                           waits=[(sACT, te_d[i]), (sACT, ve_d.get(i - 1, 0))], inc=sPE)
            kfe_d[i] = p.op("scalar", lambda e: e.activation(
                out=kfb[:, :], in_=pKV[:, :], func=AF.Copy),
                waits=[(sPE, kf_d[i]), (sPE, dk_d.get(i - 1, 0))], inc=sACT)
            sq_d[i] = p.op("vector", lambda e: e.tensor_tensor(
                out=sqb[:, :], in0=kfb[:, :], in1=kfb[:, :], op=ALU.mult),
                waits=[(sACT, kfe_d[i]), (sPE, dk_d.get(i - 1, 0))], inc=sDVE)

        sqa_d = {}
        a_ln(0)
        for i in range(nsub_a):
            a_heavy(i)
            if i + 1 < nsub_a:
                a_ln(i + 1)
            pp = i % 2

            def vv(e):
                for nb in range(2):
                    for dc in range(8):
                        e.matmul(pKV[:, nb * 512:(nb + 1) * 512],
                                 xfb[:, dc * 128:(dc + 1) * 128],
                                 wt_b[:, dc * 1024 + nb * 512:dc * 1024 + (nb + 1) * 512],
                                 start=(dc == 0), stop=False, skip_group_check=True)
                for nb in range(2):
                    inst = e.matmul(pKV[:, nb * 512:(nb + 1) * 512],
                                    onesrow_b[0:1, :], bvr[0:1, nb * 512:(nb + 1) * 512],
                                    start=False, stop=True, skip_group_check=True)
                return inst
            v_d[i] = p.op("tensor", vv, waits=[(sACT, kfe_d[i])], inc=sPE)
            ve_d[i] = p.op("scalar", lambda e: e.activation(
                out=vtb[:, :].rearrange("p (h n) -> p h n", n=65)[:, :, 0:64],
                in_=pKV[:, :].rearrange("p (h n) -> p h n", n=64), func=AF.Copy),
                waits=[(sPE, v_d[i]), (sPE, u_d.get(i - 1, 0))], inc=sACT)

            dkw = {}; ekw = {}
            for w in range(2):
                def dk_wave(e, w=w):
                    for k in range(8):
                        h = 8 * w + k
                        rb = (h % 2) * 64
                        sl = k // 2
                        pb = pDKp[h % 2]
                        cb = (h // 2) * 128
                        kf_h = kfb[rb:rb + 64, cb:cb + 128]
                        sq_h = sqb[rb:rb + 64, cb:cb + 128]
                        e.matmul(pb[:, sl * 64:(sl + 1) * 64], kf_h,
                                 projT[rb:rb + 64, (h // 2) * 64:(h // 2) * 64 + 64],
                                 start=(k < 2), stop=False,
                                 skip_group_check=True, tile_position=(rb, 0))
                        e.matmul(pb[:, sl * 64:(sl + 1) * 64], sq_h, negq[rb:rb + 64, :],
                                 start=False, stop=False, skip_group_check=True,
                                 tile_position=(rb, 0))
                        inst = e.matmul(pb[:, 256 + sl:257 + sl], sq_h,
                                        posq[rb:rb + 64, :],
                                        start=(False), stop=(k >= 6), skip_group_check=True,
                                        tile_position=(rb, 0))
                    return inst
                if w == 0:
                    wlw = [(sDVE, sq_d[i]), (sACT, ek_d.get(i - 1, 0))]
                else:
                    wlw = [(sACT, ekw[0])]
                dkw[w] = p.op("tensor", dk_wave, waits=wlw, inc=sPE)

                def ekfn(e, w=w):
                    for par in range(2):
                        base = (8 * w + par) * 65
                        ov = ekb[:, base:base + 4 * 130].rearrange(
                            "p (t n) -> p t n", n=130)[:, :, 0:64]
                        inst = e.activation(
                            out=ov,
                            in_=pDKp[par][:, 0:256].rearrange("p (t n) -> p t n", n=64),
                            func=AF.Exp)
                    return inst
                ekw[w] = p.op("scalar", ekfn,
                    waits=[(sPE, dkw[w]),
                           (sPE, u_d.get(i - 1, 0) if w == 0 else 0),
                           (sDVE, st_d.get(i - 1, 0) if w == 0 else 0)],
                    inc=sACT)

            dk_d[i] = dkw[1]
            ek_d[i] = ekw[1]
            sv1 = p.op("vector", lambda e: e.tensor_reduce(
                out=stsc[:, 0:16],
                in_=ekb[:, 0:1040].rearrange("p (h n) -> p h n", n=65),
                axis=AX.X, op=ALU.max), waits=[(sACT, ekw[1])], inc=sDVE)
            st_d[i] = p.op("vector", lambda e: e.tensor_tensor(
                out=mrun[:, :], in0=mrun[:, :], in1=stsc[:, 0:16], op=ALU.max),
                waits=[(sDVE, sv1)], inc=sDVE)

            def uacc(e, i=i):
                for h in range(H):
                    ut, c = u_ap(h)
                    e.matmul(ut[:, c:c + 64],
                             ekb[:, h * 65:(h + 1) * 65], vtb[:, h * 65:h * 65 + 64],
                             start=(i == 0 and h % 7 == 0), stop=False, skip_group_check=True)
                    inst = e.matmul(ut[:, c + 64:c + 65],
                                    ekb[:, h * 65:(h + 1) * 65], onescol_b[:, :],
                                    start=False, stop=False, skip_group_check=True)
                return inst
            u_d[i] = p.op("tensor", uacc,
                          waits=[(sACT, ek_d[i]), (sACT, ve_d[i])], inc=sPE)

        fv = p.op("tensor", lambda e: e.matmul(
            pDKp[0][0:16, 0:128], mrun[:, :], eyef_s[:, :], is_transpose=True,
            start=True, stop=True, skip_group_check=True),
            waits=[(sDVE, st_d[nsub_a - 1]), (sPE, u_d[nsub_a - 1]),
                   (sACT, ek_d[nsub_a - 1])], inc=sPE)
        fv2 = p.op("vector", lambda e: e.tensor_reduce(
            out=stsc[0:16, 0:1], in_=pDKp[0][0:16, 0:128], axis=AX.X, op=ALU.max),
            waits=[(sPE, fv)], inc=sDVE)
        fv3 = p.op("tensor", lambda e: e.matmul(
            pDKp[1][0:1, 0:16], stsc[0:16, 0:1], eyef_s[0:16, 0:16], is_transpose=True,
            start=True, stop=True, skip_group_check=True), waits=[(sDVE, fv2)], inc=sPE)
        fv4 = p.op("scalar", lambda e: e.activation(
            out=m_row[:, :], in_=pDKp[1][0:1, 0:16], func=AF.Ln), waits=[(sPE, fv3)], inc=sACT)

        def fexp(e):
            e.activation(out=a1_row[:, :], in_=m_row[:, :], func=AF.Exp, bias=cfix[0:1, 2:3])
            return e.activation(out=c1_row[:, :], in_=m_row[:, :], func=AF.Exp,
                                scale=cfix[0:1, 4:5], bias=cfix[0:1, 3:4])
        fv5 = p.op("scalar", fexp, waits=[(sACT, fv4)], inc=sACT)

        def ftr(e):
            for h in range(H):
                ut, c = u_ap(h)
                inst = e.activation(out=trows[0:1, h * 65:(h + 1) * 65],
                                    in_=ut[64:65, c:c + 65], func=AF.Copy,
                                    scale=a1_row[0:1, h:h + 1])
            return inst
        fv6 = p.op("scalar", ftr, waits=[(sACT, fv5)], inc=sACT)

        def fcorr(e):
            for h in range(H):
                ut, c = u_ap(h)
                last = (h % 7 == 6) or (h == H - 1)
                inst = e.matmul(ut[0:64, c:c + 65], onesrow_b[0:1, 0:64],
                                trows[0:1, h * 65:(h + 1) * 65],
                                start=False, stop=last, skip_group_check=True)
            return inst
        fv7 = p.op("tensor", fcorr, waits=[(sACT, fv6)], inc=sPE)
        fv8 = p.op("tensor", lambda e: e.matmul(
            pDKp[0][0:64, 16:32], onesrow[0:1, 0:64], c1_row[0:1, :],
            start=True, stop=True, skip_group_check=True),
            waits=[(sACT, fv6), (sDVE, fv2)], inc=sPE)
        fv9 = p.op("scalar", lambda e: e.activation(
            out=c1b[:, :], in_=pDKp[0][0:64, 16:32], func=AF.Copy), waits=[(sPE, fv8)], inc=sACT)

        def fkv(e):
            for h in range(H):
                ut, c = u_ap(h)
                inst = e.activation(out=kv_sb[:, h * 65:(h + 1) * 65],
                                    in_=ut[0:64, c:c + 65], func=AF.Copy,
                                    scale=c1b[:, h:h + 1])
            return inst
        fv10 = p.op("scalar", fkv, waits=[(sPE, fv7), (sACT, fv9)], inc=sACT)
        pKS = bank(0, BF16)[0:1, 0:1024]

        def fkst(e):
            for h in range(H):
                inst = e.matmul(pKS[0:1, h * 64:(h + 1) * 64],
                                kv_sb[:, h * 65 + 64:h * 65 + 65], eyeb_s[0:64, 0:64],
                                is_transpose=True, start=(h == 0), stop=(h == H - 1),
                                skip_group_check=True)
            return inst
        fv11 = p.op("tensor", fkst, waits=[(sACT, fv10)], inc=sPE)
        fv12 = p.op("scalar", lambda e: e.activation(
            out=trows[0:1, 0:1024], in_=pKS[0:1, :], func=AF.Copy),
            waits=[(sPE, fv11)], inc=sACT)
        pKB = bank(1, F32, 2)

        def fksb(e):
            for nb in range(2):
                inst = e.matmul(pKB[:, nb * 512:(nb + 1) * 512], onesrow_b[0:1, :],
                                trows[0:1, nb * 512:(nb + 1) * 512],
                                start=True, stop=True, skip_group_check=True)
            return inst
        fv13 = p.op("tensor", fksb, waits=[(sACT, fv12)], inc=sPE)
        fv14 = p.op("scalar", lambda e: e.activation(
            out=ksumB[:, :], in_=pKB[:, :], func=AF.Copy), waits=[(sPE, fv13)], inc=sACT)

        if debug == "A":
            dbgsrc = {"kfb": kfb[:, 0:1024], "xfb": xfb[:, 0:1024], "sqb": sqb[:, 0:1024],
                      "vtb": vtb[:, 0:1024], "ekb": ekb[:, 0:1024],
                      "xnb": xnb[0][:, 0:1024], "lnsc": lnsc[0][:, 0:8]}[dbgname]
            d0 = p.op("vector", lambda e: e.memset(scr32[:, 0:2048], 0.0),
                      waits=[(sACT, fv14)], inc=sDVE)
            d1 = p.op("vector", lambda e: e.tensor_copy(out=scr32[0:64, 0:1040], in_=kv_sb[:, :]),
                      waits=[(sDVE, d0)], inc=sDVE)
            d2 = p.op("vector", lambda e: e.tensor_copy(out=scr32[0:1, 1040:1056], in_=m_row[:, :]),
                      waits=[(sDVE, d1)], inc=sDVE)
            d3 = p.op("vector", lambda e: e.tensor_copy(out=scr32[:, 2048:3072], in_=ksumB[:, :]),
                      waits=[(sACT, fv14)], inc=sDVE)
            d4a = p.op("vector", lambda e: e.memset(scr32[:, 0:1024], 0.0),
                       waits=[(sDVE, d3), (sST, 32)], inc=sDVE)
            d4 = p.op("vector", lambda e: e.tensor_copy(
                out=scr32[:, 0:dbgsrc.shape[1]], in_=dbgsrc),
                waits=[(sDVE, d4a)], inc=sDVE)
            p.op("sync", lambda e: e.dma_start(out=dbg[:, 0:2048], in_=scr32[:, 0:2048]),
                 waits=[(sDVE, d2)], inc=sST, inck=16)
            p.op("sync", lambda e: e.dma_start(out=dbg[:, 2048:3072], in_=scr32[:, 2048:3072]),
                 waits=[(sDVE, d3)], inc=sST, inck=16)
            p.op("sync", lambda e: e.dma_start(out=dbg[:, 3072:4096], in_=scr32[:, 0:1024]),
                 waits=[(sDVE, d4)], inc=sST, inck=16)
            p.op("sync", lambda e: None, waits=[(sST, sST.n)])
            _emit_bodies(p, block)
            return nc

        p.barrier(bar, drains=[("sync", sMS), ("sync", sST), ("sync", sX[0]), ("sync", sX[1]), ("sync", sXq[2]), ("sync", sXq[3]), ("gpsimd", sWD), ("gpsimd", sWT), ("gpsimd", sW1), ("gpsimd", sW2), ("gpsimd", sSL[0]), ("gpsimd", sSL[1]), ("sync", sBB)])

        st_total = Sem(None)
        hid_last = {"pe": 0}
        out_last = {"pe": 0}
        w1_ld = {"v": 0}
        w2_ld = {"v": 0}
        b1_rdy = {0: 0, 1: 0}
        b2_rdy = {0: 0, 1: 0}
        gelu_last = {0: 0, 1: 0}
        fin_d = {}
        store_d = {}

        for q in range(4):
            pTf = bank(0, F32, 2)
            pR = bank(2)[:, 0:128]
            pWT = bank(4, F32, 2)
            pTW2 = bank(3, BF16)[:, 0:1024]
            c0_ln = {}; c0_t = {}; c0_xe = {}; c0_xr = {}; c0_r = {}; c0_m = {}; c0_sqa = {}
            c0_wme = {}
            if q == 0:
                wt_state["evac"] = 0

                def emit_wt2(dst, wsrc):
                    for r in range(8):
                        ld = p.op("gpsimd", lambda e, r=r, w=wsrc: e.dma_start(
                            out=wst128[:, :], in_=w[r * 128:(r + 1) * 128, :]),
                            waits=[(sACT, wt_state["evac"])], inc=sWT, inck=16)

                        def tfn(e, r=r):
                            for dc in range(8):
                                inst = e.matmul(pTW2[:, dc * 128:(dc + 1) * 128],
                                                wst128[:, dc * 128:(dc + 1) * 128], eyeb_s[:, :],
                                                is_transpose=True, start=(dc == 0), stop=(dc == 7),
                                                skip_group_check=True)
                            return inst
                        tv = p.op("tensor", tfn, waits=[(sWT, ld), (sACT, wt_state["evac"])],
                                  inc=sPE)
                        dv = dst[:, :].rearrange("p (dc n) -> p dc n", n=1024)[:, :, r * 128:(r + 1) * 128]
                        ev = p.op("scalar", lambda e, dv=dv: e.activation(
                            out=dv, in_=pTW2.rearrange("p (dc n) -> p dc n", n=128), func=AF.Copy),
                            waits=[(sPE, tv)], inc=sACT)
                        wt_state["evac"] = ev
                emit_wt2(wt_a, wq)
                for r in range(8):
                    ld = p.op("gpsimd", lambda e, r=r: e.dma_start(
                        out=wst128[:, :], in_=wo[r * 128:(r + 1) * 128, :]),
                        waits=[(sBB, wt_state.get("wost", 0)), (sACT, wt_state["evac"])],
                        inc=sWT, inck=16)

                    def tfn(e, r=r):
                        for dc in range(8):
                            inst = e.matmul(pTW2[:, dc * 128:(dc + 1) * 128],
                                            wst128[:, dc * 128:(dc + 1) * 128], eyeb_s[:, :],
                                            is_transpose=True, start=(dc == 0), stop=(dc == 7),
                                            skip_group_check=True)
                        return inst
                    tv = p.op("tensor", tfn, waits=[(sWT, ld), (sACT, wt_state["evac"])],
                              inc=sPE)
                    ev = p.op("scalar", lambda e: e.activation(
                        out=wst128[:, :], in_=pTW2[:, :], func=AF.Copy),
                        waits=[(sPE, tv)], inc=sACT)
                    wt_state["evac"] = ev
                    st = p.op("sync", lambda e, r=r: e.dma_start(
                        out=wodram.rearrange("(dc p) i -> p dc i", p=128)[:, :, r * 128:(r + 1) * 128],
                        in_=wst128[:, :].rearrange("p (dc i) -> p dc i", i=128)),
                        waits=[(sACT, ev)], inc=sBB, inck=16)
                    wt_state["wost"] = st
                for a in range(8):
                    p.op("sync", lambda e, a=a: e.dma_start(
                        out=wt_b[0:64, a * 1024:(a + 1) * 1024],
                        in_=wodram[a * 64:(a + 1) * 64, :]),
                        waits=[(sBB, wt_state["wost"])], inc=sMS, inck=16)
                    p.op("sync", lambda e, a=a: e.dma_start(
                        out=wt_b[64:128, a * 1024:(a + 1) * 1024],
                        in_=wodram[(a + 8) * 64:(a + 9) * 64, :]),
                        waits=[(sBB, wt_state["wost"])], inc=sMS, inck=16)

            def emit_ln(j):
                pp = j % 2
                ix = j % 4
                row0 = q * NQ + j * 128
                dmj = p.op("sync", lambda e, row0=row0, ix=ix: e.dma_start(
                    out=xb4[ix][:, :], in_=x_own[row0:row0 + 128, :]),
                    waits=[(sDVE, c0_ln.get(j - 4, 0)), (sPE, c0_t.get(j - 4, 0)),
                           (sACT, c0_sqa.get(j - 4, 0))],
                    inc=sXq[ix], inck=16)

                sqa = p.op("scalar", lambda e, pp=pp, ix=ix: e.activation(
                    out=xnb[pp][:, :], in_=xb4[ix][:, :], func=AF.Square,
                    accum_out=lnsc[ix][:, 1:2]),
                    waits=[(sXq[ix], dmj)], inc=sACT)
                c0_sqa[j] = sqa
                sma = p.op("vector", lambda e, ix=ix: e.tensor_scalar(
                    out=junkb[:, :], in0=xb4[ix][:, :], scalar1=1.0, scalar2=None, op0=ALU.mult,
                    op1=ALU.add, accum_out=lnsc[ix][:, 0:1]),
                    waits=[(sXq[ix], dmj)], inc=sDVE)
                v1a = p.op("vector", lambda e, ix=ix: e.tensor_scalar(
                    out=lnsc[ix][:, 2:3], in0=lnsc[ix][:, 0:1], scalar1=1.0 / D, scalar2=None,
                    op0=ALU.mult), waits=[(sDVE, sma)], inc=sDVE)
                v1b = p.op("vector", lambda e, ix=ix: e.tensor_tensor(
                    out=lnsc[ix][:, 3:4], in0=lnsc[ix][:, 2:3], in1=lnsc[ix][:, 2:3],
                    op=ALU.mult), waits=[(sDVE, v1a)], inc=sDVE)
                v1 = p.op("vector", lambda e, ix=ix: e.tensor_scalar(
                    out=lnsc[ix][:, 4:5], in0=lnsc[ix][:, 1:2], scalar1=1.0 / D,
                    scalar2=lnsc[ix][:, 3:4], op0=ALU.mult, op1=ALU.subtract),
                    waits=[(sDVE, v1b), (sACT, sqa)], inc=sDVE)
                a1a = p.op("scalar", lambda e, ix=ix: e.activation(
                    out=lnsc[ix][:, 5:6], in_=lnsc[ix][:, 4:5], func=AF.Ln, bias=cfix[:, 0:1]),
                    waits=[(sDVE, v1)], inc=sACT)
                a1 = p.op("scalar", lambda e, ix=ix: e.activation(
                    out=lnsc[ix][:, 6:7], in_=lnsc[ix][:, 5:6], func=AF.Exp, scale=cfix[:, 1:2]),
                    waits=[(sACT, a1a)], inc=sACT)
                c0_ln[j] = p.op("vector", lambda e, ix=ix: e.tensor_scalar(
                    out=xb4[ix][:, :], in0=xb4[ix][:, :], scalar1=lnsc[ix][:, 2:3],
                    scalar2=lnsc[ix][:, 6:7], op0=ALU.subtract, op1=ALU.mult),
                    waits=[(sACT, a1)], inc=sDVE)

            def emit_route(j):
                pp = j % 2
                ix = j % 4

                def tx(e, ix=ix):
                    for dc in range(8):
                        inst = e.matmul(pTf[:, dc * 128:(dc + 1) * 128],
                                        xb4[ix][:, dc * 128:(dc + 1) * 128], eyef_s[:, :],
                                        is_transpose=True, start=(dc % 4 == 0), stop=(dc % 4 == 3),
                                        skip_group_check=True)
                    return inst
                c0_t[j] = p.op("tensor", tx,
                               waits=[(sDVE, c0_ln[j]), (sACT, c0_xe.get(j - 1, 0)),
                                      (sACT, c0_xr.get(j - 1, 0))], inc=sPE)
                dvx = xnF[:, :].rearrange("p (dc n) -> p dc n", n=NQ)[:, :, j * 128:(j + 1) * 128]
                c0_xe[j] = p.op("scalar", lambda e, dvx=dvx: e.activation(
                    out=dvx, in_=pTf[:, :].rearrange("p (dc n) -> p dc n", n=128), func=AF.Copy),
                    waits=[(sPE, c0_t[j])], inc=sACT)
                c0_xr[j] = p.op("scalar", lambda e, pp=pp: e.activation(
                    out=xr32[pp], in_=pTf[:, :], func=AF.Copy),
                    waits=[(sPE, c0_t[j]), (sPE, c0_r.get(j - 1, 0))], inc=sACT)

                def rmm(e, pp=pp):
                    for dc in range(8):
                        inst = e.matmul(pR[:, 0:E], xr32[pp][:, dc * 128:(dc + 1) * 128],
                                        rw_sb[:, dc * E:(dc + 1) * E],
                                        start=(dc == 0), stop=(dc == 7), skip_group_check=True)
                    return inst
                c0_r[j] = p.op("tensor", rmm,
                               waits=[(sACT, c0_xr[j]), (sDVE, c0_m.get(j - 1, 0))], inc=sPE)

                rp0 = p.op("vector", lambda e: e.tensor_copy(out=lgt[:, :], in_=pR[:, 0:E]),
                           waits=[(sPE, c0_r[j])], inc=sDVE)
                rp1v = p.op("vector", lambda e: e.tensor_reduce(
                    out=nlmax[:, :], in_=lgt[:, :], axis=AX.X, op=ALU.max, negate=True),
                    waits=[(sDVE, rp0)], inc=sDVE)
                rp2v = p.op("scalar", lambda e: e.activation(
                    out=ejk[:, :], in_=lgt[:, :], func=AF.Exp, bias=nlmax[:, 0:1],
                    accum_out=ssum[:, :]), waits=[(sDVE, rp1v)], inc=sACT)
                rp3a = p.op("vector", lambda e: e.max(out=mx8[:, :], in_=lgt[:, :]),
                            waits=[(sACT, rp2v)], inc=sDVE)
                rp3b = p.op("vector", lambda e: e.max_index(
                    out=ix8[:, :], in_max=mx8[:, :], in_values=lgt[:, :]),
                    waits=[(sDVE, rp3a)], inc=sDVE)
                rp3c = p.op("vector", lambda e: (
                    e.tensor_copy(out=ixf[:, :], in_=ix8[:, 0:1]),
                    e.reciprocal(out=wrec[:, :], in_=ssum[:, :]))[-1],
                    waits=[(sDVE, rp3b)], inc=sDVE)
                c0_m[j] = p.op("vector", lambda e, j=j: e.tensor_scalar(
                    out=wmask[:, j * E:(j + 1) * E], in0=iotaB[:, :],
                    scalar1=ixf[:, 0:1], scalar2=wrec[:, 0:1],
                    op0=ALU.is_equal, op1=ALU.mult), waits=[(sDVE, rp3c)], inc=sDVE)

            emit_ln(0)
            for j in range(NSUB_Q):
                if j + 1 < NSUB_Q:
                    emit_ln(j + 1)
                emit_route(j)

            def wtall(e):
                for j in range(NSUB_Q):
                    inst = e.matmul(pWT[0:14, j * 128:(j + 1) * 128],
                                    wmask[:, j * E:(j + 1) * E], eyef_s[:, :],
                                    is_transpose=True, start=(j % 4 == 0), stop=(j % 4 == 3),
                                    skip_group_check=True)
                return inst
            wt_v = p.op("tensor", wtall, waits=[(sDVE, c0_m[NSUB_Q - 1])], inc=sPE)
            p.op("scalar", lambda e: e.activation(
                out=wmTq[:, 0:NSUB_Q * 128], in_=pWT[0:14, 0:NSUB_Q * 128], func=AF.Copy),
                waits=[(sPE, wt_v)], inc=sACT)

            if debug == "WT" and q == 0:
                dW0 = p.op("vector", lambda e: e.tensor_copy(
                    out=scr32[:, 0:1024], in_=wt_b[:, 0:1024]),
                    waits=[(sMS, sMS.n), (sBB, sBB.n), (sWT, sWT.n),
                           (sDVE, c0_m[NSUB_Q - 1])], inc=sDVE)
                p.op("sync", lambda e: e.dma_start(out=dbg[:, 0:1024], in_=scr32[:, 0:1024]),
                     waits=[(sDVE, dW0)], inc=sST, inck=16)
                dW1 = p.op("vector", lambda e: e.tensor_copy(
                    out=scr32[:, 0:1024], in_=wt_a[:, 0:1024]),
                    waits=[(sST, sST.n)], inc=sDVE)
                p.op("sync", lambda e: e.dma_start(out=dbg[:, 1024:2048], in_=scr32[:, 0:1024]),
                     waits=[(sDVE, dW1)], inc=sST, inck=16)
                dW2 = p.op("vector", lambda e: e.tensor_copy(
                    out=scr32[:, 0:1024], in_=projT[:, 0:512].broadcast_to([128, 512]) if False else wt_b[:, 1024:2048]),
                    waits=[(sST, sST.n)], inc=sDVE)
                p.op("sync", lambda e: e.dma_start(out=dbg[:, 2048:3072], in_=scr32[:, 0:1024]),
                     waits=[(sDVE, dW2)], inc=sST, inck=16)
                p.op("sync", lambda e: None, waits=[(sST, sST.n)])
                _emit_bodies(p, block)
                return nc

            if debug == "C0" and q == 0:
                d1 = p.op("vector", lambda e: e.tensor_copy(
                    out=scr32[:, 0:NSUB_Q * E], in_=wmask[:, :]),
                    waits=[(sDVE, c0_m[NSUB_Q - 1])], inc=sDVE)
                p.op("sync", lambda e: e.dma_start(out=dbg[:, 0:NSUB_Q * E],
                                                   in_=scr32[:, 0:NSUB_Q * E]),
                     waits=[(sDVE, d1)], inc=sST, inck=16)
                p.op("sync", lambda e: None, waits=[(sST, sST.n)])
                _emit_bodies(p, block)
                return nc

            p.barrier(bar, drains=[("sync", sMS), ("sync", sST), ("sync", sX[0]), ("sync", sX[1]), ("sync", sXq[2]), ("sync", sXq[3]), ("gpsimd", sWD), ("gpsimd", sWT), ("gpsimd", sW1), ("gpsimd", sW2), ("gpsimd", sSL[0]), ("gpsimd", sSL[1]), ("sync", sBB)])

            pH = (bank(0), bank(1))
            pO = (bank(2, F32, 2), bank(4, F32, 2))
            pB1 = bank(6, BF16)[:, 0:32]
            gelu_d = {}
            outmm_d = {}
            stt_d = {}
            b1tv_prev = {"v": 0}
            out_em = {0: 0, 1: 0}
            for s in range(NSUB_Q):
                def b2mm(e, s=s):
                    for hf in range(2):
                        inst = e.matmul(pO[s % 2][:, hf * 512:(hf + 1) * 512],
                                        wmTq[:, s * 128:(s + 1) * 128],
                                        b2all[:, hf * 512:(hf + 1) * 512],
                                        start=True, stop=True, skip_group_check=True)
                    return inst
                bmv = p.op("tensor", b2mm,
                           waits=[(sDVE, stt_d.get(("init", s - 2), 0))], inc=sPE)
                stt_d[("init", s)] = p.op("vector", lambda e, s=s: e.tensor_copy(
                    out=accq[:, s * 1024:(s + 1) * 1024], in_=pO[s % 2][:, :]),
                    waits=[(sPE, bmv), (sDVE, fin_d.get((q - 1, 7), 0) if s == 0 else 0)],
                    inc=sDVE)
            for eix in range(E):
                em = eix % 2
                b1_ld = p.op("gpsimd", lambda e, eix=eix: e.dma_start(
                    out=b1row[0:1, :], in_=b1[eix:eix + 1, :]),
                    waits=[(sPE, b1tv_prev["v"]), (sACT, b1_rdy[em])], inc=sSL[em], inck=16)
                w1_ld["v"] = p.op("gpsimd", lambda e, eix=eix: e.dma_start(
                    out=w1buf[:, :].rearrange("p (dc c) -> p dc c", c=FF),
                    in_=w1[eix].rearrange("(dc p) c -> p dc c", p=128)),
                    waits=[(sPE, hid_last["pe"])], inc=sW1, inck=16)
                w2_ld["v"] = p.op("gpsimd", lambda e, eix=eix: e.dma_start(
                    out=w2buf[:, :].rearrange("p (fc c) -> p fc c", c=D),
                    in_=w2[eix].rearrange("(fc p) c -> p fc c", p=128)),
                    waits=[(sPE, out_last["pe"])], inc=sW2, inck=16)

                def b1t(e):
                    for k in range(16):
                        inst = e.matmul(pB1[:, 2 * k:2 * k + 1],
                                        b1row[0:1, k * 128:(k + 1) * 128],
                                        eyeb_s[0:1, 0:1],
                                        is_transpose=True, start=(k == 0), stop=(k == 15),
                                        skip_group_check=True)
                    return inst
                b1tv = p.op("tensor", b1t, waits=[(sSL[em], b1_ld), (sACT, b1_rdy[em])], inc=sPE)
                b1tv_prev["v"] = b1tv
                b1_rdy[em] = p.op("scalar", lambda e, em=em: e.activation(
                    out=b1c[em][:, :],
                    in_=pB1[:, 0:32].rearrange("p (k two) -> p k two", two=2)[:, :, 0:1],
                    func=AF.Copy),
                    waits=[(sPE, b1tv), (sACT, gelu_last[em])], inc=sACT)
                b1_now = b1_rdy[em]

                for g in range(2):
                    for fo in range(16):
                        def hmm(e, g=g, fo=fo):
                            for dc in range(8):
                                inst = e.matmul(
                                    pH[fo % 2][:, :],
                                    w1buf[:, dc * FF + fo * 128:dc * FF + (fo + 1) * 128],
                                    xnF[:, dc * NQ + g * 512:dc * NQ + (g + 1) * 512],
                                    start=(dc == 0), stop=(dc == 7), skip_group_check=True)
                            return inst
                        pw = gelu_d.get((g, fo - 2), 0) if fo >= 2 else \
                            gelu_d.get((1 if g == 0 else g - 1, fo + 14), 0)
                        hv = p.op("tensor", hmm,
                                  waits=[(sW1, w1_ld["v"] if (g == 0 and fo == 0) else 0),
                                         (sACT, pw)],
                                  inc=sPE)
                        if g == 1 and fo == 15:
                            hid_last["pe"] = hv
                        gwl = [(sPE, hv), (sACT, b1_now if fo == 0 and g == 0 else 0),
                               (sPE, outmm_d.get((g, 3), 0) if fo <= 1 else 0)]
                        gelu_d[(g, fo)] = p.op("scalar", lambda e, g=g, fo=fo, em=em: e.activation(
                            out=htile(g, fo), in_=pH[fo % 2][:, :],
                            func=AF.Gelu, bias=b1c[em][:, fo:fo + 1]),
                            waits=gwl, inc=sACT)
                gelu_last[em] = gelu_d[(1, 15)]
                for g in range(2):
                    for ts_ in range(4):
                        s = g * 4 + ts_

                        def omm(e, g=g, ts_=ts_, em=em):
                            po = pO[ts_ % 2]
                            for fc in range(16):
                                for hf in range(2):
                                    inst = e.matmul(po[:, hf * 512:(hf + 1) * 512],
                                             htile(g, fc)[:, ts_ * 128:(ts_ + 1) * 128],
                                             w2buf[:, fc * D + hf * 512:fc * D + (hf + 1) * 512],
                                             start=(fc == 0), stop=(fc == 15),
                                             skip_group_check=True)
                            return inst
                        prev_stt = stt_d.get((g, ts_ - 2), 0) if ts_ >= 2 else \
                            stt_d.get((0 if g == 1 else 1, ts_ + 2), 0)
                        if eix == 0 and prev_stt == 0:
                            prev_stt = stt_d.get(("init", 6 + ts_ % 2), 0)
                        owl = [(sACT, gelu_d[(g, 15)]),
                               (sW2, w2_ld["v"] if (g == 0 and ts_ == 0) else 0),
                               (sDVE, prev_stt)]
                        outmm_d[(g, ts_)] = p.op("tensor", omm, waits=owl, inc=sPE)
                        if g == 1 and ts_ == 3:
                            out_last["pe"] = outmm_d[(g, ts_)]
                            out_em[em] = outmm_d[(g, ts_)]

                        def sttf(e, s=s, ts_=ts_, eix=eix):
                            po = pO[ts_ % 2]
                            acc_sl = accq[:, s * 1024:(s + 1) * 1024]
                            msk = wmask[:, s * E + eix:s * E + eix + 1]
                            return e.scalar_tensor_tensor(out=acc_sl, in0=po[:, :], scalar=msk,
                                                          in1=acc_sl, op0=ALU.mult, op1=ALU.add)
                        stt_d[(g, ts_)] = p.op("vector", sttf,
                                               waits=[(sPE, outmm_d[(g, ts_)]),
                                                      (sDVE, stt_d.get(("init", s), 0) if eix == 0 else 0)],
                                               inc=sDVE)

            if debug == "C1" and q == 0:
                last_store = 0
                for half in range(2):
                    d1 = p.op("vector", lambda e, half=half: e.tensor_copy(
                        out=scr32[:, 0:2048], in_=accq[:, half * 2048:(half + 1) * 2048]),
                        waits=[(sDVE, stt_d[(1, 3)]), (sST, last_store)], inc=sDVE)
                    last_store = p.op("sync", lambda e, half=half: e.dma_start(
                        out=dbg[:, half * 2048:(half + 1) * 2048], in_=scr32[:, 0:2048]),
                        waits=[(sDVE, d1)], inc=sST, inck=16)
                p.op("sync", lambda e: None, waits=[(sST, sST.n)])
                _emit_bodies(p, block)
                return nc

            p.barrier(bar, drains=[("sync", sMS), ("sync", sST), ("sync", sX[0]), ("sync", sX[1]), ("sync", sXq[2]), ("sync", sXq[3]), ("gpsimd", sWD), ("gpsimd", sWT), ("gpsimd", sW1), ("gpsimd", sW2), ("gpsimd", sSL[0]), ("gpsimd", sSL[1]), ("sync", sBB)])

            pQ = bank(0, F32, 2)
            pQT = bank(6, BF16, 2)
            pDKdp = (bank(2), bank(3))
            pAO = bank(4, F32, 2)
            pWo = bank(6, F32, 2)
            d_dma = {}; d_q = {}; d_qe = {}; d_sq = {}; d_dk = {}; d_qp = {}; d_ek = {}
            d_pr = {}; d_qt = {}; d_qte = {}; d_ao = {}; d_aoe = {}; d_wo = {}

            def emit_qmm(j):
                def qmm(e, j=j):
                    for ic in range(8):
                        for dc in range(8):
                            e.matmul(pQ[:, ic * 128:(ic + 1) * 128],
                                     wt_a[:, dc * 1024 + ic * 128:dc * 1024 + (ic + 1) * 128],
                                     xnF[:, dc * NQ + j * 128:dc * NQ + (j + 1) * 128],
                                     start=(ic % 4 == 0 and dc == 0), stop=False,
                                     skip_group_check=True)
                    for ic in range(8):
                        inst = e.matmul(pQ[:, ic * 128:(ic + 1) * 128],
                                        bqr[0:1, ic * 128:(ic + 1) * 128], onesrow_b[0:1, :],
                                        start=False, stop=(ic % 4 == 3), skip_group_check=True)
                    return inst
                d_q[j] = p.op("tensor", qmm,
                              waits=[(sDVE, d_qe.get(j - 1, 0))], inc=sPE)

            def emit_qesq(j, war=0):
                d_qe[j] = p.op("vector", lambda e: e.tensor_copy(
                    out=qfb[:, :], in_=pQ[:, :]),
                    waits=[(sPE, max(d_q[j], war))], inc=sDVE)
                d_sq[j] = p.op("vector", lambda e: e.tensor_tensor(
                    out=sqb[:, :], in0=qfb[:, :], in1=qfb[:, :], op=ALU.mult),
                    waits=[(sDVE, d_qe[j])], inc=sDVE)
            def emit_tail_a(j):
                def qtf(e):
                    for h in range(H):
                        inst = e.matmul(pQT[0:64, h * 128:(h + 1) * 128],
                                        qnb[:, h * 64:(h + 1) * 64], eyeb_s[:, :],
                                        is_transpose=True, start=(h % 8 == 0), stop=(h % 8 == 7),
                                        skip_group_check=True)
                    return inst
                d_qt[j] = p.op("tensor", qtf,
                               waits=[(sDVE, d_pr[j]), (sDVE, fin_d.get((q, j - 1), 0))],
                               inc=sPE)
                d_qte[j] = p.op("vector", lambda e: e.tensor_copy(
                    out=qtb[:, :], in_=pQT[0:64, 0:2048]),
                    waits=[(sPE, d_qt[j])], inc=sDVE)


            def emit_tail_b(j):
                pp = j % 2
                row0 = q * NQ + j * 128
                def aof(e):
                    for h in range(H):
                        a = h % 8
                        rb = (h // 8) * 64
                        inst = e.matmul(pAO[rb:rb + 64, a * 128:(a + 1) * 128],
                                        kv_sb[:, h * 65:h * 65 + 64],
                                        qtb[:, h * 128:(h + 1) * 128],
                                        start=(h % 4 == 0), stop=(h == 11 or h == 15),
                                        skip_group_check=True, tile_position=(0, rb))
                    return inst
                d_ao[j] = p.op("tensor", aof,
                               waits=[(sDVE, d_qte[j]), (sACT, d_aoe.get(j - 1, 0))], inc=sPE)
                d_aoe[j] = p.op("scalar", lambda e: e.activation(
                    out=apre[:, :], in_=pAO[:, :], func=AF.Copy),
                    waits=[(sPE, d_ao[j]), (sPE, d_wo.get(j - 1, 0))], inc=sACT)

                def wof(e):
                    for nb in range(2):
                        for a in range(8):
                            e.matmul(pWo[:, nb * 512:(nb + 1) * 512],
                                     apre[:, a * 128:(a + 1) * 128],
                                     wt_b[:, a * 1024 + nb * 512:a * 1024 + (nb + 1) * 512],
                                     start=(a == 0), stop=False, skip_group_check=True)
                    for nb in range(2):
                        inst = e.matmul(pWo[:, nb * 512:(nb + 1) * 512], onesrow_b[0:1, :],
                                        bor[0:1, nb * 512:(nb + 1) * 512],
                                        start=False, stop=True, skip_group_check=True)
                    return inst
                if debug == "D0" and q == 0 and j == 0:
                    dD0 = p.op("vector", lambda e: e.tensor_copy(
                        out=scr32[:, 0:1024], in_=apre[:, 0:1024]),
                        waits=[(sACT, d_aoe[0])], inc=sDVE)
                    p.op("sync", lambda e: e.dma_start(out=dbg[:, 0:1024], in_=scr32[:, 0:1024]),
                         waits=[(sDVE, dD0)], inc=sST, inck=16)
                    dD1 = p.op("vector", lambda e: e.tensor_copy(
                        out=scr32[0:64, 0:1024], in_=qtb[0:64, 0:2048].bitcast(F32)[0:64, 0:1024]),
                        waits=[(sST, sST.n)], inc=sDVE)
                    p.op("sync", lambda e: e.dma_start(out=dbg[:, 1024:2048], in_=scr32[:, 0:1024]),
                         waits=[(sDVE, dD1)], inc=sST, inck=16)
                    dD2 = p.op("vector", lambda e: e.tensor_copy(
                        out=scr32[:, 0:16], in_=den16[:, :]),
                        waits=[(sST, sST.n)], inc=sDVE)
                    p.op("sync", lambda e: e.dma_start(out=dbg[:, 2048:2064], in_=scr32[:, 0:16]),
                         waits=[(sDVE, dD2)], inc=sST, inck=16)
                    p.op("sync", lambda e: None, waits=[(sST, sST.n)])
                    _emit_bodies(p, block)
                    return nc

                d_wo[j] = p.op("tensor", wof,
                               waits=[(sACT, d_aoe[j])], inc=sPE)

                def finf(e, j=j, pp=pp):
                    s = j
                    e.scalar_tensor_tensor(out=outb[:, :], in0=pWo[:, :], scalar=1.0,
                                           in1=accq[:, s * 1024:(s + 1) * 1024],
                                           op0=ALU.mult, op1=ALU.add)
                    return e.tensor_tensor(out=outb[:, :], in0=outb[:, :], in1=xb[pp][:, :],
                                           op=ALU.add)
                fin_d[(q, j)] = p.op("vector", finf,
                                     waits=[(sPE, d_wo[j]), (sX[pp], d_dma[j]),
                                            (sST, store_d.get((q, j - 1), 0))], inc=sDVE)
                store_d[(q, j)] = p.op("sync", lambda e, row0=row0: e.dma_start(
                    out=out[row0:row0 + 128, :], in_=outb[:, :]),
                    waits=[(sDVE, fin_d[(q, j)])], inc=sST, inck=16)
                return None

            emit_qmm(0)
            emit_qesq(0)
            for j in range(NSUB_Q):
                pp = j % 2
                row0 = q * NQ + j * 128
                d_dma[j] = p.op("sync", lambda e, row0=row0, pp=pp: e.dma_start(
                    out=xb[pp][:, :], in_=x_own[row0:row0 + 128, :]),
                    waits=[(sDVE, fin_d.get((q, j - 2), 0))], inc=sX[pp], inck=16)
                ekw = {}; qpw = {}; dkw = {}
                for w in range(2):
                    if w == 1 and j + 1 < NSUB_Q:
                        emit_qmm(j + 1)

                    def dq_wave(e, w=w):
                        for k in range(8):
                            h = 8 * w + k
                            rb = (h % 2) * 64
                            sl = k // 2
                            pb = pDKdp[h % 2]
                            cb = (h // 2) * 128
                            qf_h = qfb[rb:rb + 64, cb:cb + 128]
                            sq_h = sqb[rb:rb + 64, cb:cb + 128]
                            e.matmul(pb[:, sl * 64:(sl + 1) * 64], qf_h,
                                     projT[rb:rb + 64, (h // 2) * 64:(h // 2) * 64 + 64],
                                     start=(k < 2), stop=False, skip_group_check=True,
                                     tile_position=(rb, 0))
                            e.matmul(pb[:, sl * 64:(sl + 1) * 64], sq_h, negq[rb:rb + 64, :],
                                     start=False, stop=False, skip_group_check=True,
                                     tile_position=(rb, 0))
                            inst = e.matmul(pb[:, 256 + sl:257 + sl], sq_h,
                                            posq[rb:rb + 64, :],
                                            start=False, stop=(k >= 6), skip_group_check=True,
                                            tile_position=(rb, 0))
                        return inst
                    if w == 0:
                        wlw = [(sDVE, d_sq[j]), (sACT, d_ek.get(j - 1, 0)),
                               (sDVE, d_qp.get(j - 1, 0))]
                    else:
                        wlw = [(sACT, ekw[0]), (sDVE, qpw[0])]
                    dkw[w] = p.op("tensor", dq_wave, waits=wlw, inc=sPE)

                    def qpf1(e, w=w):
                        for par in range(2):
                            c0 = 8 * w + 4 * par
                            inst = e.tensor_reduce(
                                out=stsc[:, c0:c0 + 4],
                                in_=pDKdp[par][:, 0:256].rearrange("p (t n) -> p t n", n=64),
                                axis=AX.X, op=ALU.max, negate=True)
                        return inst
                    qv1 = p.op("vector", qpf1, waits=[(sPE, dkw[w])], inc=sDVE)

                    def qpf2(e, w=w):
                        for par in range(2):
                            c0 = 8 * w + 4 * par
                            inst = e.tensor_tensor(out=nstab[:, c0:c0 + 4],
                                                   in0=stsc[:, c0:c0 + 4],
                                                   in1=pDKdp[par][:, 256:260], op=ALU.subtract)
                        return inst
                    qpw[w] = p.op("vector", qpf2, waits=[(sDVE, qv1)], inc=sDVE)

                    def ekf(e, w=w):
                        for k in range(8):
                            h = 8 * w + k
                            mc = mcol(h)
                            inst = e.activation(out=qeb[:, h * 64:(h + 1) * 64],
                                                in_=pDKdp[h % 2][:, (k // 2) * 64:(k // 2 + 1) * 64],
                                                func=AF.Exp, bias=nstab[:, mc:mc + 1])
                        return inst
                    ekw[w] = p.op("scalar", ekf,
                                  waits=[(sDVE, qpw[w]),
                                         (sDVE, d_pr.get(j - 1, 0) if w == 0 else 0)], inc=sACT)
                d_dk[j] = dkw[1]; d_qp[j] = qpw[1]; d_ek[j] = ekw[1]
                if j + 1 < NSUB_Q:
                    emit_qesq(j + 1, war=dkw[1])
                if j >= 1:
                    emit_tail_a(j - 1)
                    rr = emit_tail_b(j - 1)
                    if rr is not None:
                        return rr

                def prf1(e):
                    e.tensor_scalar(out=qpb[:, :], in0=qeb[:, :], scalar1=RATIO,
                                    scalar2=EPS_FEAT, op0=ALU.mult, op1=ALU.add)
                    e.tensor_tensor(out=qprod[:, :], in0=qpb[:, :], in1=ksumB[:, :], op=ALU.mult)
                    return e.tensor_reduce(out=den16[:, :],
                                           in_=qprod[:, :].rearrange("p (h n) -> p h n", n=64),
                                           axis=AX.X, op=ALU.add)
                pr1 = p.op("vector", prf1,
                           waits=[(sACT, d_ek[j]), (sPE, d_qt.get(j - 1, 0))], inc=sDVE)
                pr2 = p.op("vector", lambda e: e.tensor_scalar(
                    out=den16[:, :], in0=den16[:, :], scalar1=EPS_DEN, scalar2=None,
                    op0=ALU.max), waits=[(sDVE, pr1)], inc=sDVE)
                pr3 = p.op("vector", lambda e: e.reciprocal(out=rden16[:, :], in_=den16[:, :]),
                           waits=[(sDVE, pr2)], inc=sDVE)
                d_pr[j] = p.op("vector", lambda e: e.tensor_tensor(
                    out=qnb[:, :].rearrange("p (h n) -> p h n", n=64),
                    in0=qpb[:, :].rearrange("p (h n) -> p h n", n=64),
                    in1=rden16[:, :].rearrange("p (h one) -> p h one", one=1).broadcast_to([128, 16, 64]),
                    op=ALU.mult), waits=[(sDVE, pr3)], inc=sDVE)

            emit_tail_a(NSUB_Q - 1)
            rr = emit_tail_b(NSUB_Q - 1)
            if rr is not None:
                return rr


            p.barrier(bar, drains=[("sync", sMS), ("sync", sST), ("sync", sX[0]), ("sync", sX[1]), ("sync", sXq[2]), ("sync", sXq[3]), ("gpsimd", sWD), ("gpsimd", sWT), ("gpsimd", sW1), ("gpsimd", sW2), ("gpsimd", sSL[0]), ("gpsimd", sSL[1]), ("sync", sBB)])

        p.op("sync", lambda e: None, waits=[(sST, sST.n)])

        _emit_bodies(p, block)
    return nc


def _emit_bodies(p, block):
    def body(eng):
        def run(e):
            for thunk in p.q[eng]:
                thunk(e)
        return run
    block.sync(body("sync"))
    block.scalar(body("scalar"))
    block.vector(body("vector"))
    block.tensor(body("tensor"))
    block.gpsimd(body("gpsimd"))


_NC_CACHE = {}


def _get_nc(debug=None):
    if debug not in _NC_CACHE:
        _NC_CACHE[debug] = _build(debug)
    return _NC_CACHE[debug]


def make_in_maps(inputs):
    x = np.ascontiguousarray(inputs["L0_attn_query"], dtype=np.float32)
    eyeb = np.eye(128, dtype=ml_dtypes.bfloat16)
    eyef = np.eye(128, dtype=np.float32)
    shared = {
        "wq": np.ascontiguousarray(inputs["L0_attn_Wq"]),
        "wk": np.ascontiguousarray(inputs["L0_attn_Wk"]),
        "wv": np.ascontiguousarray(inputs["L0_attn_Wv"]),
        "wo": np.ascontiguousarray(inputs["L0_attn_Wo"]),
        "bq": np.ascontiguousarray(inputs["L0_attn_bq"]).reshape(1, D),
        "bk": np.ascontiguousarray(inputs["L0_attn_bk"]).reshape(1, D),
        "bv": np.ascontiguousarray(inputs["L0_attn_bv"]).reshape(1, D),
        "bo": np.ascontiguousarray(inputs["L0_attn_bo"]).reshape(1, D),
        "proj": np.ascontiguousarray(inputs["L0_attn_proj"]),
        "rw": np.ascontiguousarray(inputs["L0_ffn_router_w"]),
        "w1": np.ascontiguousarray(inputs["L0_ffn_w1"]),
        "b1": np.ascontiguousarray(inputs["L0_ffn_b1"]),
        "w2": np.ascontiguousarray(inputs["L0_ffn_w2"]),
        "b2": np.ascontiguousarray(inputs["L0_ffn_b2"]),
        "eyeb": eyeb,
        "eyef": eyef,
    }
    in_maps = []
    for c in range(8):
        b, h = c // 2, c % 2
        im = dict(shared)
        im["x_full"] = np.ascontiguousarray(x[b])
        im["x_own"] = np.ascontiguousarray(x[b, h * NOWN:(h + 1) * NOWN])
        in_maps.append(im)
    return in_maps


def kernel(**inputs):
    nc = _get_nc(None)
    in_maps = make_in_maps(inputs)
    res = run_bass_kernel_spmd(nc, in_maps, core_ids=list(range(8)))
    outa = np.empty((B, N, D), np.float32)
    for c in range(8):
        b, h = c // 2, c % 2
        outa[b, h * NOWN:(h + 1) * NOWN] = res.results[c]["out"]
    return outa
